# version 10
# baseline (speedup 1.0000x reference)
import numpy as np
from contextlib import ExitStack
import concourse.bass as bass
import concourse.mybir as mybir
from concourse.bass_utils import run_bass_kernel_spmd

F32 = mybir.dt.float32
BF16 = mybir.dt.bfloat16
I32 = mybir.dt.int32
AF = mybir.ActivationFunctionType
ALU = mybir.AluOpType

T = 2048
TC = 512
NCH = T // TC
DM = 1024
DMT = DM // 128
DIN = 2048
DINT = DIN // 128
NST = 16
DFF = 4096
DFFT = DFF // 128
HID = 3072
HIDT = HID // 128
EPS = 1e-5
R127 = np.float32(1.0) / np.float32(127.0)
TC2P = 2 * TC + 1

ENGS = ("sync", "scalar", "vector", "tensor", "gpsimd")


class KB:

    def __init__(self, nc, es):
        self.nc = nc
        self.prog = {e: [] for e in ENGS}
        self.sem = {}
        self.cnt = {}
        self.mark = {e: {} for e in ENGS}
        for s in ("d", "d2", "s", "v", "t", "g"):
            self.sem[s] = es.enter_context(nc.semaphore("sem_" + s))
            self.cnt[s] = 0
        self.bank = {}

    def wait(self, eng, sname, val):
        if val is None or val <= 0:
            return
        if self.mark[eng].get(sname, 0) >= val:
            return
        self.mark[eng][sname] = val
        sem = self.sem[sname]
        self.prog[eng].append(lambda e, sem=sem, val=val: e.wait_ge(sem, val))

    def waits(self, eng, d):
        if d:
            for k, v in d.items():
                self.wait(eng, k, v)

    def run(self, eng, fn, inc=None, n=1, after=None):
        self.waits(eng, after)
        tgt = None
        if inc:
            self.cnt[inc] += n
            tgt = self.cnt[inc]
            sem = self.sem[inc]
            self.prog[eng].append(lambda e, fn=fn, sem=sem, n=n: fn(e).then_inc(sem, n))
        else:
            self.prog[eng].append(lambda e, fn=fn: fn(e))
        return tgt

    def dma(self, out, in_, after=None, eng="sync"):
        sem = "d2" if eng == "gpsimd" else "d"
        return self.run(eng, lambda e, o=out, i=in_: e.dma_start(out=o, in_=i),
                        inc=sem, n=16, after=after)

    def bank_prod_after(self, name):
        st = self.bank.get(name)
        if st and st.get("cons"):
            return {st["cons"][0]: st["cons"][1]}
        return {}

    def set_bank_prod(self, name, val):
        self.bank[name] = {"prod": ("t", val), "cons": None}

    def bank_cons_after(self, name):
        st = self.bank.get(name)
        if st and st.get("prod"):
            return {st["prod"][0]: st["prod"][1]}
        return {}

    def set_bank_cons(self, name, sem, val):
        st = self.bank.setdefault(name, {"prod": None, "cons": None})
        st["cons"] = (sem, val)


def merge(*ds):
    out = {}
    for d in ds:
        if d:
            for k, v in d.items():
                if v is None:
                    continue
                if k not in out or out[k] < v:
                    out[k] = v
    return out


def bcol(tile, i):
    col = tile[:, i:i + 1]
    return bass.AP(col.tensor, col.offset, [list(col.ap[0]), [0, 128]])


def build_nc(debug=False):
    nc = bass.Bass("TRN2", target_bir_lowering=False)

    x_h = nc.dram_tensor("x", [128, DMT, T], F32, kind="ExternalInput").ap()
    out_h = nc.dram_tensor("out", [128, DMT, T], F32, kind="ExternalOutput").ap()

    w_in = [nc.dram_tensor(f"win{b}", [DM, 2 * DIN], F32, kind="ExternalInput").ap() for b in (1, 2)]
    w_out = [nc.dram_tensor(f"wout{b}", [DIN, DM], F32, kind="ExternalInput").ap() for b in (1, 2)]
    w_xp = [nc.dram_tensor(f"wxp{b}", [128, DINT, 96], F32, kind="ExternalInput").ap() for b in (1, 2)]
    w_dt = [nc.dram_tensor(f"wdt{b}", [64, DIN], F32, kind="ExternalInput").ap() for b in (1, 2)]
    w_f1 = nc.dram_tensor("wf1", [DM, DFF], F32, kind="ExternalInput").ap()
    w_f2 = nc.dram_tensor("wf2", [DFF, DM], F32, kind="ExternalInput").ap()
    w_g = nc.dram_tensor("wg", [DM, HID], I32, kind="ExternalInput").ap()
    w_u = nc.dram_tensor("wu", [DM, HID], I32, kind="ExternalInput").ap()
    w_d = nc.dram_tensor("wd", [HID, DM], I32, kind="ExternalInput").ap()

    v_g = nc.dram_tensor("v_g", [128, DMT], F32, kind="ExternalInput").ap()
    v_b = nc.dram_tensor("v_b", [128, DMT], F32, kind="ExternalInput").ap()
    v_rms = [nc.dram_tensor(f"v_rms{b}", [128, DMT], F32, kind="ExternalInput").ap() for b in (1, 2)]
    v_inb = [nc.dram_tensor(f"v_inb{b}", [128, 32], F32, kind="ExternalInput").ap() for b in (1, 2)]
    v_cw = [nc.dram_tensor(f"v_cw{b}", [128, 64], F32, kind="ExternalInput").ap() for b in (1, 2)]
    v_cb = [nc.dram_tensor(f"v_cb{b}", [128, DINT], F32, kind="ExternalInput").ap() for b in (1, 2)]
    v_dtb = [nc.dram_tensor(f"v_dtb{b}", [128, DINT], F32, kind="ExternalInput").ap() for b in (1, 2)]
    v_alog = [nc.dram_tensor(f"v_alog{b}", [128, DINT * NST], F32, kind="ExternalInput").ap() for b in (1, 2)]
    v_D = [nc.dram_tensor(f"v_D{b}", [128, DINT], F32, kind="ExternalInput").ap() for b in (1, 2)]
    v_outb = [nc.dram_tensor(f"v_outb{b}", [128, DMT], F32, kind="ExternalInput").ap() for b in (1, 2)]
    v_fb1 = nc.dram_tensor("v_fb1", [128, DFFT], F32, kind="ExternalInput").ap()
    v_fb2 = nc.dram_tensor("v_fb2", [128, DMT], F32, kind="ExternalInput").ap()
    selbc_h = nc.dram_tensor("selbc_h", [96, 32], BF16, kind="ExternalInput").ap()
    ident_h = nc.dram_tensor("ident_h", [128, 128], BF16, kind="ExternalInput").ap()

    s_win = [nc.dram_tensor(f"s_win{b}", [DM, 2 * DIN], BF16, kind="Internal").ap() for b in (1, 2)]
    s_wout = [nc.dram_tensor(f"s_wout{b}", [DIN, DM], BF16, kind="Internal").ap() for b in (1, 2)]
    s_wf1 = nc.dram_tensor("s_wf1", [DM, DFF], BF16, kind="Internal").ap()
    s_wf2 = nc.dram_tensor("s_wf2", [DFF, DM], BF16, kind="Internal").ap()
    s_wg = nc.dram_tensor("s_wg", [DM, HID], BF16, kind="Internal").ap()
    s_wu = nc.dram_tensor("s_wu", [DM, HID], BF16, kind="Internal").ap()
    s_wd = nc.dram_tensor("s_wd", [HID, DM], BF16, kind="Internal").ap()
    s_xmid = nc.dram_tensor("s_xmid", [128, DMT, T], F32, kind="Internal").ap()

    es_top = ExitStack()
    kb = KB(nc, es_top)

    dbg_names = []
    dbg_h = nc.dram_tensor("dbg", [128, 40 * 4096], F32, kind="ExternalOutput").ap() if debug else None
    dbgb_h = nc.dram_tensor("dbgb", [128, 40 * 4096], BF16, kind="ExternalOutput").ap() if debug else None
    dbg_off = [0, 0]

    def dump(name, ap, count, bf=False, parts=128):
        if not debug:
            return
        k = 1 if bf else 0
        dst = (dbgb_h if bf else dbg_h)[0:parts, dbg_off[k]:dbg_off[k] + count]
        aftr = {kk: kb.cnt[kk] for kk in ("s", "v", "t", "g")}
        tgt = kb.dma(dst, ap, after=aftr)
        dbg_names.append((name, k, dbg_off[k], count))
        dbg_off[k] += count
        for e in ("scalar", "vector", "tensor", "gpsimd", "sync"):
            kb.wait(e, "d", tgt)
    nc._dbg_names = dbg_names

    sb = lambda name, shape, dt: es_top.enter_context(nc.sbuf_tensor(name, shape, dt))
    onescol = sb("onescol", [128, 1], F32)
    ones_k1f = sb("ones_k1f", [1, 128], F32)
    cconst = sb("cconst", [128, 8], F32)
    selbc = sb("selbc", [96, 32], BF16)
    identb = sb("identb", [128, 128], BF16)
    cg = sb("cg", [128, DMT], F32)
    cb_ = sb("cb_", [128, DMT], F32)
    cinb = [sb(f"cinb{b}", [128, 32], F32) for b in (0, 1)]
    ccw = [sb(f"ccw{b}", [128, 64], F32) for b in (0, 1)]
    ccb = [sb(f"ccb{b}", [128, DINT], F32) for b in (0, 1)]
    cdtb = [sb(f"cdtb{b}", [128, DINT], F32) for b in (0, 1)]
    cnegcb = [sb(f"cnegcb{b}", [128, DINT], F32) for b in (0, 1)]
    cexpA = [sb(f"cexpA{b}", [128, DINT * NST], F32) for b in (0, 1)]
    cD = [sb(f"cD{b}", [128, DINT], F32) for b in (0, 1)]
    coutb = [sb(f"coutb{b}", [128, DMT], F32) for b in (0, 1)]
    cfb1 = sb("cfb1", [128, DFFT], F32)
    cfb2 = sb("cfb2", [128, DMT], F32)
    crms = [sb(f"crms{b}", [128, DMT], F32) for b in (0, 1)]
    xpw = [sb(f"xpw{b}", [128, DINT * 96], BF16) for b in (0, 1)]
    dtw = [sb(f"dtw{b}", [64, DIN], BF16) for b in (0, 1)]
    tails = [sb(f"tails{b}", [128, DINT * 3], F32) for b in (0, 1)]
    state = [sb(f"state{b}", [128, DINT * NST], F32) for b in (0, 1)]

    PS = [es_top.enter_context(nc.psum_tensor(f"ps{i}", [128, 512], F32)) for i in range(8)]

    es_pre = ExitStack()
    wtf = [es_pre.enter_context(nc.sbuf_tensor(f"wtf{i}", [128, 4096], F32))
           for i in range(2)]
    wto = [es_pre.enter_context(nc.sbuf_tensor(f"wto{i}", [128, 4096], BF16))
           for i in range(2)]
    cast_guard = [0, 0]

    def gmemset(ap, val):
        return kb.run("gpsimd", lambda e, a=ap, v=val: e.memset(a, v), inc="g")

    gmemset(onescol[:, :], 1.0)
    gmemset(ones_k1f[:, :], 1.0)
    for j, val in enumerate([EPS, EPS * EPS, 1.0 + EPS, -0.5, -1.0, float(R127),
                             1.0 / DM, 1.0]):
        gmemset(cconst[:, j:j + 1], val)
    for b in (0, 1):
        gmemset(tails[b][:, :], 0.0)
        gmemset(state[b][:, :], 0.0)
    g_done = kb.cnt["g"]

    for dst, src in [(cg, v_g), (cb_, v_b), (cfb1, v_fb1), (cfb2, v_fb2),
                     (selbc, selbc_h), (identb, ident_h)]:
        kb.dma(dst[:, :], src)
    for b in (0, 1):
        for dst, src in [(cinb[b], v_inb[b]), (ccw[b], v_cw[b]), (ccb[b], v_cb[b]),
                         (cD[b], v_D[b]), (coutb[b], v_outb[b]), (crms[b], v_rms[b])]:
            kb.dma(dst[:, :], src)

    for b in (0, 1):
        d1 = kb.dma(wtf[0][:, 0:DINT * NST], v_alog[b])
        se_ = kb.run("scalar", lambda e, o=cexpA[b][:, :], i=wtf[0][:, 0:DINT * NST]:
                     e.activation(out=o, in_=i, func=AF.Exp), inc="s", after={"d": d1})
        kb.run("vector", lambda e, o=cexpA[b][:, :], i=cexpA[b][:, :]:
               e.tensor_scalar(out=o, in0=i, scalar1=-1.0, scalar2=None, op0=ALU.mult),
               inc="v", after={"s": se_})
        kb.dma(cdtb[b][:, :], v_dtb[b])
        dcb = kb.dma(wtf[0][:, DINT * NST:DINT * NST + DINT], v_cb[b],
                     after={"s": se_})
        kb.run("vector", lambda e, o=cnegcb[b][:, :], i=wtf[0][:, DINT * NST:DINT * NST + DINT]:
               e.tensor_scalar(out=o, in0=i, scalar1=-1.0, scalar2=None, op0=ALU.mult),
               inc="v", after={"d": dcb})
    cnegA = cexpA
    v_after_pre = kb.cnt["v"]

    for b in (0, 1):
        d1 = kb.dma(wtf[1][:, 0:DINT * 96], w_xp[b].rearrange("p j m -> p (j m)"),
                    after={"v": v_after_pre, "s": cast_guard[1]})
        cg1 = kb.run("scalar", lambda e, o=xpw[b][:, :], i=wtf[1][:, 0:DINT * 96]:
                     e.activation(out=o, in_=i, func=AF.Identity), inc="s", after={"d": d1})
        cast_guard[1] = cg1
        d2 = kb.dma(wtf[1][0:64, 2048:4096], w_dt[b], after={"s": cast_guard[1]})
        cg1 = kb.run("scalar", lambda e, o=dtw[b][:, :], i=wtf[1][0:64, 2048:4096]:
                     e.activation(out=o, in_=i, func=AF.Identity), inc="s", after={"d": d2})
        cast_guard[1] = cg1

    def cast_weight_scaled(src, dst, rows, cols, scale_tile, j0=0):
        for j in range(rows // 128):
            jj = (j0 + j) % 2
            d1 = kb.dma(wtf[jj][:, 0:cols], src[j * 128:(j + 1) * 128, :],
                        after={"s": cast_guard[jj]})
            sc = scale_tile[:, j:j + 1]
            cst = kb.run("scalar", lambda e, o=wto[jj][:, 0:cols],
                         i=wtf[jj][:, 0:cols], s=sc:
                         e.activation(out=o, in_=i, func=AF.Identity, scale=s),
                         inc="s", after={"d": d1})
            cast_guard[jj] = cst
            kb.dma(dst[j * 128:(j + 1) * 128, :], wto[jj][:, 0:cols],
                   after={"s": cst})

    cast_weight_scaled(w_in[0], s_win[0], DM, 2 * DIN, crms[0])
    cast_weight_scaled(w_in[1], s_win[1], DM, 2 * DIN, crms[1], j0=0)
    cast_d = {}
    for src, dst, key in [(w_out[0], s_wout[0], "wout1"), (w_out[1], s_wout[1], "wout2"),
                          (w_f1, s_wf1, "wf1"), (w_f2, s_wf2, "wf2"),
                          (w_g, s_wg, "wg"), (w_u, s_wu, "wu"), (w_d, s_wd, "wd")]:
        cast_d[key] = kb.dma(dst, src, eng="gpsimd")

    pre_s, pre_v, pre_g, pre_d = kb.cnt["s"], kb.cnt["v"], kb.cnt["g"], kb.cnt["d"]
    es_pre.close()

    es1 = ExitStack()
    sb1 = lambda name, shape, dt: es1.enter_context(nc.sbuf_tensor(name, shape, dt))
    x_ln = sb1("x_ln", [128, DMT * TC], F32)
    h_sb = sb1("h_sb", [128, DMT * TC], F32)
    hr_sb = sb1("hr_sb", [128, DMT * TC], BF16)
    st0 = sb1("st0", [1, TC], F32)
    bcA = sb1("bcA", [128, TC], F32)
    bcB = sb1("bcB", [128, TC], F32)
    cbuf = [sb1(f"cbuf{i}", [128, TC], F32) for i in range(2)]
    sgbuf = [sb1(f"sgbuf{i}", [128, TC], F32) for i in range(2)]
    ebuf2 = sb1("ebuf2", [128, TC], F32)
    xcp = sb1("xcp", [128, DINT * (TC + 3)], BF16)
    dl3 = [sb1(f"dl3_{i}", [128, TC], BF16) for i in range(3)]
    z_sb = sb1("z_sb", [128, DINT * TC], BF16)
    xc_sb = sb1("xc_sb", [128, DINT * TC], BF16)
    Ball = sb1("Ball", [128, NST * TC], BF16)
    Call = sb1("Call", [128, (NST // 2) * TC2P], BF16)
    u_sb = sb1("u_sb", [128, DMT * TC], BF16)
    dbc = sb1("dbc", [96, TC], BF16)
    wb_in = [sb1(f"wb_in{i}", [128, DMT * 256], BF16) for i in range(2)]
    wb_out = [sb1(f"wb_out{i}", [128, DINT * 128], BF16) for i in range(2)]
    dxc0 = sb1("dxc0", [128, TC], BF16)
    dAb = [sb1(f"dAb{i}", [128, TC2P], BF16) for i in range(2)]
    dBu4 = sb1("dBu4", [128, TC2P], BF16)
    hbb = [sb1(f"hbb{i}", [128, TC2P], BF16) for i in range(2)]
    chb = [sb1(f"chb{i}", [128, TC2P], BF16) for i in range(2)]
    xcd = [sb1(f"xcd{i}", [128, TC], BF16) for i in range(2)]
    gmemset(dAb[0][:, TC:TC + 1], 0.0)
    gmemset(dAb[1][:, TC:TC + 1], 0.0)
    g_done = kb.cnt["g"]

    def slc(tile, idx, width, off=0):
        return tile[:, idx * width + off: idx * width + off + width]

    def xcp_sl(dt, a, b):
        return xcp[:, dt * (TC + 3) + a: dt * (TC + 3) + b]

    def s_act(out, in_, func, bias=None, scale=None, after=None):
        kw = {}
        if bias is not None:
            kw["bias"] = bias
        if scale is not None:
            kw["scale"] = scale
        return kb.run("scalar", lambda e, o=out, i=in_, f=func, kw=kw:
                      e.activation(out=o, in_=i, func=f, **kw), inc="s", after=after)

    def v_tt(out, a, b, op, after=None):
        return kb.run("vector", lambda e, o=out, x=a, y=b, p=op:
                      e.tensor_tensor(out=o, in0=x, in1=y, op=p), inc="v", after=after)

    def v_ts(out, a, s1, op0, s2=None, op1=None, after=None):
        if isinstance(s1, np.floating):
            s1 = float(s1)
        if isinstance(s2, np.floating):
            s2 = float(s2)
        def f(e, o=out, x=a, s1=s1, s2=s2, op0=op0, op1=op1):
            if op1 is None:
                return e.tensor_scalar(out=o, in0=x, scalar1=s1, scalar2=None, op0=op0)
            return e.tensor_scalar(out=o, in0=x, scalar1=s1, scalar2=s2, op0=op0, op1=op1)
        return kb.run("vector", f, inc="v", after=after)

    def v_stt(out, in0, scalar, in1, op0, op1, after=None):
        if isinstance(scalar, np.floating):
            scalar = float(scalar)
        return kb.run("vector", lambda e, o=out, a=in0, s=scalar, b=in1, p0=op0, p1=op1:
                      e.scalar_tensor_tensor(out=o, in0=a, scalar=s, in1=b, op0=p0, op1=p1),
                      inc="v", after=after)

    def t_mm(psname, out, lhsT, rhs, start, stop, after=None, inc=False):
        return kb.run("tensor", lambda e, o=out, l=lhsT, r=rhs, st=start, sp=stop:
                      e.matmul(o, l, r, start=st, stop=sp),
                      inc=("t" if inc else None), after=after)

    def stats_pass(tiles_fn, n, pa, after_each=None):
        tgt = None
        for j in range(n):
            rhs, aft = tiles_fn(j)
            a = merge(kb.bank_prod_after(pa) if j == 0 else None, aft, after_each)
            tgt = t_mm(pa, PS[pa][0:1, :], onescol[:, 0:1], rhs,
                       start=(j == 0), stop=(j == n - 1), after=a,
                       inc=(j == n - 1))
        kb.set_bank_prod(pa, tgt)
        return tgt

    st_guard = {"t": 0}

    def bc_sum(dst, src_ps, stage, pa, after, scale_ci=6):
        s1 = s_act(stage[:, :], src_ps, AF.Identity,
                   after=merge(after, {"g": g_done}, {"t": st_guard["t"]}))
        a = merge(kb.bank_prod_after(pa), {"s": s1})
        tm = t_mm(pa, PS[pa][:, :], ones_k1f[:, :], stage[:, :],
                  start=True, stop=True, after=a, inc=True)
        kb.set_bank_prod(pa, tm)
        st_guard["t"] = tm
        sc = s_act(dst, PS[pa][:, :], AF.Identity,
                   scale=(cconst[:, scale_ci:scale_ci + 1]
                          if scale_ci is not None else None),
                   after={"t": tm})
        kb.set_bank_cons(pa, "s", sc)
        return sc

    x3 = x_ln[:, :].rearrange("p (j t) -> p j t", j=DMT)

    ph1_barrier = {"s": pre_s, "v": pre_v, "g": pre_g}

    P = {
        "xmid_v": 0, "xmid_dma": 0, "outp_t": 0, "inproj_t": 0,
    }
    state_g = {}
    wb_guard = [0, 0]
    vdx_cur = {}

    def emit_ln(c):
        dx = kb.dma(x3, x_h[:, :, c * TC:(c + 1) * TC],
                    after=merge({"v": P["xmid_v"]}, ph1_barrier if c == 0 else None))
        def xt(j, dx=dx):
            return slc(x_ln, j, TC), {"d": dx, "g": g_done}
        t_s1 = stats_pass(xt, DMT, 6)
        sq_s = [None, None]
        tgt = None
        for j in range(DMT):
            a = merge({"d": dx}, {"t": sq_s[j % 2]} if sq_s[j % 2] else None)
            sq_s[j % 2] = s_act(cbuf[j % 2][:, :], slc(x_ln, j, TC), AF.Square, after=a)
            aft = merge({"s": sq_s[j % 2]}, kb.bank_prod_after(7) if j == 0 else None)
            tgt = t_mm(7, PS[7][0:1, :], onescol[:, 0:1], cbuf[j % 2][:, :],
                       start=(j == 0), stop=(j == DMT - 1), after=aft, inc=True)
            sq_s[j % 2] = tgt
        kb.set_bank_prod(7, tgt)
        t_s2 = tgt
        bmu = bc_sum(bcA[:, :], PS[6][0:1, :], st0, 4, {"t": t_s1})
        kb.set_bank_cons(6, "s", bmu)
        bex = bc_sum(bcB[:, :], PS[7][0:1, :], st0, 5, {"t": t_s2, "s": bmu})
        kb.set_bank_cons(7, "s", bex)
        vq = v_tt(cbuf[0][:, :], bcA[:, :], bcA[:, :], ALU.mult,
                  after={"s": bmu, "t": t_s2})
        vvar = v_tt(bcB[:, :], bcB[:, :], cbuf[0][:, :], ALU.subtract,
                    after={"s": bex})
        vcen = None
        for j in range(DMT):
            vcen = v_tt(slc(x_ln, j, TC), slc(x_ln, j, TC), bcA[:, :], ALU.subtract)
        sa1a = s_act(bcA[:, :], bcB[:, :], AF.Ln, bias=cconst[:, 0:1],
                     after={"v": vcen})
        sa1 = s_act(bcA[:, :], bcA[:, :], AF.Exp, scale=cconst[:, 3:4],
                    after={"s": sa1a})
        sa2a = s_act(bcB[:, :], bcB[:, :], AF.Ln, scale=cconst[:, 2:3],
                     bias=cconst[:, 1:2], after={"s": sa1})
        sa2 = s_act(bcB[:, :], bcB[:, :], AF.Exp, scale=cconst[:, 3:4],
                    after={"s": sa2a})
        h_done = []
        vv = None
        for j in range(DMT):
            vv = v_tt(cbuf[j % 2][:, :], slc(x_ln, j, TC), bcB[:, :], ALU.mult,
                      after=merge({"s": sa2} if j == 0 else None,
                                  {"s": h_done[j - 2]} if j >= 2 else None))
            hd = s_act(slc(h_sb, j, TC), cbuf[j % 2][:, :], AF.Identity,
                       bias=cb_[:, j:j + 1], scale=cg[:, j:j + 1],
                       after=merge({"v": vv}, {"d": P["xmid_dma"]} if j == 0 else None))
            h_done.append(hd)
        vlp = None
        for j in range(DMT):
            vlp = v_tt(slc(x_ln, j, TC), slc(x_ln, j, TC), bcA[:, :], ALU.mult,
                       after={"s": sa1} if j == 0 else None)
        if c == 0:
            dump("h", h_sb[:, :], DMT * TC)
            dump("LNp", x_ln[:, :], DMT * TC)
        sqh = [None, None]
        tgt = None
        for j in range(DMT):
            a = merge({"s": h_done[j]}, {"t": sqh[j % 2]} if sqh[j % 2] else None,
                      {"v": vv} if j == 0 else None,
                      kb.bank_prod_after(6) if j == 0 else None)
            ss = s_act(cbuf[j % 2][:, :], slc(h_sb, j, TC), AF.Square, after=a)
            aft = merge({"s": ss}, kb.bank_prod_after(6) if j == 0 else None)
            tgt = t_mm(6, PS[6][0:1, :], onescol[:, 0:1], cbuf[j % 2][:, :],
                       start=(j == 0), stop=(j == DMT - 1), after=aft, inc=True)
            sqh[j % 2] = tgt
        kb.set_bank_prod(6, tgt)
        brm = bc_sum(bcB[:, :], PS[6][0:1, :], st0, 4, {"t": tgt, "v": vv})
        kb.set_bank_cons(6, "s", brm)
        srm = s_act(bcB[:, :], bcB[:, :], AF.Ln, bias=cconst[:, 0:1],
                    after={"s": brm})
        srm = s_act(bcB[:, :], bcB[:, :], AF.Exp, scale=cconst[:, 3:4],
                    after={"s": srm})
        hr_done = None
        for j in range(DMT):
            hr_done = v_tt(slc(hr_sb, j, TC), slc(h_sb, j, TC), bcB[:, :], ALU.mult,
                           after=merge({"s": srm},
                                       {"t": P["inproj_t"]} if j == 0 else None))
        if c == 0:
            dump("hr", hr_sb[:, :], DMT * TC, bf=True)
        return {"hr": hr_done, "sqh": tgt, "c": c}

    def make_stage1(c, b, lnh):
        sw = s_win[b].rearrange("(j p) m -> p j m", p=128)
        ctx = {"xc": {}, "evt": {}, "evs": {}, "xproj_t": None, "dl": {},
               "zev": {}, "sg": {}, "c": c, "b": b}

        def inproj_m(m):
            buf = m % 2
            wdma = kb.dma(wb_in[buf][:, 0:DMT * 128].rearrange(
                              "p (j m) -> p j m", j=DMT),
                          sw[:, :, m * 128:(m + 1) * 128],
                          after={"t": wb_guard[buf],
                                 "v": lnh["hr"] if m == 0 else None})
            pa = m % 2
            aft = merge({"d": wdma}, {"v": lnh["hr"]},
                        kb.bank_prod_after(pa), kb.bank_cons_after(pa))
            tl = None
            for k in range(DMT):
                tl = t_mm(pa, PS[pa][:, :], wb_in[buf][:, k * 128:(k + 1) * 128],
                          slc(hr_sb, k, TC), start=(k == 0), stop=(k == DMT - 1),
                          after=aft if k == 0 else None, inc=(k == DMT - 1))
            kb.set_bank_prod(pa, tl)
            wb_guard[buf] = tl
            return pa, tl

        def piece(dt):
            pa, tl = inproj_m(dt)
            sev = s_act(xcp_sl(dt, 3, TC + 3), PS[pa][:, :], AF.Identity,
                        bias=cinb[b][:, dt:dt + 1], after={"t": tl})
            kb.set_bank_cons(pa, "s", sev)
            ctx["evt"][dt] = tl
            st = s_act(xcp_sl(dt, 0, 3), tails[b][:, dt * 3:(dt + 1) * 3],
                       AF.Identity, after={"g": g_done})
            va = v_ts(cbuf[dt % 2][:, :], xcp_sl(dt, 0, TC),
                      ccw[b][:, dt * 4: dt * 4 + 1], ALU.mult,
                      after={"s": st, "t": lnh["sqh"]})
            for k2 in (1, 2, 3):
                va = v_stt(cbuf[dt % 2][:, :], xcp_sl(dt, k2, TC + k2),
                           ccw[b][:, dt * 4 + k2: dt * 4 + k2 + 1],
                           cbuf[dt % 2][:, :], ALU.mult, ALU.add)
            sb_ = s_act(sgbuf[dt % 2][:, :], cbuf[dt % 2][:, :], AF.Exp,
                        scale=cconst[:, 4:5],
                        bias=cnegcb[b][:, dt:dt + 1], after={"v": va})
            sb_ = s_act(sgbuf[dt % 2][:, :], sgbuf[dt % 2][:, :], AF.Ln,
                        bias=cconst[:, 7:8], after={"s": sb_})
            ctx["sg"][dt] = s_act(sgbuf[dt % 2][:, :], sgbuf[dt % 2][:, :],
                                  AF.Exp, scale=cconst[:, 4:5],
                                  after={"s": sb_})
            s_act(tails[b][:, dt * 3:(dt + 1) * 3], xcp_sl(dt, TC, TC + 3),
                  AF.Identity)
            if dt >= 1:
                conv_fin(dt - 1)
            if dt >= 2:
                k = dt - 2
                aft = merge(kb.bank_prod_after(5) if k == 0 else None,
                            {"v": ctx["xc"][k]})
                ctx["xproj_t"] = t_mm(5, PS[5][0:96, :],
                                      xpw[b][:, k * 96:(k + 1) * 96],
                                      slc(xc_sb, k, TC), start=(k == 0),
                                      stop=False, after=aft,
                                      inc=False)

        def conv_fin(dt):
            ctx["xc"][dt] = v_stt(slc(xc_sb, dt, TC), cbuf[dt % 2][:, :],
                                  ccb[b][:, dt:dt + 1], sgbuf[dt % 2][:, :],
                                  ALU.add, ALU.mult,
                                  after=merge({"s": ctx["sg"][dt]},
                                              {"v": vdx_cur.get(dt)}))

        def dtmm_softplus(dt):
            pa = 6 + (dt % 2)
            aft = merge(kb.bank_prod_after(pa), kb.bank_cons_after(pa),
                        {"s": ctx["dbc_s"]})
            tm = t_mm(pa, PS[pa][:, :], dtw[b][:, dt * 128:(dt + 1) * 128],
                      dbc[0:64, :], start=True, stop=True, after=aft, inc=True)
            kb.set_bank_prod(pa, tm)
            se = s_act(ebuf2[:, :], PS[pa][:, :], AF.Exp,
                       bias=cdtb[b][:, dt:dt + 1], after={"t": tm})
            kb.set_bank_cons(pa, "s", se)
            ctx["dl"][dt] = s_act(dl3[dt % 3][:, :], ebuf2[:, :], AF.Ln,
                                  bias=cconst[:, 7:8],
                                  after=merge({"s": se},
                                              {"v": vdx_cur.get((dt - 2) % DINT)}))

        def tail():
            conv_fin(DINT - 1)
            k = DINT - 2
            aft = merge({"v": ctx["xc"][k]},
                        kb.bank_prod_after(5) if ctx["xproj_t"] is None else None)
            t_mm(5, PS[5][0:96, :], xpw[b][:, k * 96:(k + 1) * 96],
                 slc(xc_sb, k, TC), start=(ctx["xproj_t"] is None),
                 stop=False, after=aft, inc=False)
            ctx["xproj_t"] = 1
            k = DINT - 1
            tl = t_mm(5, PS[5][0:96, :], xpw[b][:, k * 96:(k + 1) * 96],
                      slc(xc_sb, k, TC), start=False,
                      stop=True, after={"v": ctx["xc"][k]}, inc=True)
            kb.set_bank_prod(5, tl)
            ctx["dbc_s"] = s_act(dbc[:, :], PS[5][0:96, :], AF.Identity,
                                 after={"t": tl})
            kb.set_bank_cons(5, "s", ctx["dbc_s"])
            if c == 0 and b == 0:
                dump("dbc", dbc[:, :], 512, bf=True, parts=96)
            dtmm_softplus(0)
            dtmm_softplus(1)

        ctx["piece"] = piece
        ctx["tail"] = tail
        ctx["dtmm_softplus"] = dtmm_softplus
        return ctx

    def emit_zhalf(ctx):
        b = ctx["b"]
        outg = P["outp_t"]
        zlast = None
        for i in range(DINT):
            m = DINT + i
            pa, tl = None, None
            buf = m % 2
            sw = s_win[b].rearrange("(j p) m -> p j m", p=128)
            wdma = kb.dma(wb_in[buf][:, 0:DMT * 128].rearrange(
                              "p (j m) -> p j m", j=DMT),
                          sw[:, :, m * 128:(m + 1) * 128],
                          after={"t": wb_guard[buf]})
            pa = m % 2
            aft = merge({"d": wdma}, kb.bank_prod_after(pa),
                        kb.bank_cons_after(pa))
            for k in range(DMT):
                tl = t_mm(pa, PS[pa][:, :], wb_in[buf][:, k * 128:(k + 1) * 128],
                          slc(hr_sb, k, TC), start=(k == 0), stop=(k == DMT - 1),
                          after=aft if k == 0 else None, inc=(k == DMT - 1))
            kb.set_bank_prod(pa, tl)
            wb_guard[buf] = tl
            sev = s_act(slc(z_sb, i, TC), PS[pa][:, :], AF.Silu,
                        bias=cinb[b][:, DINT + i:DINT + i + 1],
                        after=merge({"t": tl}, {"t": outg} if i == 0 else None))
            kb.set_bank_cons(pa, "s", sev)
            ctx["zev"][i] = sev
            zlast = sev
        return zlast

    def emit_scan(ctx, nxt_cb):
        c, b = ctx["c"], ctx["b"]
        scan_v = {}
        yfin_v = {}
        gsnap = kb.cnt["g"]
        bcast_s = {}

        def bcast(i):
            pa = 6 + (i % 2)
            aft = merge(kb.bank_prod_after(pa), kb.bank_cons_after(pa),
                        {"s": ctx["dbc_s"]}, {"g": g_done})
            tm = t_mm(pa, PS[pa][:, :], bcol(selbc, i), dbc[:, :],
                      start=True, stop=True, after=aft, inc=True)
            kb.set_bank_prod(pa, tm)
            if i < 16:
                dst = slc(Ball, i, TC)
            else:
                n_ = i - 16
                o_ = (n_ // 2) * TC2P + (n_ % 2) * (TC + 1)
                dst = Call[:, o_:o_ + TC]
            sc = s_act(dst, PS[pa][:, :], AF.Identity, after={"t": tm})
            kb.set_bank_cons(pa, "s", sc)
            return sc

        pstr = list(dxc0[:, :].ap[0])
        psy_t = {}
        psy_ch = {}
        for dt in range(DINT):
            vdx = v_tt(dxc0[:, :], dl3[dt % 3][:, :], slc(xc_sb, dt, TC),
                       ALU.mult, after={"s": ctx["dl"][dt], "g": g_done,
                                        "v": ctx["xc"][dt]})
            vdx_cur[dt] = vdx
            pyb = 3 + (dt % 2)
            sxc = s_act(xcd[dt % 2][:, :], slc(xc_sb, dt, TC),
                        AF.Identity, scale=cD[b][:, dt:dt + 1],
                        after={"t": psy_t.get(dt - 2)})
            aft0 = merge({"s": sxc}, kb.bank_prod_after(pyb))
            im = t_mm(pyb, PS[pyb][:, :], identb[:, :], xcd[dt % 2][:, :],
                      start=True, stop=False, after=aft0, inc=True)
            psy_t[dt] = im
            for np_ in range(0, NST, 2):
                if np_ == 2:
                    if dt < DINT - 2:
                        ctx["dtmm_softplus"](dt + 2)
                p = np_ // 2
                sidx = dt * NST + np_
                if dt == 0:
                    for i2 in (np_, np_ + 1, 16 + np_, 17 + np_):
                        bcast_s[i2] = bcast(i2)
                prev_scan = scan_v.get((dt, np_ - 4))
                if prev_scan is None and np_ < 4:
                    prev_scan = scan_v.get((dt - 1, NST + np_ - 4))
                sda = s_act(dAb[p % 2][:, 0:TC], dl3[dt % 3][:, :], AF.Exp,
                            scale=cnegA[b][:, sidx:sidx + 1],
                            after={"v": prev_scan})
                sda = s_act(dAb[p % 2][:, TC + 1:TC2P], dl3[dt % 3][:, :],
                            AF.Exp, scale=cnegA[b][:, sidx + 1:sidx + 2],
                            after={"s": sda})
                dxv = bass.AP(dxc0[:, :].tensor, dxc0[:, :].offset,
                              [pstr, [0, 2], [1, TC]])
                dbo = bass.AP(dBu4[:, :].tensor, dBu4[:, :].offset,
                              [list(dBu4[:, :].ap[0]), [TC + 1, 2], [1, TC]])
                vdb = v_tt(dbo, dxv,
                           Ball[:, np_ * TC:(np_ + 2) * TC].rearrange(
                               "p (n t) -> p n t", n=2), ALU.mult,
                           after=merge({"s": bcast_s.get(np_ + 1), "v": vdx}
                                       if dt == 0 else None,
                                       {"v": scan_v.get((dt, np_ - 2)) or
                                        (scan_v.get((dt - 1, NST + np_ - 2))
                                         if np_ < 2 else None)}))
                gsep = kb.run("gpsimd",
                              lambda e, o=dBu4[:, TC:TC + 1],
                              i=state[b][:, sidx + 1:sidx + 2]:
                              e.tensor_copy(out=o, in_=i), inc="g")
                if np_ >= 4:
                    gprev = state_g.get((b, dt, np_ - 4))
                elif dt >= 1:
                    gprev = state_g.get((b, dt - 1, NST + np_ - 4))
                else:
                    gprev = gsnap
                ginit = state_g.get((b, dt, np_))
                vsc = kb.run("vector",
                             lambda e, o=hbb[p % 2][:, :], a=dAb[p % 2][:, :],
                             bb2=dBu4[:, :],
                             ini=state[b][:, sidx:sidx + 1]:
                             e.tensor_tensor_scan(out=o, data0=a, data1=bb2,
                                                  initial=ini, op0=ALU.mult,
                                                  op1=ALU.add),
                             inc="v",
                             after=merge({"s": sda}, {"g": gprev},
                                         {"g": ginit}, {"g": gsep}))
                scan_v[(dt, np_)] = vsc
                hb_ = hbb[p % 2][:, TC - 1:TC]
                hsrc = bass.AP(hb_.tensor, hb_.offset,
                               [list(hb_.ap[0]), [TC + 1, 2]])
                state_g[(b, dt, np_)] = kb.run(
                    "gpsimd", lambda e, o=state[b][:, sidx:sidx + 2], i=hsrc:
                    e.tensor_copy(out=o, in_=i),
                    inc="g", after={"v": vsc})
                vch = v_tt(chb[p % 2][:, :], hbb[p % 2][:, :],
                           Call[:, p * TC2P:(p + 1) * TC2P], ALU.mult,
                           after=merge({"s": bcast_s.get(17 + np_)}
                                       if dt == 0 else None,
                                       {"t": psy_ch.get((dt, p - 2))}))
                t_mm(pyb, PS[pyb][:, :], identb[:, :], chb[p % 2][:, 0:TC],
                     start=False, stop=False, after={"v": vch}, inc=False)
                psy_ch[(dt, p)] = t_mm(pyb, PS[pyb][:, :], identb[:, :],
                                       chb[p % 2][:, TC + 1:TC2P],
                                       start=False, stop=(np_ == NST - 2),
                                       after=None, inc=True)
            kb.set_bank_prod(pyb, psy_ch[(dt, 7)])
            if dt == 1:
                emit_zhalf(ctx)
                yfin_v[0] = v_tt(slc(z_sb, 0, TC), PS[3][:, :],
                                 slc(z_sb, 0, TC), ALU.mult,
                                 after={"s": ctx["zev"][0],
                                        "t": psy_ch[(0, 7)]})
                kb.set_bank_cons(3, "v", yfin_v[0])
            if dt >= 1:
                yfin_v[dt] = v_tt(slc(z_sb, dt, TC), PS[pyb][:, :],
                                  slc(z_sb, dt, TC), ALU.mult,
                                  after={"s": ctx["zev"][dt],
                                         "t": psy_ch[(dt, 7)]})
                kb.set_bank_cons(pyb, "v", yfin_v[dt])
            for cb2fn in nxt_cb.get(dt, []):
                cb2fn()
        ctx["yfin_v"] = yfin_v
        if c == 0 and b == 0:
            dump("yfin15", z_sb[:, 15 * TC:16 * TC], TC, bf=True)

    def emit_outproj(ctx):
        c, b = ctx["c"], ctx["b"]
        cb_user_v = {}
        outp_t = {}
        swo = s_wout[b].rearrange("(j p) m -> p j m", p=128)
        for m in range(DMT):
            wdo = kb.dma(wb_out[m % 2][:, :].rearrange(
                             "p (j m) -> p j m", j=DINT),
                         swo[:, :, m * 128:(m + 1) * 128],
                         after={"t": outp_t.get(m - 2, P["outp_t"]),
                                "d2": cast_d["wout1" if b == 0 else "wout2"]})
            pa = 3 + (m % 2)
            aft = merge({"d": wdo}, {"v": ctx["yfin_v"][DINT - 1]},
                        kb.bank_prod_after(pa), kb.bank_cons_after(pa))
            tl = None
            for k in range(DINT):
                tl = t_mm(pa, PS[pa][:, :], wb_out[m % 2][:, k * 128:(k + 1) * 128],
                          slc(z_sb, k, TC), start=(k == 0), stop=(k == DINT - 1),
                          after=aft if k == 0 else None, inc=(k == DINT - 1))
            kb.set_bank_prod(pa, tl)
            outp_t[m] = tl
            sev = s_act(cbuf[m % 2][:, :], PS[pa][:, :], AF.Identity,
                        bias=coutb[b][:, m:m + 1],
                        after=merge({"t": tl},
                                    {"v": kb.cnt["v"]} if m < 2 else
                                    {"v": cb_user_v.get(m - 2)}))
            kb.set_bank_cons(pa, "s", sev)
            if b == 0:
                cb_user_v[m] = v_tt(slc(u_sb, m, TC), cbuf[m % 2][:, :],
                                    slc(h_sb, m, TC), ALU.add, after={"s": sev})
            else:
                v_tt(cbuf[m % 2][:, :], cbuf[m % 2][:, :], slc(h_sb, m, TC),
                     ALU.add, after={"s": sev})
                v_tt(cbuf[m % 2][:, :], cbuf[m % 2][:, :], slc(u_sb, m, TC),
                     ALU.mult)
                P["xmid_v"] = v_tt(slc(h_sb, m, TC), cbuf[m % 2][:, :],
                                   slc(x_ln, m, TC), ALU.add)
                cb_user_v[m] = P["xmid_v"]
        P["outp_t"] = outp_t[DMT - 1]
        P["inproj_t"] = wb_guard[0] if wb_guard[0] > wb_guard[1] else wb_guard[1]
        if c == 0 and b == 0:
            dump("u", u_sb[:, :], DMT * TC, bf=True)
        if b == 1:
            if c == 0:
                dump("xmid", h_sb[:, :], DMT * TC)
            P["xmid_dma"] = kb.dma(s_xmid[:, :, c * TC:(c + 1) * TC],
                                   h_sb[:, :].rearrange("p (j t) -> p j t", j=DMT),
                                   after={"v": P["xmid_v"]})

    units = [(c, b) for c in range(NCH) for b in (0, 1)]
    lnh = emit_ln(0)
    cur = make_stage1(0, 0, lnh)
    for dt in range(DINT):
        cur["piece"](dt)
    cur["tail"]()
    for idx, (c, b) in enumerate(units):
        nxt = units[idx + 1] if idx + 1 < len(units) else None
        nxt_cb = {}
        nctx = None
        if nxt is not None and nxt[0] == c:
            nctx = make_stage1(nxt[0], nxt[1], lnh)
            for dt in range(DINT):
                nxt_cb[dt] = [(lambda d=dt, nc2=nctx: nc2["piece"](d))]
            nxt_cb[DINT - 1].append(lambda nc2=nctx: nc2["tail"]())
        if c == 0 and b == 0:
            dump("xc", xc_sb[:, :], DINT * TC, bf=True)
        emit_scan(cur, nxt_cb)
        if c == 0 and b == 0:
            dump("Ball", Ball[:, :], NST * TC, bf=True)
            dump("Call", Call[:, :], NST * TC, bf=True)
        emit_outproj(cur)
        if nxt is not None and nxt[0] != c:
            lnh = emit_ln(nxt[0])
            nctx = make_stage1(nxt[0], nxt[1], lnh)
            for dt in range(DINT):
                nctx["piece"](dt)
            nctx["tail"]()
        if nctx is not None:
            cur = nctx

    ph1 = {k: kb.cnt[k] for k in ("s", "v", "t", "g", "d")}
    es1.close()

    es2 = ExitStack()
    sb2 = lambda name, shape, dt: es2.enter_context(nc.sbuf_tensor(name, shape, dt))
    x0_sb = sb2("x0_sb", [128, DMT * TC], F32)
    xm_sb = sb2("xm_sb", [128, DMT * TC], F32)
    h2_sb = sb2("h2_sb", [128, DMT * TC], BF16)
    h3_sb = sb2("h3_sb", [128, DMT * TC], BF16)
    g_sb = sb2("g_sb", [128, DFFT * TC], BF16)
    hu_sb = sb2("hu_sb", [128, HIDT * TC], BF16)
    st20 = sb2("st20", [1, TC], F32)
    bc2A = sb2("bc2A", [128, TC], F32)
    bc2B = sb2("bc2B", [128, TC], F32)
    cb2 = [sb2(f"cb2_{i}", [128, TC], F32) for i in range(2)]
    sgb = sb2("sgb", [128, TC], BF16)
    wbA = [sb2(f"wbA{i}", [128, DMT * 256], BF16) for i in range(2)]
    wbB = [sb2(f"wbB{i}", [128, DFFT * 128], BF16) for i in range(2)]
    wbC = [sb2(f"wbC{i}", [128, DMT * 256], BF16) for i in range(2)]

    bar2 = dict(ph1)

    def stats_ln(src_tile, stats_t, pa, pb, ln_affine, out_tile, extra_after=None,
                 wait_each=None):
        t1 = None
        for j in range(DMT):
            aft = merge(kb.bank_prod_after(pa) if j == 0 else None,
                        extra_after if j == 0 else None, {"g": g_done})
            t1 = t_mm(pa, PS[pa][0:1, :], onescol[:, 0:1], slc(src_tile, j, TC),
                      start=(j == 0), stop=(j == DMT - 1), after=aft,
                      inc=(j == DMT - 1))
        kb.set_bank_prod(pa, t1)
        sq = [None, None]
        t2 = None
        for j in range(DMT):
            a = merge({"t": sq[j % 2]} if sq[j % 2] else None,
                      extra_after if j == 0 else None)
            ss = s_act(cb2[j % 2][:, :], slc(src_tile, j, TC), AF.Square, after=a)
            aft = merge({"s": ss}, kb.bank_prod_after(pb) if j == 0 else None)
            t2 = t_mm(pb, PS[pb][0:1, :], onescol[:, 0:1], cb2[j % 2][:, :],
                      start=(j == 0), stop=(j == DMT - 1), after=aft, inc=True)
            sq[j % 2] = t2
        kb.set_bank_prod(pb, t2)
        bmu = bc_sum(bc2A[:, :], PS[pa][0:1, :], st20, pa, {"t": t1})
        kb.set_bank_cons(pa, "s", bmu)
        bex = bc_sum(bc2B[:, :], PS[pb][0:1, :], st20, pb, {"t": t2, "s": bmu})
        kb.set_bank_cons(pb, "s", bex)
        vq = v_tt(cb2[0][:, :], bc2A[:, :], bc2A[:, :], ALU.mult,
                  after={"s": bmu, "t": t2})
        vvar = v_tt(bc2B[:, :], bc2B[:, :], cb2[0][:, :], ALU.subtract,
                    after={"s": bex})
        sA = s_act(bc2B[:, :], bc2B[:, :], AF.Ln, bias=cconst[:, 0:1],
                   after={"v": vvar})
        sA = s_act(bc2B[:, :], bc2B[:, :], AF.Exp, scale=cconst[:, 3:4],
                   after={"s": sA})
        outs = []
        for j in range(DMT):
            vv = v_tt(cb2[j % 2][:, :], slc(src_tile, j, TC), bc2A[:, :],
                      ALU.subtract,
                      after=merge({"s": sA} if j == 0 else None,
                                  {"s": outs[j - 2]} if j >= 2 else None))
            vv = v_tt(cb2[j % 2][:, :], cb2[j % 2][:, :], bc2B[:, :], ALU.mult)
            if ln_affine:
                o = s_act(slc(out_tile, j, TC), cb2[j % 2][:, :], AF.Identity,
                          bias=cb_[:, j:j + 1], scale=cg[:, j:j + 1], after={"v": vv})
            else:
                o = s_act(slc(out_tile, j, TC), cb2[j % 2][:, :], AF.Identity,
                          after={"v": vv})
            outs.append(o)
        return outs

    swf1 = s_wf1.rearrange("(j p) m -> p j m", p=128)
    swf2 = s_wf2.rearrange("(j p) m -> p j m", p=128)
    swg = s_wg.rearrange("(j p) m -> p j m", p=128)
    swu = s_wu.rearrange("(j p) m -> p j m", p=128)
    swd = s_wd.rearrange("(j p) m -> p j m", p=128)

    x03 = x0_sb[:, :].rearrange("p (j t) -> p j t", j=DMT)
    xm3 = xm_sb[:, :].rearrange("p (j t) -> p j t", j=DMT)

    prev_out_v = 0
    prev_gmm_t = 0
    prev_tg_last = 0
    prev_tu_last = 0
    prev_dn_last = 0
    for c in range(NCH):
        d0 = kb.dma(x03, x_h[:, :, c * TC:(c + 1) * TC],
                    after=merge(bar2 if c == 0 else None, {"v": prev_out_v}))
        dm = kb.dma(xm3, s_xmid[:, :, c * TC:(c + 1) * TC],
                    after=merge(bar2 if c == 0 else None, {"v": prev_out_v}))

        if c == 0:
            dump("xm_in", xm_sb[:, :], DMT * TC)
        h2_done = stats_ln(xm_sb, None, 6, 7, True, h2_sb,
                           extra_after={"d": dm, "t": prev_gmm_t})
        if c == 0:
            dump("p2mu", bc2A[:, :], TC)
            dump("p2a", bc2B[:, :], TC)

        if c == 0:
            dump("h2", h2_sb[:, :], DMT * TC, bf=True)
        g_done2 = {}
        for mg in range(DFFT // 2):
            wd1 = kb.dma(wbA[mg % 2][:, :].rearrange("p (j m) -> p j m", j=DMT),
                         swf1[:, :, mg * 256:(mg + 1) * 256],
                         after={"t": g_done2.get(("t", mg - 2), prev_tg_last),
                                "d2": cast_d["wf1"]})
            for mm in range(2):
                m = mg * 2 + mm
                pa = m % 3
                aft = merge({"d": wd1}, {"s": h2_done[-1]},
                            kb.bank_prod_after(pa), kb.bank_cons_after(pa))
                tl = None
                for k in range(DMT):
                    tl = t_mm(pa, PS[pa][:, :],
                              wbA[mg % 2][:, k * 256 + mm * 128: k * 256 + (mm + 1) * 128],
                              slc(h2_sb, k, TC), start=(k == 0), stop=(k == DMT - 1),
                              after=aft if k == 0 else None, inc=(k == DMT - 1))
                kb.set_bank_prod(pa, tl)
                sev = s_act(slc(g_sb, m, TC), PS[pa][:, :], AF.Gelu,
                            bias=cfb1[:, m:m + 1], after={"t": tl})
                kb.set_bank_cons(pa, "s", sev)
            g_done2[("t", mg)] = tl
        gelu_last = sev

        x1_done = {}
        for m in range(DMT):
            wd2 = kb.dma(wbB[m % 2][:, :].rearrange("p (j m) -> p j m", j=DFFT),
                         swf2[:, :, m * 128:(m + 1) * 128],
                         after={"t": x1_done.get(("t", m - 2), prev_dn_last),
                                "d2": cast_d["wf2"]})
            pa = 3 + (m % 2)
            aft = merge({"d": wd2}, {"s": gelu_last},
                        kb.bank_prod_after(pa), kb.bank_cons_after(pa))
            tl = None
            for k in range(DFFT):
                tl = t_mm(pa, PS[pa][:, :], wbB[m % 2][:, k * 128:(k + 1) * 128],
                          slc(g_sb, k, TC), start=(k == 0), stop=(k == DFFT - 1),
                          after=aft if k == 0 else None, inc=(k == DFFT - 1))
            kb.set_bank_prod(pa, tl)
            sev = s_act(cb2[m % 2][:, :], PS[pa][:, :], AF.Identity,
                        bias=cfb2[:, m:m + 1],
                        after=merge({"t": tl},
                                    {"v": kb.cnt["v"]} if m < 2 else
                                    {"v": x1_done.get(m - 2)}))
            kb.set_bank_cons(pa, "s", sev)
            v_tt(slc(xm_sb, m, TC), slc(xm_sb, m, TC), cb2[m % 2][:, :], ALU.add,
                 after={"s": sev})
            vx1 = v_tt(slc(xm_sb, m, TC), slc(xm_sb, m, TC), slc(x0_sb, m, TC),
                       ALU.add, after={"d": d0} if m == 0 else None)
            x1_done[("t", m)] = tl
            x1_done[m] = vx1

        if c == 0:
            dump("gg", g_sb[:, :], DFFT * TC, bf=True)
            dump("x1", xm_sb[:, :], DMT * TC)
        h3_done = stats_ln(xm_sb, None, 6, 7, False, h3_sb,
                           extra_after={"v": x1_done[DMT - 1]},
                           wait_each=None)

        if c == 0:
            dump("h3", h3_sb[:, :], DMT * TC, bf=True)
        hu_done = {}
        vhu = None
        ffn1_last_t = g_done2[("t", DFFT // 2 - 1)]
        for mg in range(HIDT // 2):
            wg_d = kb.dma(wbA[mg % 2][:, :].rearrange("p (j m) -> p j m", j=DMT),
                          swg[:, :, mg * 256:(mg + 1) * 256],
                          after={"t": hu_done.get(("tg", mg - 2), ffn1_last_t),
                                 "d2": cast_d["wg"]})
            wu_d = kb.dma(wbC[mg % 2][:, :].rearrange("p (j m) -> p j m", j=DMT),
                          swu[:, :, mg * 256:(mg + 1) * 256],
                          after={"t": hu_done.get(("tu", mg - 2), prev_tu_last),
                                 "d2": cast_d["wu"]})
            for mm in range(2):
                m = mg * 2 + mm
                pg = (m % 2) * 2
                pu = 1 + (m % 2) * 4
                aftg = merge({"d": wg_d}, {"s": h3_done[-1]},
                             kb.bank_prod_after(pg), kb.bank_cons_after(pg))
                aftu = merge({"d": wu_d}, {"s": h3_done[-1]},
                             kb.bank_prod_after(pu), kb.bank_cons_after(pu))
                tg = tu = None
                for k in range(DMT):
                    tg = t_mm(pg, PS[pg][:, :],
                              wbA[mg % 2][:, k * 256 + mm * 128: k * 256 + (mm + 1) * 128],
                              slc(h3_sb, k, TC), start=(k == 0), stop=(k == DMT - 1),
                              after=aftg if k == 0 else None, inc=(k == DMT - 1))
                for k in range(DMT):
                    tu = t_mm(pu, PS[pu][:, :],
                              wbC[mg % 2][:, k * 256 + mm * 128: k * 256 + (mm + 1) * 128],
                              slc(h3_sb, k, TC), start=(k == 0), stop=(k == DMT - 1),
                              after=aftu if k == 0 else None, inc=(k == DMT - 1))
                kb.set_bank_prod(pg, tg)
                kb.set_bank_prod(pu, tu)
                ssg = s_act(sgb[:, :], PS[pg][:, :], AF.Silu,
                            scale=cconst[:, 5:6],
                            after=merge({"t": tg}, {"v": vhu} if vhu else None,
                                        {"g": g_done}))
                kb.set_bank_cons(pg, "s", ssg)
                vhu = v_stt(slc(hu_sb, m, TC), PS[pu][:, :], R127, sgb[:, :],
                            ALU.mult, ALU.mult, after={"s": ssg, "t": tu})
                kb.set_bank_cons(pu, "v", vhu)
            hu_done[("tg", mg)] = tg
            hu_done[("tu", mg)] = tu
        hu_last = vhu

        if c == 0:
            dump("hu", hu_sb[:, :], HIDT * TC, bf=True)
        out_v = {}
        dn_t = {}
        ffn2_last_t = x1_done[("t", DMT - 1)]
        for m in range(DMT):
            wd3 = kb.dma(wbB[m % 2][:, 0:HIDT * 128].rearrange("p (j m) -> p j m", j=HIDT),
                         swd[:, :, m * 128:(m + 1) * 128],
                         after={"t": dn_t.get(m - 2, ffn2_last_t),
                                "d2": cast_d["wd"]})
            pa = 3 + (m % 2)
            aft = merge({"d": wd3}, {"v": hu_last},
                        kb.bank_prod_after(pa), kb.bank_cons_after(pa))
            tl = None
            for k in range(HIDT):
                tl = t_mm(pa, PS[pa][:, :], wbB[m % 2][:, k * 128:(k + 1) * 128],
                          slc(hu_sb, k, TC), start=(k == 0), stop=(k == HIDT - 1),
                          after=aft if k == 0 else None, inc=(k == HIDT - 1))
            kb.set_bank_prod(pa, tl)
            dn_t[m] = tl
            sev = s_act(cb2[m % 2][:, :], PS[pa][:, :], AF.Identity,
                        scale=cconst[:, 5:6],
                        after=merge({"t": tl},
                                    {"v": kb.cnt["v"]} if m < 2 else
                                    {"v": out_v.get(m - 2)}, {"g": g_done}))
            kb.set_bank_cons(pa, "s", sev)
            out_v[m] = v_tt(slc(x0_sb, m, TC), slc(xm_sb, m, TC), cb2[m % 2][:, :],
                            ALU.add, after={"s": sev})
        prev_out_v = out_v[DMT - 1]
        kb.dma(out_h[:, :, c * TC:(c + 1) * TC], x03, after={"v": prev_out_v})
        prev_gmm_t = tl
        prev_tg_last = hu_done[("tg", HIDT // 2 - 1)]
        prev_tu_last = hu_done[("tu", HIDT // 2 - 1)]
        prev_dn_last = dn_t[DMT - 1]

    es2.close()

    final_d = kb.cnt["d"]
    kb.prog["sync"].append(lambda e, v=final_d: e.wait_ge(kb.sem["d"], v))

    with nc.Block() as blk:
        blk.sync(lambda e: [f(e) for f in kb.prog["sync"]])
        blk.scalar(lambda e: [f(e) for f in kb.prog["scalar"]])
        blk.vector(lambda e: [f(e) for f in kb.prog["vector"]])
        blk.tensor(lambda e: [f(e) for f in kb.prog["tensor"]])
        blk.gpsimd(lambda e: [f(e) for f in kb.prog["gpsimd"]])

    es_top.close()
    return nc


def _vec128(v, k):
    v = np.ascontiguousarray(np.asarray(v, np.float32))
    return np.ascontiguousarray(v.reshape(k, 128).T)


def kernel(**inputs):
    ii = {k: np.asarray(v) for k, v in inputs.items()}
    x = np.ascontiguousarray(ii["L0_attn_x"], dtype=np.float32)

    nc = build_nc()

    common = {}
    for b, pre in ((1, "L0_attn_m1_"), (2, "L0_attn_m2_")):
        common[f"win{b}"] = np.ascontiguousarray(ii[pre + "in_w"], np.float32)
        common[f"wout{b}"] = np.ascontiguousarray(ii[pre + "out_w"], np.float32)
        common[f"wxp{b}"] = np.ascontiguousarray(
            ii[pre + "xproj_w"].astype(np.float32).reshape(16, 128, 96).transpose(1, 0, 2))
        common[f"wdt{b}"] = np.ascontiguousarray(ii[pre + "dt_w"], np.float32)
        common[f"v_rms{b}"] = _vec128(ii[pre + "rms_w"], 8)
        common[f"v_inb{b}"] = _vec128(ii[pre + "in_b"], 32)
        common[f"v_cw{b}"] = np.ascontiguousarray(
            ii[pre + "conv_w"].astype(np.float32).reshape(16, 128, 4)
            .transpose(1, 0, 2).reshape(128, 64))
        common[f"v_cb{b}"] = _vec128(ii[pre + "conv_b"], 16)
        common[f"v_dtb{b}"] = _vec128(ii[pre + "dt_b"], 16)
        common[f"v_alog{b}"] = np.ascontiguousarray(
            ii[pre + "A_log"].astype(np.float32).reshape(16, 128, 16)
            .transpose(1, 0, 2).reshape(128, 256))
        common[f"v_D{b}"] = _vec128(ii[pre + "D"], 16)
        common[f"v_outb{b}"] = _vec128(ii[pre + "out_b"], 8)
    common["wf1"] = np.ascontiguousarray(ii["L0_attn_ffn_w1"], np.float32)
    common["wf2"] = np.ascontiguousarray(ii["L0_attn_ffn_w2"], np.float32)
    common["v_fb1"] = _vec128(ii["L0_attn_ffn_b1"], 32)
    common["v_fb2"] = _vec128(ii["L0_attn_ffn_b2"], 8)
    common["v_g"] = _vec128(ii["L0_attn_ln_gamma"], 8)
    common["v_b"] = _vec128(ii["L0_attn_ln_beta"], 8)
    import ml_dtypes
    sel = np.zeros((96, 32), dtype=ml_dtypes.bfloat16)
    for n in range(16):
        sel[64 + n, n] = 1
        sel[80 + n, 16 + n] = 1
    common["selbc_h"] = sel
    common["ident_h"] = np.eye(128, dtype=ml_dtypes.bfloat16)
    common["wg"] = np.ascontiguousarray(ii["L0_ffn_w_gate"].T.astype(np.int32))
    common["wu"] = np.ascontiguousarray(ii["L0_ffn_w_up"].T.astype(np.int32))
    common["wd"] = np.ascontiguousarray(ii["L0_ffn_w_down"].T.astype(np.int32))

    in_maps = []
    for cidx in range(8):
        m = dict(common)
        xc = np.ascontiguousarray(
            x[cidx].T.reshape(DMT, 128, T).transpose(1, 0, 2))
        m["x"] = xc
        in_maps.append(m)

    res = run_bass_kernel_spmd(nc, in_maps, core_ids=list(range(8)))
    outs = []
    for cidx in range(8):
        o = res.results[cidx]["out"]
        outs.append(np.ascontiguousarray(
            o.transpose(1, 0, 2).reshape(DM, T).T))
    return np.stack(outs, axis=0).astype(np.float32)


# version 13 (final)
# speedup vs baseline: 1.0935x; 1.0935x over previous
import numpy as np
from contextlib import ExitStack
import concourse.bass as bass
import concourse.mybir as mybir
from concourse.bass_utils import run_bass_kernel_spmd

F32 = mybir.dt.float32
BF16 = mybir.dt.bfloat16
I32 = mybir.dt.int32
AF = mybir.ActivationFunctionType
ALU = mybir.AluOpType

T = 2048
TC = 512
NCH = T // TC
DM = 1024
DMT = DM // 128
DIN = 2048
DINT = DIN // 128
NST = 16
DFF = 4096
DFFT = DFF // 128
HID = 3072
HIDT = HID // 128
EPS = 1e-5
R127 = np.float32(1.0) / np.float32(127.0)
TC2P = 2 * TC + 1

ENGS = ("sync", "scalar", "vector", "tensor", "gpsimd")


class KB:

    def __init__(self, nc, es):
        self.nc = nc
        self.prog = {e: [] for e in ENGS}
        self.sem = {}
        self.cnt = {}
        self.mark = {e: {} for e in ENGS}
        for s in ("d", "d2", "s", "v", "t", "g"):
            self.sem[s] = es.enter_context(nc.semaphore("sem_" + s))
            self.cnt[s] = 0
        self.bank = {}

    def wait(self, eng, sname, val):
        if val is None or val <= 0:
            return
        if self.mark[eng].get(sname, 0) >= val:
            return
        self.mark[eng][sname] = val
        sem = self.sem[sname]
        self.prog[eng].append(lambda e, sem=sem, val=val: e.wait_ge(sem, val))

    def waits(self, eng, d):
        if d:
            for k, v in d.items():
                self.wait(eng, k, v)

    def run(self, eng, fn, inc=None, n=1, after=None):
        self.waits(eng, after)
        tgt = None
        if inc:
            self.cnt[inc] += n
            tgt = self.cnt[inc]
            sem = self.sem[inc]
            self.prog[eng].append(lambda e, fn=fn, sem=sem, n=n: fn(e).then_inc(sem, n))
        else:
            self.prog[eng].append(lambda e, fn=fn: fn(e))
        return tgt

    def dma(self, out, in_, after=None, eng="sync"):
        sem = "d2" if eng == "gpsimd" else "d"
        return self.run(eng, lambda e, o=out, i=in_: e.dma_start(out=o, in_=i),
                        inc=sem, n=16, after=after)

    def bank_prod_after(self, name):
        st = self.bank.get(name)
        if st and st.get("cons"):
            return {st["cons"][0]: st["cons"][1]}
        return {}

    def set_bank_prod(self, name, val):
        self.bank[name] = {"prod": ("t", val), "cons": None}

    def bank_cons_after(self, name):
        st = self.bank.get(name)
        if st and st.get("prod"):
            return {st["prod"][0]: st["prod"][1]}
        return {}

    def set_bank_cons(self, name, sem, val):
        st = self.bank.setdefault(name, {"prod": None, "cons": None})
        st["cons"] = (sem, val)


def merge(*ds):
    out = {}
    for d in ds:
        if d:
            for k, v in d.items():
                if v is None:
                    continue
                if k not in out or out[k] < v:
                    out[k] = v
    return out


def bcol(tile, i):
    col = tile[:, i:i + 1]
    return bass.AP(col.tensor, col.offset, [list(col.ap[0]), [0, 128]])


def build_nc(debug=False):
    nc = bass.Bass("TRN2", target_bir_lowering=False)

    x_h = nc.dram_tensor("x", [128, DMT, T], F32, kind="ExternalInput").ap()
    out_h = nc.dram_tensor("out", [128, DMT, T], F32, kind="ExternalOutput").ap()

    w_in = [nc.dram_tensor(f"win{b}", [DM, 2 * DIN], F32, kind="ExternalInput").ap() for b in (1, 2)]
    w_out = [nc.dram_tensor(f"wout{b}", [DIN, DM], F32, kind="ExternalInput").ap() for b in (1, 2)]
    w_xp = [nc.dram_tensor(f"wxp{b}", [128, DINT, 96], F32, kind="ExternalInput").ap() for b in (1, 2)]
    w_dt = [nc.dram_tensor(f"wdt{b}", [64, DIN], F32, kind="ExternalInput").ap() for b in (1, 2)]
    w_f1 = nc.dram_tensor("wf1", [DM, DFF], F32, kind="ExternalInput").ap()
    w_f2 = nc.dram_tensor("wf2", [DFF, DM], F32, kind="ExternalInput").ap()
    w_g = nc.dram_tensor("wg", [DM, HID], I32, kind="ExternalInput").ap()
    w_u = nc.dram_tensor("wu", [DM, HID], I32, kind="ExternalInput").ap()
    w_d = nc.dram_tensor("wd", [HID, DM], I32, kind="ExternalInput").ap()

    v_g = nc.dram_tensor("v_g", [128, DMT], F32, kind="ExternalInput").ap()
    v_b = nc.dram_tensor("v_b", [128, DMT], F32, kind="ExternalInput").ap()
    v_rms = [nc.dram_tensor(f"v_rms{b}", [128, DMT], F32, kind="ExternalInput").ap() for b in (1, 2)]
    v_inb = [nc.dram_tensor(f"v_inb{b}", [128, 32], F32, kind="ExternalInput").ap() for b in (1, 2)]
    v_cw = [nc.dram_tensor(f"v_cw{b}", [128, 64], F32, kind="ExternalInput").ap() for b in (1, 2)]
    v_cb = [nc.dram_tensor(f"v_cb{b}", [128, DINT], F32, kind="ExternalInput").ap() for b in (1, 2)]
    v_dtb = [nc.dram_tensor(f"v_dtb{b}", [128, DINT], F32, kind="ExternalInput").ap() for b in (1, 2)]
    v_alog = [nc.dram_tensor(f"v_alog{b}", [128, DINT * NST], F32, kind="ExternalInput").ap() for b in (1, 2)]
    v_D = [nc.dram_tensor(f"v_D{b}", [128, DINT], F32, kind="ExternalInput").ap() for b in (1, 2)]
    v_outb = [nc.dram_tensor(f"v_outb{b}", [128, DMT], F32, kind="ExternalInput").ap() for b in (1, 2)]
    v_fb1 = nc.dram_tensor("v_fb1", [128, DFFT], F32, kind="ExternalInput").ap()
    v_fb2 = nc.dram_tensor("v_fb2", [128, DMT], F32, kind="ExternalInput").ap()
    selbc_h = nc.dram_tensor("selbc_h", [96, 32], BF16, kind="ExternalInput").ap()
    ident_h = nc.dram_tensor("ident_h", [128, 128], BF16, kind="ExternalInput").ap()

    s_win = [nc.dram_tensor(f"s_win{b}", [DM, 2 * DIN], BF16, kind="Internal").ap() for b in (1, 2)]
    s_wout = [nc.dram_tensor(f"s_wout{b}", [DIN, DM], BF16, kind="Internal").ap() for b in (1, 2)]
    s_wf1 = nc.dram_tensor("s_wf1", [DM, DFF], BF16, kind="Internal").ap()
    s_wf2 = nc.dram_tensor("s_wf2", [DFF, DM], BF16, kind="Internal").ap()
    s_wg = nc.dram_tensor("s_wg", [DM, HID], BF16, kind="Internal").ap()
    s_wu = nc.dram_tensor("s_wu", [DM, HID], BF16, kind="Internal").ap()
    s_wd = nc.dram_tensor("s_wd", [HID, DM], BF16, kind="Internal").ap()
    s_xmid = nc.dram_tensor("s_xmid", [128, DMT, T], F32, kind="Internal").ap()

    es_top = ExitStack()
    kb = KB(nc, es_top)

    dbg_names = []
    dbg_h = nc.dram_tensor("dbg", [128, 40 * 4096], F32, kind="ExternalOutput").ap() if debug else None
    dbgb_h = nc.dram_tensor("dbgb", [128, 40 * 4096], BF16, kind="ExternalOutput").ap() if debug else None
    dbg_off = [0, 0]

    def dump(name, ap, count, bf=False, parts=128):
        if not debug:
            return
        k = 1 if bf else 0
        dst = (dbgb_h if bf else dbg_h)[0:parts, dbg_off[k]:dbg_off[k] + count]
        aftr = {kk: kb.cnt[kk] for kk in ("s", "v", "t", "g")}
        tgt = kb.dma(dst, ap, after=aftr)
        dbg_names.append((name, k, dbg_off[k], count))
        dbg_off[k] += count
        for e in ("scalar", "vector", "tensor", "gpsimd", "sync"):
            kb.wait(e, "d", tgt)
    nc._dbg_names = dbg_names

    sb = lambda name, shape, dt: es_top.enter_context(nc.sbuf_tensor(name, shape, dt))
    onescol = sb("onescol", [128, 1], F32)
    ones_k1f = sb("ones_k1f", [1, 128], F32)
    cconst = sb("cconst", [128, 8], F32)
    selbc = sb("selbc", [96, 32], BF16)
    identb = sb("identb", [128, 128], BF16)
    cg = sb("cg", [128, DMT], F32)
    cb_ = sb("cb_", [128, DMT], F32)
    cinb = [sb(f"cinb{b}", [128, 32], F32) for b in (0, 1)]
    ccw = [sb(f"ccw{b}", [128, 64], F32) for b in (0, 1)]
    ccb = [sb(f"ccb{b}", [128, DINT], F32) for b in (0, 1)]
    cdtb = [sb(f"cdtb{b}", [128, DINT], F32) for b in (0, 1)]
    cnegcb = [sb(f"cnegcb{b}", [128, DINT], F32) for b in (0, 1)]
    cexpA = [sb(f"cexpA{b}", [128, DINT * NST], F32) for b in (0, 1)]
    cD = [sb(f"cD{b}", [128, DINT], F32) for b in (0, 1)]
    coutb = [sb(f"coutb{b}", [128, DMT], F32) for b in (0, 1)]
    cfb1 = sb("cfb1", [128, DFFT], F32)
    cfb2 = sb("cfb2", [128, DMT], F32)
    crms = [sb(f"crms{b}", [128, DMT], F32) for b in (0, 1)]
    xpw = [sb(f"xpw{b}", [128, DINT * 96], BF16) for b in (0, 1)]
    dtw = [sb(f"dtw{b}", [64, DIN], BF16) for b in (0, 1)]
    tails = [sb(f"tails{b}", [128, DINT * 3], F32) for b in (0, 1)]
    state = [sb(f"state{b}", [128, DINT * NST], F32) for b in (0, 1)]

    PS = [es_top.enter_context(nc.psum_tensor(f"ps{i}", [128, 512], F32)) for i in range(8)]

    es_pre = ExitStack()
    wtf = [es_pre.enter_context(nc.sbuf_tensor(f"wtf{i}", [128, 4096], F32))
           for i in range(2)]
    wto = [es_pre.enter_context(nc.sbuf_tensor(f"wto{i}", [128, 4096], BF16))
           for i in range(2)]
    cast_guard = [0, 0]

    def gmemset(ap, val):
        return kb.run("gpsimd", lambda e, a=ap, v=val: e.memset(a, v), inc="g")

    gmemset(onescol[:, :], 1.0)
    gmemset(ones_k1f[:, :], 1.0)
    for j, val in enumerate([EPS, EPS * EPS, 1.0 + EPS, -0.5, -1.0, float(R127),
                             1.0 / DM, 1.0]):
        gmemset(cconst[:, j:j + 1], val)
    for b in (0, 1):
        gmemset(tails[b][:, :], 0.0)
        gmemset(state[b][:, :], 0.0)
    g_done = kb.cnt["g"]

    for dst, src in [(cg, v_g), (cb_, v_b), (cfb1, v_fb1), (cfb2, v_fb2),
                     (selbc, selbc_h), (identb, ident_h)]:
        kb.dma(dst[:, :], src)
    for b in (0, 1):
        for dst, src in [(cinb[b], v_inb[b]), (ccw[b], v_cw[b]), (ccb[b], v_cb[b]),
                         (cD[b], v_D[b]), (coutb[b], v_outb[b]), (crms[b], v_rms[b])]:
            kb.dma(dst[:, :], src)

    for b in (0, 1):
        d1 = kb.dma(wtf[0][:, 0:DINT * NST], v_alog[b])
        se_ = kb.run("scalar", lambda e, o=cexpA[b][:, :], i=wtf[0][:, 0:DINT * NST]:
                     e.activation(out=o, in_=i, func=AF.Exp), inc="s", after={"d": d1})
        kb.run("vector", lambda e, o=cexpA[b][:, :], i=cexpA[b][:, :]:
               e.tensor_scalar(out=o, in0=i, scalar1=-1.0, scalar2=None, op0=ALU.mult),
               inc="v", after={"s": se_})
        kb.dma(cdtb[b][:, :], v_dtb[b])
        dcb = kb.dma(wtf[0][:, DINT * NST:DINT * NST + DINT], v_cb[b],
                     after={"s": se_})
        kb.run("vector", lambda e, o=cnegcb[b][:, :], i=wtf[0][:, DINT * NST:DINT * NST + DINT]:
               e.tensor_scalar(out=o, in0=i, scalar1=-1.0, scalar2=None, op0=ALU.mult),
               inc="v", after={"d": dcb})
    cnegA = cexpA
    v_after_pre = kb.cnt["v"]

    for b in (0, 1):
        d1 = kb.dma(wtf[1][:, 0:DINT * 96], w_xp[b].rearrange("p j m -> p (j m)"),
                    after={"v": v_after_pre, "s": cast_guard[1]})
        cg1 = kb.run("scalar", lambda e, o=xpw[b][:, :], i=wtf[1][:, 0:DINT * 96]:
                     e.activation(out=o, in_=i, func=AF.Identity), inc="s", after={"d": d1})
        cast_guard[1] = cg1
        d2 = kb.dma(wtf[1][0:64, 2048:4096], w_dt[b], after={"s": cast_guard[1]})
        cg1 = kb.run("scalar", lambda e, o=dtw[b][:, :], i=wtf[1][0:64, 2048:4096]:
                     e.activation(out=o, in_=i, func=AF.Identity), inc="s", after={"d": d2})
        cast_guard[1] = cg1

    def cast_weight_scaled(src, dst, rows, cols, scale_tile, j0=0):
        for j in range(rows // 128):
            jj = (j0 + j) % 2
            d1 = kb.dma(wtf[jj][:, 0:cols], src[j * 128:(j + 1) * 128, :],
                        after={"s": cast_guard[jj]})
            sc = scale_tile[:, j:j + 1]
            cst = kb.run("scalar", lambda e, o=wto[jj][:, 0:cols],
                         i=wtf[jj][:, 0:cols], s=sc:
                         e.activation(out=o, in_=i, func=AF.Identity, scale=s),
                         inc="s", after={"d": d1})
            cast_guard[jj] = cst
            kb.dma(dst[j * 128:(j + 1) * 128, :], wto[jj][:, 0:cols],
                   after={"s": cst})

    cast_weight_scaled(w_in[0], s_win[0], DM, 2 * DIN, crms[0])
    cast_weight_scaled(w_in[1], s_win[1], DM, 2 * DIN, crms[1], j0=0)
    cast_d = {}
    for src, dst, key in [(w_out[0], s_wout[0], "wout1"), (w_out[1], s_wout[1], "wout2"),
                          (w_f1, s_wf1, "wf1"), (w_f2, s_wf2, "wf2"),
                          (w_g, s_wg, "wg"), (w_u, s_wu, "wu"), (w_d, s_wd, "wd")]:
        cast_d[key] = kb.dma(dst, src, eng="gpsimd")

    pre_s, pre_v, pre_g, pre_d = kb.cnt["s"], kb.cnt["v"], kb.cnt["g"], kb.cnt["d"]
    es_pre.close()

    es1 = ExitStack()
    sb1 = lambda name, shape, dt: es1.enter_context(nc.sbuf_tensor(name, shape, dt))
    x_ln = sb1("x_ln", [128, DMT * TC], F32)
    h_sb = sb1("h_sb", [128, DMT * TC], F32)
    hr_sb = sb1("hr_sb", [128, DMT * TC], BF16)
    st0 = sb1("st0", [1, TC], F32)
    bcA = sb1("bcA", [128, TC], F32)
    bcB = sb1("bcB", [128, TC], F32)
    cbuf = [sb1(f"cbuf{i}", [128, TC], F32) for i in range(2)]
    sgbuf = [sb1(f"sgbuf{i}", [128, TC], F32) for i in range(2)]
    ebuf2 = sb1("ebuf2", [128, TC], F32)
    xcp = sb1("xcp", [128, DINT * (TC + 3)], BF16)
    dl3 = [sb1(f"dl3_{i}", [128, TC], BF16) for i in range(3)]
    z_sb = sb1("z_sb", [128, DINT * TC], BF16)
    xc_sb = sb1("xc_sb", [128, DINT * TC], BF16)
    Ball = sb1("Ball", [128, NST * TC], BF16)
    Call = sb1("Call", [128, (NST // 2) * TC2P], BF16)
    u_sb = sb1("u_sb", [128, DMT * TC], BF16)
    dbc = sb1("dbc", [96, TC], BF16)
    wb_in = [sb1(f"wb_in{i}", [128, DMT * 256], BF16) for i in range(2)]
    wb_out = [sb1(f"wb_out{i}", [128, DINT * 128], BF16) for i in range(2)]
    dxc0 = sb1("dxc0", [128, TC], BF16)
    dAb = [sb1(f"dAb{i}", [128, TC2P], BF16) for i in range(2)]
    dBu4 = sb1("dBu4", [128, TC2P], BF16)
    hbb = [sb1(f"hbb{i}", [128, TC2P], BF16) for i in range(2)]
    chb = [sb1(f"chb{i}", [128, TC2P], BF16) for i in range(3)]
    xcd = [sb1(f"xcd{i}", [128, TC], BF16) for i in range(2)]
    gmemset(dAb[0][:, TC:TC + 1], 0.0)
    gmemset(dAb[1][:, TC:TC + 1], 0.0)
    g_done = kb.cnt["g"]

    def slc(tile, idx, width, off=0):
        return tile[:, idx * width + off: idx * width + off + width]

    def xcp_sl(dt, a, b):
        return xcp[:, dt * (TC + 3) + a: dt * (TC + 3) + b]

    def s_act(out, in_, func, bias=None, scale=None, after=None):
        kw = {}
        if bias is not None:
            kw["bias"] = bias
        if scale is not None:
            kw["scale"] = scale
        return kb.run("scalar", lambda e, o=out, i=in_, f=func, kw=kw:
                      e.activation(out=o, in_=i, func=f, **kw), inc="s", after=after)

    def v_tt(out, a, b, op, after=None):
        return kb.run("vector", lambda e, o=out, x=a, y=b, p=op:
                      e.tensor_tensor(out=o, in0=x, in1=y, op=p), inc="v", after=after)

    def v_ts(out, a, s1, op0, s2=None, op1=None, after=None):
        if isinstance(s1, np.floating):
            s1 = float(s1)
        if isinstance(s2, np.floating):
            s2 = float(s2)
        def f(e, o=out, x=a, s1=s1, s2=s2, op0=op0, op1=op1):
            if op1 is None:
                return e.tensor_scalar(out=o, in0=x, scalar1=s1, scalar2=None, op0=op0)
            return e.tensor_scalar(out=o, in0=x, scalar1=s1, scalar2=s2, op0=op0, op1=op1)
        return kb.run("vector", f, inc="v", after=after)

    def v_stt(out, in0, scalar, in1, op0, op1, after=None):
        if isinstance(scalar, np.floating):
            scalar = float(scalar)
        return kb.run("vector", lambda e, o=out, a=in0, s=scalar, b=in1, p0=op0, p1=op1:
                      e.scalar_tensor_tensor(out=o, in0=a, scalar=s, in1=b, op0=p0, op1=p1),
                      inc="v", after=after)

    def t_mm(psname, out, lhsT, rhs, start, stop, after=None, inc=False):
        return kb.run("tensor", lambda e, o=out, l=lhsT, r=rhs, st=start, sp=stop:
                      e.matmul(o, l, r, start=st, stop=sp),
                      inc=("t" if inc else None), after=after)

    def stats_pass(tiles_fn, n, pa, after_each=None):
        tgt = None
        for j in range(n):
            rhs, aft = tiles_fn(j)
            a = merge(kb.bank_prod_after(pa) if j == 0 else None, aft, after_each)
            tgt = t_mm(pa, PS[pa][0:1, :], onescol[:, 0:1], rhs,
                       start=(j == 0), stop=(j == n - 1), after=a,
                       inc=(j == n - 1))
        kb.set_bank_prod(pa, tgt)
        return tgt

    st_guard = {"t": 0}

    def bc_sum(dst, src_ps, stage, pa, after, scale_ci=6):
        s1 = s_act(stage[:, :], src_ps, AF.Identity,
                   after=merge(after, {"g": g_done}, {"t": st_guard["t"]}))
        a = merge(kb.bank_prod_after(pa), {"s": s1})
        tm = t_mm(pa, PS[pa][:, :], ones_k1f[:, :], stage[:, :],
                  start=True, stop=True, after=a, inc=True)
        kb.set_bank_prod(pa, tm)
        st_guard["t"] = tm
        sc = s_act(dst, PS[pa][:, :], AF.Identity,
                   scale=(cconst[:, scale_ci:scale_ci + 1]
                          if scale_ci is not None else None),
                   after={"t": tm})
        kb.set_bank_cons(pa, "s", sc)
        return sc

    x3 = x_ln[:, :].rearrange("p (j t) -> p j t", j=DMT)

    ph1_barrier = {"s": pre_s, "v": pre_v, "g": pre_g}

    P = {
        "xmid_v": 0, "xmid_dma": 0, "outp_t": 0, "inproj_t": 0,
    }
    state_g = {}
    wb_guard = [0, 0]
    vdx_cur = {}

    def emit_ln(c):
        dx = kb.dma(x3, x_h[:, :, c * TC:(c + 1) * TC],
                    after=merge({"v": P["xmid_v"]}, ph1_barrier if c == 0 else None))
        def xt(j, dx=dx):
            return slc(x_ln, j, TC), {"d": dx, "g": g_done}
        t_s1 = stats_pass(xt, DMT, 6)
        sq_s = [None, None]
        tgt = None
        for j in range(DMT):
            a = merge({"d": dx}, {"t": sq_s[j % 2]} if sq_s[j % 2] else None)
            sq_s[j % 2] = s_act(cbuf[j % 2][:, :], slc(x_ln, j, TC), AF.Square, after=a)
            aft = merge({"s": sq_s[j % 2]}, kb.bank_prod_after(7) if j == 0 else None)
            tgt = t_mm(7, PS[7][0:1, :], onescol[:, 0:1], cbuf[j % 2][:, :],
                       start=(j == 0), stop=(j == DMT - 1), after=aft, inc=True)
            sq_s[j % 2] = tgt
        kb.set_bank_prod(7, tgt)
        t_s2 = tgt
        bmu = bc_sum(bcA[:, :], PS[6][0:1, :], st0, 4, {"t": t_s1})
        kb.set_bank_cons(6, "s", bmu)
        bex = bc_sum(bcB[:, :], PS[7][0:1, :], st0, 5, {"t": t_s2, "s": bmu})
        kb.set_bank_cons(7, "s", bex)
        vq = v_tt(cbuf[0][:, :], bcA[:, :], bcA[:, :], ALU.mult,
                  after={"s": bmu, "t": t_s2})
        vvar = v_tt(bcB[:, :], bcB[:, :], cbuf[0][:, :], ALU.subtract,
                    after={"s": bex})
        vcen = None
        for j in range(DMT):
            vcen = v_tt(slc(x_ln, j, TC), slc(x_ln, j, TC), bcA[:, :], ALU.subtract)
        sa1a = s_act(bcA[:, :], bcB[:, :], AF.Ln, bias=cconst[:, 0:1],
                     after={"v": vcen})
        sa1 = s_act(bcA[:, :], bcA[:, :], AF.Exp, scale=cconst[:, 3:4],
                    after={"s": sa1a})
        sa2a = s_act(bcB[:, :], bcB[:, :], AF.Ln, scale=cconst[:, 2:3],
                     bias=cconst[:, 1:2], after={"s": sa1})
        sa2 = s_act(bcB[:, :], bcB[:, :], AF.Exp, scale=cconst[:, 3:4],
                    after={"s": sa2a})
        h_done = []
        vv = None
        for j in range(DMT):
            vv = v_tt(cbuf[j % 2][:, :], slc(x_ln, j, TC), bcB[:, :], ALU.mult,
                      after=merge({"s": sa2} if j == 0 else None,
                                  {"s": h_done[j - 2]} if j >= 2 else None))
            hd = s_act(slc(h_sb, j, TC), cbuf[j % 2][:, :], AF.Identity,
                       bias=cb_[:, j:j + 1], scale=cg[:, j:j + 1],
                       after=merge({"v": vv}, {"d": P["xmid_dma"]} if j == 0 else None))
            h_done.append(hd)
        vlp = None
        for j in range(DMT):
            vlp = v_tt(slc(x_ln, j, TC), slc(x_ln, j, TC), bcA[:, :], ALU.mult,
                       after={"s": sa1} if j == 0 else None)
        if c == 0:
            dump("h", h_sb[:, :], DMT * TC)
            dump("LNp", x_ln[:, :], DMT * TC)
        sqh = [None, None]
        tgt = None
        for j in range(DMT):
            a = merge({"s": h_done[j]}, {"t": sqh[j % 2]} if sqh[j % 2] else None,
                      {"v": vv} if j == 0 else None,
                      kb.bank_prod_after(6) if j == 0 else None)
            ss = s_act(cbuf[j % 2][:, :], slc(h_sb, j, TC), AF.Square, after=a)
            aft = merge({"s": ss}, kb.bank_prod_after(6) if j == 0 else None)
            tgt = t_mm(6, PS[6][0:1, :], onescol[:, 0:1], cbuf[j % 2][:, :],
                       start=(j == 0), stop=(j == DMT - 1), after=aft, inc=True)
            sqh[j % 2] = tgt
        kb.set_bank_prod(6, tgt)
        brm = bc_sum(bcB[:, :], PS[6][0:1, :], st0, 4, {"t": tgt, "v": vv})
        kb.set_bank_cons(6, "s", brm)
        srm = s_act(bcB[:, :], bcB[:, :], AF.Ln, bias=cconst[:, 0:1],
                    after={"s": brm})
        srm = s_act(bcB[:, :], bcB[:, :], AF.Exp, scale=cconst[:, 3:4],
                    after={"s": srm})
        hr_done = None
        for j in range(DMT):
            hr_done = v_tt(slc(hr_sb, j, TC), slc(h_sb, j, TC), bcB[:, :], ALU.mult,
                           after=merge({"s": srm},
                                       {"t": P["inproj_t"]} if j == 0 else None))
        if c == 0:
            dump("hr", hr_sb[:, :], DMT * TC, bf=True)
        return {"hr": hr_done, "sqh": tgt, "c": c}

    def make_stage1(c, b, lnh):
        sw = s_win[b].rearrange("(j p) m -> p j m", p=128)
        ctx = {"xc": {}, "evt": {}, "evs": {}, "xproj_t": None, "dl": {},
               "zev": {}, "sg": {}, "c": c, "b": b}

        def inproj_m(m):
            buf = m % 2
            wdma = kb.dma(wb_in[buf][:, 0:DMT * 128].rearrange(
                              "p (j m) -> p j m", j=DMT),
                          sw[:, :, m * 128:(m + 1) * 128],
                          after={"t": wb_guard[buf],
                                 "v": lnh["hr"] if m == 0 else None})
            pa = m % 2
            aft = merge({"d": wdma}, {"v": lnh["hr"]},
                        kb.bank_prod_after(pa), kb.bank_cons_after(pa))
            tl = None
            for k in range(DMT):
                tl = t_mm(pa, PS[pa][:, :], wb_in[buf][:, k * 128:(k + 1) * 128],
                          slc(hr_sb, k, TC), start=(k == 0), stop=(k == DMT - 1),
                          after=aft if k == 0 else None, inc=(k == DMT - 1))
            kb.set_bank_prod(pa, tl)
            wb_guard[buf] = tl
            return pa, tl

        def pieceT(dt):
            pa, tl = inproj_m(dt)
            sev = s_act(xcp_sl(dt, 3, TC + 3), PS[pa][:, :], AF.Identity,
                        bias=cinb[b][:, dt:dt + 1], after={"t": tl})
            kb.set_bank_cons(pa, "s", sev)
            ctx["evt"][dt] = tl

        def piece(dt):
            st = s_act(xcp_sl(dt, 0, 3), tails[b][:, dt * 3:(dt + 1) * 3],
                       AF.Identity, after={"g": g_done})
            va = v_ts(cbuf[dt % 2][:, :], xcp_sl(dt, 0, TC),
                      ccw[b][:, dt * 4: dt * 4 + 1], ALU.mult,
                      after={"s": st, "t": lnh["sqh"]})
            for k2 in (1, 2, 3):
                va = v_stt(cbuf[dt % 2][:, :], xcp_sl(dt, k2, TC + k2),
                           ccw[b][:, dt * 4 + k2: dt * 4 + k2 + 1],
                           cbuf[dt % 2][:, :], ALU.mult, ALU.add)
            sb_ = s_act(sgbuf[dt % 2][:, :], cbuf[dt % 2][:, :], AF.Exp,
                        scale=cconst[:, 4:5],
                        bias=cnegcb[b][:, dt:dt + 1], after={"v": va})
            sb_ = s_act(sgbuf[dt % 2][:, :], sgbuf[dt % 2][:, :], AF.Ln,
                        bias=cconst[:, 7:8], after={"s": sb_})
            ctx["sg"][dt] = s_act(sgbuf[dt % 2][:, :], sgbuf[dt % 2][:, :],
                                  AF.Exp, scale=cconst[:, 4:5],
                                  after={"s": sb_})
            s_act(tails[b][:, dt * 3:(dt + 1) * 3], xcp_sl(dt, TC, TC + 3),
                  AF.Identity)
            if dt >= 1:
                conv_fin(dt - 1)
            if dt >= 2:
                k = dt - 2
                aft = merge(kb.bank_prod_after(5) if k == 0 else None,
                            {"v": ctx["xc"][k]})
                ctx["xproj_t"] = t_mm(5, PS[5][0:96, :],
                                      xpw[b][:, k * 96:(k + 1) * 96],
                                      slc(xc_sb, k, TC), start=(k == 0),
                                      stop=False, after=aft,
                                      inc=False)

        def conv_fin(dt):
            ctx["xc"][dt] = v_stt(slc(xc_sb, dt, TC), cbuf[dt % 2][:, :],
                                  ccb[b][:, dt:dt + 1], sgbuf[dt % 2][:, :],
                                  ALU.add, ALU.mult,
                                  after=merge({"s": ctx["sg"][dt]},
                                              {"v": vdx_cur.get(dt)}))

        def dtmm_softplus(dt):
            pa = 6 + (dt % 2)
            aft = merge(kb.bank_prod_after(pa), kb.bank_cons_after(pa),
                        {"s": ctx["dbc_s"]})
            tm = t_mm(pa, PS[pa][:, :], dtw[b][:, dt * 128:(dt + 1) * 128],
                      dbc[0:64, :], start=True, stop=True, after=aft, inc=True)
            kb.set_bank_prod(pa, tm)
            se = s_act(ebuf2[:, :], PS[pa][:, :], AF.Exp,
                       bias=cdtb[b][:, dt:dt + 1], after={"t": tm})
            kb.set_bank_cons(pa, "s", se)
            ctx["dl"][dt] = s_act(dl3[dt % 3][:, :], ebuf2[:, :], AF.Ln,
                                  bias=cconst[:, 7:8],
                                  after=merge({"s": se},
                                              {"v": vdx_cur.get((dt - 2) % DINT)}))

        def tail():
            conv_fin(DINT - 1)
            k = DINT - 2
            aft = merge({"v": ctx["xc"][k]},
                        kb.bank_prod_after(5) if ctx["xproj_t"] is None else None)
            t_mm(5, PS[5][0:96, :], xpw[b][:, k * 96:(k + 1) * 96],
                 slc(xc_sb, k, TC), start=(ctx["xproj_t"] is None),
                 stop=False, after=aft, inc=False)
            ctx["xproj_t"] = 1
            k = DINT - 1
            tl = t_mm(5, PS[5][0:96, :], xpw[b][:, k * 96:(k + 1) * 96],
                      slc(xc_sb, k, TC), start=False,
                      stop=True, after={"v": ctx["xc"][k]}, inc=True)
            kb.set_bank_prod(5, tl)
            ctx["dbc_s"] = s_act(dbc[:, :], PS[5][0:96, :], AF.Identity,
                                 after={"t": tl})
            kb.set_bank_cons(5, "s", ctx["dbc_s"])
            if c == 0 and b == 0:
                dump("dbc", dbc[:, :], 512, bf=True, parts=96)
            dtmm_softplus(0)
            dtmm_softplus(1)

        ctx["piece"] = piece
        ctx["pieceT"] = pieceT
        ctx["tail"] = tail
        ctx["dtmm_softplus"] = dtmm_softplus
        return ctx

    def emit_zpiece(ctx, i):
        b = ctx["b"]
        m = DINT + i
        buf = m % 2
        sw = s_win[b].rearrange("(j p) m -> p j m", p=128)
        wdma = kb.dma(wb_in[buf][:, 0:DMT * 128].rearrange(
                          "p (j m) -> p j m", j=DMT),
                      sw[:, :, m * 128:(m + 1) * 128],
                      after={"t": wb_guard[buf]})
        pa = m % 2
        aft = merge({"d": wdma}, kb.bank_prod_after(pa),
                    kb.bank_cons_after(pa))
        tl = None
        for k in range(DMT):
            tl = t_mm(pa, PS[pa][:, :], wb_in[buf][:, k * 128:(k + 1) * 128],
                      slc(hr_sb, k, TC), start=(k == 0), stop=(k == DMT - 1),
                      after=aft if k == 0 else None, inc=(k == DMT - 1))
        kb.set_bank_prod(pa, tl)
        wb_guard[buf] = tl
        sev = s_act(slc(z_sb, i, TC), PS[pa][:, :], AF.Silu,
                    bias=cinb[b][:, DINT + i:DINT + i + 1],
                    after=merge({"t": tl},
                                {"t": P["outp_t"]} if i == 0 else None))
        kb.set_bank_cons(pa, "s", sev)
        ctx["zev"][i] = sev

    def emit_scan(ctx, nxt_cb, nxt_cb_top=None):
        c, b = ctx["c"], ctx["b"]
        scan_v = {}
        yfin_v = {}
        gsnap = kb.cnt["g"]
        bcast_s = {}

        def bcast(i):
            pa = 6 + (i % 2)
            aft = merge(kb.bank_prod_after(pa), kb.bank_cons_after(pa),
                        {"s": ctx["dbc_s"]}, {"g": g_done})
            tm = t_mm(pa, PS[pa][:, :], bcol(selbc, i), dbc[:, :],
                      start=True, stop=True, after=aft, inc=True)
            kb.set_bank_prod(pa, tm)
            if i < 16:
                dst = slc(Ball, i, TC)
            else:
                n_ = i - 16
                o_ = (n_ // 2) * TC2P + (n_ % 2) * (TC + 1)
                dst = Call[:, o_:o_ + TC]
            sc = s_act(dst, PS[pa][:, :], AF.Identity, after={"t": tm})
            kb.set_bank_cons(pa, "s", sc)
            return sc

        pstr = list(dxc0[:, :].ap[0])
        psy_t = {}
        psy_ch = {}
        for dt in range(DINT):
            vdx = v_tt(dxc0[:, :], dl3[dt % 3][:, :], slc(xc_sb, dt, TC),
                       ALU.mult, after={"s": ctx["dl"][dt], "g": g_done,
                                        "v": ctx["xc"][dt]})
            vdx_cur[dt] = vdx
            for cb2fn in (nxt_cb_top or {}).get(dt, []):
                cb2fn()
            pyb = 3 + (dt % 2)
            sxc = s_act(xcd[dt % 2][:, :], slc(xc_sb, dt, TC),
                        AF.Identity, scale=cD[b][:, dt:dt + 1],
                        after={"t": psy_t.get(dt - 2)})
            aft0 = merge({"s": sxc}, kb.bank_prod_after(pyb))
            im = t_mm(pyb, PS[pyb][:, :], identb[:, :], xcd[dt % 2][:, :],
                      start=True, stop=False, after=aft0, inc=True)
            psy_t[dt] = im
            def emit_sda(np2):
                p2 = np2 // 2
                si = dt * NST + np2
                prev_scan = scan_v.get((dt, np2 - 4))
                if prev_scan is None and np2 < 4:
                    prev_scan = scan_v.get((dt - 1, NST + np2 - 4))
                sa = s_act(dAb[p2 % 2][:, 0:TC], dl3[dt % 3][:, :], AF.Exp,
                           scale=cnegA[b][:, si:si + 1],
                           after={"v": prev_scan})
                return s_act(dAb[p2 % 2][:, TC + 1:TC2P], dl3[dt % 3][:, :],
                             AF.Exp, scale=cnegA[b][:, si + 1:si + 2],
                             after={"s": sa})
            sda_t = {}
            sda_t[0] = emit_sda(0)
            sda_t[2] = emit_sda(2)
            for np_ in range(0, NST, 2):
                if np_ == 2:
                    if dt < DINT - 2:
                        ctx["dtmm_softplus"](dt + 2)
                p = np_ // 2
                sidx = dt * NST + np_
                if dt == 0:
                    for i2 in (np_, np_ + 1, 16 + np_, 17 + np_):
                        bcast_s[i2] = bcast(i2)
                sda = sda_t[np_]
                dxv = bass.AP(dxc0[:, :].tensor, dxc0[:, :].offset,
                              [pstr, [0, 2], [1, TC]])
                dbo = bass.AP(dBu4[:, :].tensor, dBu4[:, :].offset,
                              [list(dBu4[:, :].ap[0]), [TC + 1, 2], [1, TC]])
                vdb = v_tt(dbo, dxv,
                           Ball[:, np_ * TC:(np_ + 2) * TC].rearrange(
                               "p (n t) -> p n t", n=2), ALU.mult,
                           after=merge({"s": bcast_s.get(np_ + 1), "v": vdx}
                                       if dt == 0 else None,
                                       {"v": scan_v.get((dt, np_ - 2)) or
                                        (scan_v.get((dt - 1, NST + np_ - 2))
                                         if np_ < 2 else None)}))
                gsep = kb.run("gpsimd",
                              lambda e, o=dBu4[:, TC:TC + 1],
                              i=state[b][:, sidx + 1:sidx + 2]:
                              e.tensor_copy(out=o, in_=i), inc="g")
                if np_ >= 4:
                    gprev = state_g.get((b, dt, np_ - 4))
                elif dt >= 1:
                    gprev = state_g.get((b, dt - 1, NST + np_ - 4))
                else:
                    gprev = gsnap
                ginit = state_g.get((b, dt, np_))
                vsc = kb.run("vector",
                             lambda e, o=hbb[p % 2][:, :], a=dAb[p % 2][:, :],
                             bb2=dBu4[:, :],
                             ini=state[b][:, sidx:sidx + 1]:
                             e.tensor_tensor_scan(out=o, data0=a, data1=bb2,
                                                  initial=ini, op0=ALU.mult,
                                                  op1=ALU.add),
                             inc="v",
                             after=merge({"s": sda}, {"g": gprev},
                                         {"g": ginit}, {"g": gsep}))
                scan_v[(dt, np_)] = vsc
                if np_ + 4 < NST:
                    sda_t[np_ + 4] = emit_sda(np_ + 4)
                hb_ = hbb[p % 2][:, TC - 1:TC]
                hsrc = bass.AP(hb_.tensor, hb_.offset,
                               [list(hb_.ap[0]), [TC + 1, 2]])
                state_g[(b, dt, np_)] = kb.run(
                    "gpsimd", lambda e, o=state[b][:, sidx:sidx + 2], i=hsrc:
                    e.tensor_copy(out=o, in_=i),
                    inc="g", after={"v": vsc})
                pch = psy_ch.get((dt, p - 3))
                if pch is None and p < 3:
                    pch = psy_ch.get((dt - 1, p + 5))
                vch = v_tt(chb[p % 3][:, :], hbb[p % 2][:, :],
                           Call[:, p * TC2P:(p + 1) * TC2P], ALU.mult,
                           after=merge({"s": bcast_s.get(17 + np_)}
                                       if dt == 0 else None,
                                       {"t": pch}))
                t_mm(pyb, PS[pyb][:, :], identb[:, :], chb[p % 3][:, 0:TC],
                     start=False, stop=False, after={"v": vch}, inc=False)
                psy_ch[(dt, p)] = t_mm(pyb, PS[pyb][:, :], identb[:, :],
                                       chb[p % 3][:, TC + 1:TC2P],
                                       start=False, stop=(np_ == NST - 2),
                                       after=None, inc=True)
            kb.set_bank_prod(pyb, psy_ch[(dt, 7)])
            if 1 <= dt <= 4:
                for i in range(4 * (dt - 1), 4 * dt):
                    emit_zpiece(ctx, i)
            if dt == 1:
                yfin_v[0] = v_tt(slc(z_sb, 0, TC), PS[3][:, :],
                                 slc(z_sb, 0, TC), ALU.mult,
                                 after={"s": ctx["zev"][0],
                                        "t": psy_ch[(0, 7)]})
                kb.set_bank_cons(3, "v", yfin_v[0])
            if dt >= 1:
                yfin_v[dt] = v_tt(slc(z_sb, dt, TC), PS[pyb][:, :],
                                  slc(z_sb, dt, TC), ALU.mult,
                                  after={"s": ctx["zev"][dt],
                                         "t": psy_ch[(dt, 7)]})
                kb.set_bank_cons(pyb, "v", yfin_v[dt])
            for cb2fn in nxt_cb.get(dt, []):
                cb2fn()
        ctx["yfin_v"] = yfin_v
        if c == 0 and b == 0:
            dump("yfin15", z_sb[:, 15 * TC:16 * TC], TC, bf=True)

    def emit_outproj(ctx):
        c, b = ctx["c"], ctx["b"]
        cb_user_v = {}
        outp_t = {}
        swo = s_wout[b].rearrange("(j p) m -> p j m", p=128)
        for m in range(DMT):
            wdo = kb.dma(wb_out[m % 2][:, :].rearrange(
                             "p (j m) -> p j m", j=DINT),
                         swo[:, :, m * 128:(m + 1) * 128],
                         after={"t": outp_t.get(m - 2, P["outp_t"]),
                                "d2": cast_d["wout1" if b == 0 else "wout2"]})
            pa = 3 + (m % 2)
            aft = merge({"d": wdo}, {"v": ctx["yfin_v"][DINT - 1]},
                        kb.bank_prod_after(pa), kb.bank_cons_after(pa))
            tl = None
            for k in range(DINT):
                tl = t_mm(pa, PS[pa][:, :], wb_out[m % 2][:, k * 128:(k + 1) * 128],
                          slc(z_sb, k, TC), start=(k == 0), stop=(k == DINT - 1),
                          after=aft if k == 0 else None, inc=(k == DINT - 1))
            kb.set_bank_prod(pa, tl)
            outp_t[m] = tl
            sev = s_act(cbuf[m % 2][:, :], PS[pa][:, :], AF.Identity,
                        bias=coutb[b][:, m:m + 1],
                        after=merge({"t": tl},
                                    {"v": kb.cnt["v"]} if m < 2 else
                                    {"v": cb_user_v.get(m - 2)}))
            kb.set_bank_cons(pa, "s", sev)
            if b == 0:
                cb_user_v[m] = v_tt(slc(u_sb, m, TC), cbuf[m % 2][:, :],
                                    slc(h_sb, m, TC), ALU.add, after={"s": sev})
            else:
                v_tt(cbuf[m % 2][:, :], cbuf[m % 2][:, :], slc(h_sb, m, TC),
                     ALU.add, after={"s": sev})
                v_tt(cbuf[m % 2][:, :], cbuf[m % 2][:, :], slc(u_sb, m, TC),
                     ALU.mult)
                P["xmid_v"] = v_tt(slc(h_sb, m, TC), cbuf[m % 2][:, :],
                                   slc(x_ln, m, TC), ALU.add)
                cb_user_v[m] = P["xmid_v"]
        P["outp_t"] = outp_t[DMT - 1]
        P["inproj_t"] = wb_guard[0] if wb_guard[0] > wb_guard[1] else wb_guard[1]
        if c == 0 and b == 0:
            dump("u", u_sb[:, :], DMT * TC, bf=True)
        if b == 1:
            if c == 0:
                dump("xmid", h_sb[:, :], DMT * TC)
            P["xmid_dma"] = kb.dma(s_xmid[:, :, c * TC:(c + 1) * TC],
                                   h_sb[:, :].rearrange("p (j t) -> p j t", j=DMT),
                                   after={"v": P["xmid_v"]})

    units = [(c, b) for c in range(NCH) for b in (0, 1)]
    lnh = emit_ln(0)
    cur = make_stage1(0, 0, lnh)
    for dt in range(DINT):
        cur["pieceT"](dt)
        cur["piece"](dt)
    cur["tail"]()
    for idx, (c, b) in enumerate(units):
        nxt = units[idx + 1] if idx + 1 < len(units) else None
        nxt_cb = {}
        nctx = None
        nxt_cb_top = {}
        if nxt is not None and nxt[0] == c:
            nctx = make_stage1(nxt[0], nxt[1], lnh)
            for dt in range(DINT):
                nxt_cb_top[dt] = [(lambda d=dt, nc2=nctx: nc2["pieceT"](d))]
                nxt_cb[dt] = [(lambda d=dt, nc2=nctx: nc2["piece"](d))]
            nxt_cb[DINT - 1].append(lambda nc2=nctx: nc2["tail"]())
        if c == 0 and b == 0:
            dump("xc", xc_sb[:, :], DINT * TC, bf=True)
        emit_scan(cur, nxt_cb, nxt_cb_top)
        if c == 0 and b == 0:
            dump("Ball", Ball[:, :], NST * TC, bf=True)
            dump("Call", Call[:, :], NST * TC, bf=True)
        emit_outproj(cur)
        if nxt is not None and nxt[0] != c:
            lnh = emit_ln(nxt[0])
            nctx = make_stage1(nxt[0], nxt[1], lnh)
            for dt in range(DINT):
                nctx["pieceT"](dt)
                nctx["piece"](dt)
            nctx["tail"]()
        if nctx is not None:
            cur = nctx

    ph1 = {k: kb.cnt[k] for k in ("s", "v", "t", "g", "d")}
    es1.close()

    es2 = ExitStack()
    sb2 = lambda name, shape, dt: es2.enter_context(nc.sbuf_tensor(name, shape, dt))
    x0_sb = sb2("x0_sb", [128, DMT * TC], F32)
    xm2 = [sb2(f"xm_sb{i}", [128, DMT * TC], F32) for i in range(3)]
    h2b = [sb2(f"h2_sb{i}", [128, DMT * TC], BF16) for i in range(2)]
    h3_sb = sb2("h3_sb", [128, DMT * TC], BF16)
    g_sb = sb2("g_sb", [128, DFFT * TC], BF16)
    hu_sb = sb2("hu_sb", [128, HIDT * TC], BF16)
    st20 = sb2("st20", [1, TC], F32)
    bc2A = sb2("bc2A", [128, TC], F32)
    bc2B = sb2("bc2B", [128, TC], F32)
    cb2 = [sb2(f"cb2_{i}", [128, TC], F32) for i in range(2)]
    sgb = sb2("sgb", [128, TC], BF16)
    wbF = [sb2(f"wbF{i}", [128, DMT * 128], BF16) for i in range(2)]
    wbA = [sb2(f"wbA{i}", [128, DMT * 128], BF16) for i in range(2)]
    wbB = [sb2(f"wbB{i}", [128, DFFT * 128], BF16) for i in range(2)]
    wbC = [sb2(f"wbC{i}", [128, DMT * 128], BF16) for i in range(2)]

    bar2 = dict(ph1)

    S2 = {"statsv": 0, "outv": {}, "x0d": {}, "dmA": {},
          "ffn1_last": {}, "ffn2_last": {}, "dn_last": {},
          "tu_last": {}, "hu_last": {},
          "ffn1_slot": {}, "wg_slot": {}, "wu_slot": {},
          "h2_done": {}, "gelu_last": {}, "h3_done": {}, "dn_t": {}}

    def stats_ln2(src_tile, pa, pb, ln_affine, out_tile, src_g, out_t):
        t1 = None
        for j in range(DMT):
            aft = merge(kb.bank_prod_after(pa) if j == 0 else None,
                        src_g if j == 0 else None, {"g": g_done})
            t1 = t_mm(pa, PS[pa][0:1, :], onescol[:, 0:1], slc(src_tile, j, TC),
                      start=(j == 0), stop=(j == DMT - 1), after=aft,
                      inc=(j == DMT - 1))
        kb.set_bank_prod(pa, t1)
        sq = [None, None]
        t2 = None
        for j in range(DMT):
            a = merge({"t": sq[j % 2]} if sq[j % 2] else None,
                      merge(src_g, {"v": S2["statsv"]}) if j < 2 else None)
            ss = s_act(cb2[j % 2][:, :], slc(src_tile, j, TC), AF.Square, after=a)
            aft = merge({"s": ss}, kb.bank_prod_after(pb) if j == 0 else None)
            t2 = t_mm(pb, PS[pb][0:1, :], onescol[:, 0:1], cb2[j % 2][:, :],
                      start=(j == 0), stop=(j == DMT - 1), after=aft, inc=True)
            sq[j % 2] = t2
        kb.set_bank_prod(pb, t2)
        bmu = bc_sum(bc2A[:, :], PS[pa][0:1, :], st20, pa,
                     merge({"t": t1}, {"v": S2["statsv"]}))
        kb.set_bank_cons(pa, "s", bmu)
        bex = bc_sum(bc2B[:, :], PS[pb][0:1, :], st20, pb, {"t": t2, "s": bmu})
        kb.set_bank_cons(pb, "s", bex)
        vq = v_tt(cb2[0][:, :], bc2A[:, :], bc2A[:, :], ALU.mult,
                  after={"s": bmu, "t": t2})
        vvar = v_tt(bc2B[:, :], bc2B[:, :], cb2[0][:, :], ALU.subtract,
                    after={"s": bex})
        sA = s_act(bc2B[:, :], bc2B[:, :], AF.Ln, bias=cconst[:, 0:1],
                   after={"v": vvar})
        sA = s_act(bc2B[:, :], bc2B[:, :], AF.Exp, scale=cconst[:, 3:4],
                   after={"s": sA})
        outs = []
        vlast = None
        for j in range(DMT):
            vv = v_tt(cb2[j % 2][:, :], slc(src_tile, j, TC), bc2A[:, :],
                      ALU.subtract,
                      after=merge({"s": sA} if j == 0 else None,
                                  {"s": outs[j - 2]} if j >= 2 else None))
            vv = v_tt(cb2[j % 2][:, :], cb2[j % 2][:, :], bc2B[:, :], ALU.mult)
            vlast = vv
            og = merge({"v": vv}, out_t if j == 0 else None)
            if ln_affine:
                o = s_act(slc(out_tile, j, TC), cb2[j % 2][:, :], AF.Identity,
                          bias=cb_[:, j:j + 1], scale=cg[:, j:j + 1], after=og)
            else:
                o = s_act(slc(out_tile, j, TC), cb2[j % 2][:, :], AF.Identity,
                          after=og)
            outs.append(o)
        S2["statsv"] = vlast
        return outs

    swf1 = s_wf1.rearrange("(j p) m -> p j m", p=128)
    swf2 = s_wf2.rearrange("(j p) m -> p j m", p=128)
    swg = s_wg.rearrange("(j p) m -> p j m", p=128)
    swu = s_wu.rearrange("(j p) m -> p j m", p=128)
    swd = s_wd.rearrange("(j p) m -> p j m", p=128)

    x03 = x0_sb[:, :].rearrange("p (j t) -> p j t", j=DMT)
    xm3 = [xm2[i][:, :].rearrange("p (j t) -> p j t", j=DMT) for i in range(3)]

    def dmaA(c):
        aft = merge(bar2 if c == 0 else None,
                    {"v": S2["outv"].get(c - 3)} if c >= 3 else None)
        S2["dmA"][c] = kb.dma(xm3[c % 3], s_xmid[:, :, c * TC:(c + 1) * TC],
                              after=aft, eng="gpsimd")

    def statsA(c):
        outt = {"t": S2["ffn1_last"].get(c - 2)} if c >= 2 else None
        S2["h2_done"][c] = stats_ln2(xm2[c % 3], 6, 7, True, h2b[c % 2],
                                     {"d2": S2["dmA"][c]}, outt)

    def stageB(c):
        h2d = S2["h2_done"][c][-1]
        ge_prev = {"t": S2["ffn2_last"].get(c - 1)} if c >= 1 else None
        tl = sev = None
        for m in range(DFFT):
            slot = S2["ffn1_slot"].get((c, m - 2),
                                       S2["ffn1_slot"].get((c - 1, m - 2 + DFFT), 0))
            wd1 = kb.dma(wbF[m % 2][:, :].rearrange("p (j m) -> p j m", j=DMT),
                         swf1[:, :, m * 128:(m + 1) * 128],
                         after={"t": slot, "d2": cast_d["wf1"]})
            pa = m % 3
            aft = merge({"d": wd1}, {"s": h2d},
                        kb.bank_prod_after(pa), kb.bank_cons_after(pa))
            tl = None
            for k in range(DMT):
                tl = t_mm(pa, PS[pa][:, :], wbF[m % 2][:, k * 128:(k + 1) * 128],
                          slc(h2b[c % 2], k, TC), start=(k == 0),
                          stop=(k == DMT - 1),
                          after=aft if k == 0 else None, inc=(k == DMT - 1))
            kb.set_bank_prod(pa, tl)
            sev = s_act(slc(g_sb, m, TC), PS[pa][:, :], AF.Gelu,
                        bias=cfb1[:, m:m + 1],
                        after=merge({"t": tl}, ge_prev if m == 0 else None))
            kb.set_bank_cons(pa, "s", sev)
            S2["ffn1_slot"][(c, m)] = tl
        S2["ffn1_last"][c] = tl
        S2["gelu_last"][c] = sev

    def stageC(c):
        xmt = xm2[c % 3]
        gelu_l = S2["gelu_last"][c]
        x1_done = {}
        tl = None
        for m in range(DMT):
            slot = x1_done.get(("t", m - 2),
                               S2["dn_t"].get((c - 1, 6 + m), 0) if m < 2 else 0)
            wd2 = kb.dma(wbB[m % 2][:, :].rearrange("p (j m) -> p j m", j=DFFT),
                         swf2[:, :, m * 128:(m + 1) * 128],
                         after={"t": slot, "d2": cast_d["wf2"]})
            pa = 3 + (m % 2)
            aft = merge({"d": wd2}, {"s": gelu_l},
                        kb.bank_prod_after(pa), kb.bank_cons_after(pa))
            tl = None
            for k in range(DFFT):
                tl = t_mm(pa, PS[pa][:, :], wbB[m % 2][:, k * 128:(k + 1) * 128],
                          slc(g_sb, k, TC), start=(k == 0), stop=(k == DFFT - 1),
                          after=aft if k == 0 else None, inc=(k == DFFT - 1))
            kb.set_bank_prod(pa, tl)
            sev = s_act(cb2[m % 2][:, :], PS[pa][:, :], AF.Identity,
                        bias=cfb2[:, m:m + 1],
                        after=merge({"t": tl},
                                    {"v": kb.cnt["v"]} if m < 2 else
                                    {"v": x1_done.get(m - 2)}))
            kb.set_bank_cons(pa, "s", sev)
            v_tt(slc(xmt, m, TC), slc(xmt, m, TC), cb2[m % 2][:, :], ALU.add,
                 after={"s": sev})
            vx1 = v_tt(slc(xmt, m, TC), slc(xmt, m, TC), slc(x0_sb, m, TC),
                       ALU.add, after={"d": S2["x0d"][c]} if m == 0 else None)
            x1_done[("t", m)] = tl
            x1_done[m] = vx1
        S2["ffn2_last"][c] = tl
        outt = {"t": S2["tu_last"].get(c - 1)} if c >= 1 else None
        S2["h3_done"][c] = stats_ln2(xmt, 6, 7, False, h3_sb,
                                     {"v": x1_done[DMT - 1]}, outt)

    def stageD(c):
        h3d = S2["h3_done"][c][-1]
        vhu = None
        tg = tu = None
        for m in range(HIDT):
            gslot = S2["wg_slot"].get((c, m - 2),
                                      S2["wg_slot"].get((c - 1, m - 2 + HIDT), 0))
            uslot = S2["wu_slot"].get((c, m - 2),
                                      S2["wu_slot"].get((c - 1, m - 2 + HIDT), 0))
            wg_d = kb.dma(wbA[m % 2][:, :].rearrange("p (j m) -> p j m", j=DMT),
                          swg[:, :, m * 128:(m + 1) * 128],
                          after={"t": gslot, "d2": cast_d["wg"]})
            wu_d = kb.dma(wbC[m % 2][:, :].rearrange("p (j m) -> p j m", j=DMT),
                          swu[:, :, m * 128:(m + 1) * 128],
                          after={"t": uslot, "d2": cast_d["wu"]})
            pg = (m % 2) * 2
            pu = 1 + (m % 2) * 4
            aftg = merge({"d": wg_d}, {"s": h3d},
                         kb.bank_prod_after(pg), kb.bank_cons_after(pg))
            aftu = merge({"d": wu_d}, {"s": h3d},
                         kb.bank_prod_after(pu), kb.bank_cons_after(pu))
            for k in range(DMT):
                tg = t_mm(pg, PS[pg][:, :], wbA[m % 2][:, k * 128:(k + 1) * 128],
                          slc(h3_sb, k, TC), start=(k == 0), stop=(k == DMT - 1),
                          after=aftg if k == 0 else None, inc=(k == DMT - 1))
            for k in range(DMT):
                tu = t_mm(pu, PS[pu][:, :], wbC[m % 2][:, k * 128:(k + 1) * 128],
                          slc(h3_sb, k, TC), start=(k == 0), stop=(k == DMT - 1),
                          after=aftu if k == 0 else None, inc=(k == DMT - 1))
            kb.set_bank_prod(pg, tg)
            kb.set_bank_prod(pu, tu)
            ssg = s_act(sgb[:, :], PS[pg][:, :], AF.Silu,
                        scale=cconst[:, 5:6],
                        after=merge({"t": tg},
                                    {"v": vhu} if vhu else
                                    {"v": S2["hu_last"].get(c - 1)},
                                    {"g": g_done}))
            kb.set_bank_cons(pg, "s", ssg)
            vhu = v_stt(slc(hu_sb, m, TC), PS[pu][:, :], R127, sgb[:, :],
                        ALU.mult, ALU.mult,
                        after=merge({"s": ssg, "t": tu},
                                    {"t": S2["dn_last"].get(c - 1)}
                                    if m == 0 else None))
            kb.set_bank_cons(pu, "v", vhu)
            S2["wg_slot"][(c, m)] = tg
            S2["wu_slot"][(c, m)] = tu
        S2["tu_last"][c] = tu
        S2["hu_last"][c] = vhu

    def stageE(c):
        xmt = xm2[c % 3]
        hu_l = S2["hu_last"][c]
        out_v = {}
        tl = None
        for m in range(DMT):
            slot = S2["dn_t"].get((c, m - 2), S2["ffn2_last"][c])
            wd3 = kb.dma(wbB[m % 2][:, 0:HIDT * 128].rearrange("p (j m) -> p j m", j=HIDT),
                         swd[:, :, m * 128:(m + 1) * 128],
                         after={"t": slot, "d2": cast_d["wd"]})
            pa = 3 + (m % 2)
            aft = merge({"d": wd3}, {"v": hu_l},
                        kb.bank_prod_after(pa), kb.bank_cons_after(pa))
            tl = None
            for k in range(HIDT):
                tl = t_mm(pa, PS[pa][:, :], wbB[m % 2][:, k * 128:(k + 1) * 128],
                          slc(hu_sb, k, TC), start=(k == 0), stop=(k == HIDT - 1),
                          after=aft if k == 0 else None, inc=(k == HIDT - 1))
            kb.set_bank_prod(pa, tl)
            S2["dn_t"][(c, m)] = tl
            sev = s_act(cb2[m % 2][:, :], PS[pa][:, :], AF.Identity,
                        scale=cconst[:, 5:6],
                        after=merge({"t": tl},
                                    {"v": kb.cnt["v"]} if m < 2 else
                                    {"v": out_v.get(m - 2)}, {"g": g_done}))
            kb.set_bank_cons(pa, "s", sev)
            out_v[m] = v_tt(slc(x0_sb, m, TC), slc(xmt, m, TC), cb2[m % 2][:, :],
                            ALU.add, after={"s": sev})
        S2["outv"][c] = out_v[DMT - 1]
        S2["dn_last"][c] = tl
        kb.dma(out_h[:, :, c * TC:(c + 1) * TC], x03, after={"v": S2["outv"][c]})
        if c + 1 < NCH:
            S2["x0d"][c + 1] = kb.dma(x03, x_h[:, :, (c + 1) * TC:(c + 2) * TC])

    S2["x0d"][0] = kb.dma(x03, x_h[:, :, 0:TC], after=bar2)
    dmaA(0)
    statsA(0)
    dmaA(1)
    stageB(0)
    for c in range(NCH):
        if c + 1 < NCH:
            statsA(c + 1)
        stageC(c)
        if c + 1 < NCH:
            stageB(c + 1)
        stageD(c)
        stageE(c)
        if c + 2 < NCH:
            dmaA(c + 2)

    es2.close()

    final_d = kb.cnt["d"]
    kb.prog["sync"].append(lambda e, v=final_d: e.wait_ge(kb.sem["d"], v))

    with nc.Block() as blk:
        blk.sync(lambda e: [f(e) for f in kb.prog["sync"]])
        blk.scalar(lambda e: [f(e) for f in kb.prog["scalar"]])
        blk.vector(lambda e: [f(e) for f in kb.prog["vector"]])
        blk.tensor(lambda e: [f(e) for f in kb.prog["tensor"]])
        blk.gpsimd(lambda e: [f(e) for f in kb.prog["gpsimd"]])

    es_top.close()
    return nc


def _vec128(v, k):
    v = np.ascontiguousarray(np.asarray(v, np.float32))
    return np.ascontiguousarray(v.reshape(k, 128).T)


def kernel(**inputs):
    ii = {k: np.asarray(v) for k, v in inputs.items()}
    x = np.ascontiguousarray(ii["L0_attn_x"], dtype=np.float32)

    nc = build_nc()

    common = {}
    for b, pre in ((1, "L0_attn_m1_"), (2, "L0_attn_m2_")):
        common[f"win{b}"] = np.ascontiguousarray(ii[pre + "in_w"], np.float32)
        common[f"wout{b}"] = np.ascontiguousarray(ii[pre + "out_w"], np.float32)
        common[f"wxp{b}"] = np.ascontiguousarray(
            ii[pre + "xproj_w"].astype(np.float32).reshape(16, 128, 96).transpose(1, 0, 2))
        common[f"wdt{b}"] = np.ascontiguousarray(ii[pre + "dt_w"], np.float32)
        common[f"v_rms{b}"] = _vec128(ii[pre + "rms_w"], 8)
        common[f"v_inb{b}"] = _vec128(ii[pre + "in_b"], 32)
        common[f"v_cw{b}"] = np.ascontiguousarray(
            ii[pre + "conv_w"].astype(np.float32).reshape(16, 128, 4)
            .transpose(1, 0, 2).reshape(128, 64))
        common[f"v_cb{b}"] = _vec128(ii[pre + "conv_b"], 16)
        common[f"v_dtb{b}"] = _vec128(ii[pre + "dt_b"], 16)
        common[f"v_alog{b}"] = np.ascontiguousarray(
            ii[pre + "A_log"].astype(np.float32).reshape(16, 128, 16)
            .transpose(1, 0, 2).reshape(128, 256))
        common[f"v_D{b}"] = _vec128(ii[pre + "D"], 16)
        common[f"v_outb{b}"] = _vec128(ii[pre + "out_b"], 8)
    common["wf1"] = np.ascontiguousarray(ii["L0_attn_ffn_w1"], np.float32)
    common["wf2"] = np.ascontiguousarray(ii["L0_attn_ffn_w2"], np.float32)
    common["v_fb1"] = _vec128(ii["L0_attn_ffn_b1"], 32)
    common["v_fb2"] = _vec128(ii["L0_attn_ffn_b2"], 8)
    common["v_g"] = _vec128(ii["L0_attn_ln_gamma"], 8)
    common["v_b"] = _vec128(ii["L0_attn_ln_beta"], 8)
    import ml_dtypes
    sel = np.zeros((96, 32), dtype=ml_dtypes.bfloat16)
    for n in range(16):
        sel[64 + n, n] = 1
        sel[80 + n, 16 + n] = 1
    common["selbc_h"] = sel
    common["ident_h"] = np.eye(128, dtype=ml_dtypes.bfloat16)
    common["wg"] = np.ascontiguousarray(ii["L0_ffn_w_gate"].T.astype(np.int32))
    common["wu"] = np.ascontiguousarray(ii["L0_ffn_w_up"].T.astype(np.int32))
    common["wd"] = np.ascontiguousarray(ii["L0_ffn_w_down"].T.astype(np.int32))

    in_maps = []
    for cidx in range(8):
        m = dict(common)
        xc = np.ascontiguousarray(
            x[cidx].T.reshape(DMT, 128, T).transpose(1, 0, 2))
        m["x"] = xc
        in_maps.append(m)

    res = run_bass_kernel_spmd(nc, in_maps, core_ids=list(range(8)))
    outs = []
    for cidx in range(8):
        o = res.results[cidx]["out"]
        outs.append(np.ascontiguousarray(
            o.transpose(1, 0, 2).reshape(DM, T).T))
    return np.stack(outs, axis=0).astype(np.float32)


# version 23
# speedup vs baseline: 1.1652x; 1.0655x over previous
import numpy as np
from contextlib import ExitStack
import concourse.bass as bass
import concourse.mybir as mybir
from concourse.bass_utils import run_bass_kernel_spmd

F32 = mybir.dt.float32
BF16 = mybir.dt.bfloat16
I32 = mybir.dt.int32
AF = mybir.ActivationFunctionType
ALU = mybir.AluOpType

T = 2048
TC = 512
NCH = T // TC
DM = 1024
DMT = DM // 128
DIN = 2048
DINT = DIN // 128
NST = 16
DFF = 4096
DFFT = DFF // 128
HID = 3072
HIDT = HID // 128
EPS = 1e-5
R127 = np.float32(1.0) / np.float32(127.0)
TC2P = 2 * TC + 1

ENGS = ("sync", "scalar", "vector", "tensor", "gpsimd")


class KB:

    def __init__(self, nc, es):
        self.nc = nc
        self.prog = {e: [] for e in ENGS}
        self.sem = {}
        self.cnt = {}
        self.mark = {e: {} for e in ENGS}
        for s in ("d", "d2", "d3", "s", "v", "t", "g"):
            self.sem[s] = es.enter_context(nc.semaphore("sem_" + s))
            self.cnt[s] = 0
        self.bank = {}

    def wait(self, eng, sname, val):
        if val is None or val <= 0:
            return
        if self.mark[eng].get(sname, 0) >= val:
            return
        self.mark[eng][sname] = val
        sem = self.sem[sname]
        self.prog[eng].append(lambda e, sem=sem, val=val: e.wait_ge(sem, val))

    def waits(self, eng, d):
        if d:
            for k, v in d.items():
                self.wait(eng, k, v)

    def run(self, eng, fn, inc=None, n=1, after=None):
        self.waits(eng, after)
        tgt = None
        if inc:
            self.cnt[inc] += n
            tgt = self.cnt[inc]
            sem = self.sem[inc]
            self.prog[eng].append(lambda e, fn=fn, sem=sem, n=n: fn(e).then_inc(sem, n))
        else:
            self.prog[eng].append(lambda e, fn=fn: fn(e))
        return tgt

    def dma(self, out, in_, after=None, eng="sync"):
        sem = {"gpsimd": "d2", "scalar": "d3"}.get(eng, "d")
        return self.run(eng, lambda e, o=out, i=in_: e.dma_start(out=o, in_=i),
                        inc=sem, n=16, after=after)

    def bank_prod_after(self, name):
        st = self.bank.get(name)
        if st and st.get("cons"):
            return {st["cons"][0]: st["cons"][1]}
        return {}

    def set_bank_prod(self, name, val):
        self.bank[name] = {"prod": ("t", val), "cons": None}

    def bank_cons_after(self, name):
        st = self.bank.get(name)
        if st and st.get("prod"):
            return {st["prod"][0]: st["prod"][1]}
        return {}

    def set_bank_cons(self, name, sem, val):
        st = self.bank.setdefault(name, {"prod": None, "cons": None})
        st["cons"] = (sem, val)


def merge(*ds):
    out = {}
    for d in ds:
        if d:
            for k, v in d.items():
                if v is None:
                    continue
                if k not in out or out[k] < v:
                    out[k] = v
    return out


def bcol(tile, i):
    col = tile[:, i:i + 1]
    return bass.AP(col.tensor, col.offset, [list(col.ap[0]), [0, 128]])


def build_nc(debug=False):
    nc = bass.Bass("TRN2", target_bir_lowering=False)

    x_h = nc.dram_tensor("x", [128, DMT, T], F32, kind="ExternalInput").ap()
    out_h = nc.dram_tensor("out", [128, DMT, T], F32, kind="ExternalOutput").ap()

    w_in = [nc.dram_tensor(f"win{b}", [DM, 2 * DIN], F32, kind="ExternalInput").ap() for b in (1, 2)]
    w_out = [nc.dram_tensor(f"wout{b}", [DIN, DM], F32, kind="ExternalInput").ap() for b in (1, 2)]
    w_xp = [nc.dram_tensor(f"wxp{b}", [128, DINT, 96], F32, kind="ExternalInput").ap() for b in (1, 2)]
    w_dt = [nc.dram_tensor(f"wdt{b}", [64, DIN], F32, kind="ExternalInput").ap() for b in (1, 2)]
    w_f1 = nc.dram_tensor("wf1", [DM, DFF], F32, kind="ExternalInput").ap()
    w_f2 = nc.dram_tensor("wf2", [DFF, DM], F32, kind="ExternalInput").ap()
    w_g = nc.dram_tensor("wg", [DM, HID], I32, kind="ExternalInput").ap()
    w_u = nc.dram_tensor("wu", [DM, HID], I32, kind="ExternalInput").ap()
    w_d = nc.dram_tensor("wd", [HID, DM], I32, kind="ExternalInput").ap()

    v_g = nc.dram_tensor("v_g", [128, DMT], F32, kind="ExternalInput").ap()
    v_b = nc.dram_tensor("v_b", [128, DMT], F32, kind="ExternalInput").ap()
    v_rms = [nc.dram_tensor(f"v_rms{b}", [128, DMT], F32, kind="ExternalInput").ap() for b in (1, 2)]
    v_inb = [nc.dram_tensor(f"v_inb{b}", [128, 32], F32, kind="ExternalInput").ap() for b in (1, 2)]
    v_cw = [nc.dram_tensor(f"v_cw{b}", [128, 64], F32, kind="ExternalInput").ap() for b in (1, 2)]
    v_cb = [nc.dram_tensor(f"v_cb{b}", [128, DINT], F32, kind="ExternalInput").ap() for b in (1, 2)]
    v_dtb = [nc.dram_tensor(f"v_dtb{b}", [128, DINT], F32, kind="ExternalInput").ap() for b in (1, 2)]
    v_alog = [nc.dram_tensor(f"v_alog{b}", [128, DINT * NST], F32, kind="ExternalInput").ap() for b in (1, 2)]
    v_D = [nc.dram_tensor(f"v_D{b}", [128, DINT], F32, kind="ExternalInput").ap() for b in (1, 2)]
    v_outb = [nc.dram_tensor(f"v_outb{b}", [128, DMT], F32, kind="ExternalInput").ap() for b in (1, 2)]
    v_fb1 = nc.dram_tensor("v_fb1", [128, DFFT], F32, kind="ExternalInput").ap()
    v_fb2 = nc.dram_tensor("v_fb2", [128, DMT], F32, kind="ExternalInput").ap()
    selbc_h = nc.dram_tensor("selbc_h", [96, 32], BF16, kind="ExternalInput").ap()
    ident_h = nc.dram_tensor("ident_h", [128, 128], BF16, kind="ExternalInput").ap()

    s_win = [nc.dram_tensor(f"s_win{b}", [DM, 2 * DIN], BF16, kind="Internal").ap() for b in (1, 2)]
    s_wout = [nc.dram_tensor(f"s_wout{b}", [DIN, DM], BF16, kind="Internal").ap() for b in (1, 2)]
    s_wf1 = nc.dram_tensor("s_wf1", [DM, DFF], BF16, kind="Internal").ap()
    s_wf2 = nc.dram_tensor("s_wf2", [DFF, DM], BF16, kind="Internal").ap()
    s_wgu = nc.dram_tensor("s_wgu", [DM, 2 * HID], BF16, kind="Internal").ap()
    s_wd = nc.dram_tensor("s_wd", [HID, DM], BF16, kind="Internal").ap()
    s_xmid = nc.dram_tensor("s_xmid", [128, DMT, T], F32, kind="Internal").ap()

    es_top = ExitStack()
    kb = KB(nc, es_top)

    dbg_names = []
    dbg_h = nc.dram_tensor("dbg", [128, 40 * 4096], F32, kind="ExternalOutput").ap() if debug else None
    dbgb_h = nc.dram_tensor("dbgb", [128, 40 * 4096], BF16, kind="ExternalOutput").ap() if debug else None
    dbg_off = [0, 0]

    def dump(name, ap, count, bf=False, parts=128):
        if not debug:
            return
        k = 1 if bf else 0
        dst = (dbgb_h if bf else dbg_h)[0:parts, dbg_off[k]:dbg_off[k] + count]
        aftr = {kk: kb.cnt[kk] for kk in ("s", "v", "t", "g")}
        tgt = kb.dma(dst, ap, after=aftr)
        dbg_names.append((name, k, dbg_off[k], count))
        dbg_off[k] += count
        for e in ("scalar", "vector", "tensor", "gpsimd", "sync"):
            kb.wait(e, "d", tgt)
    nc._dbg_names = dbg_names

    sb = lambda name, shape, dt: es_top.enter_context(nc.sbuf_tensor(name, shape, dt))
    onescol = sb("onescol", [128, 1], F32)
    ones_k1f = sb("ones_k1f", [1, 128], F32)
    cconst = sb("cconst", [128, 8], F32)
    selbc = sb("selbc", [96, 32], BF16)
    identb = sb("identb", [128, 128], BF16)
    cg = sb("cg", [128, DMT], F32)
    cb_ = sb("cb_", [128, DMT], F32)
    cinb = [sb(f"cinb{b}", [128, 32], F32) for b in (0, 1)]
    ccw = [sb(f"ccw{b}", [128, 64], F32) for b in (0, 1)]
    ccb = [sb(f"ccb{b}", [128, DINT], F32) for b in (0, 1)]
    cdtb = [sb(f"cdtb{b}", [128, DINT], F32) for b in (0, 1)]
    cnegcb = [sb(f"cnegcb{b}", [128, DINT], F32) for b in (0, 1)]
    cexpA = [sb(f"cexpA{b}", [128, DINT * NST], F32) for b in (0, 1)]
    cD = [sb(f"cD{b}", [128, DINT], F32) for b in (0, 1)]
    coutb = [sb(f"coutb{b}", [128, DMT], F32) for b in (0, 1)]
    cfb1 = sb("cfb1", [128, DFFT], F32)
    cfb2 = sb("cfb2", [128, DMT], F32)
    crms = [sb(f"crms{b}", [128, DMT], F32) for b in (0, 1)]
    xpw = [sb(f"xpw{b}", [128, DINT * 96], BF16) for b in (0, 1)]
    dtw = [sb(f"dtw{b}", [64, DIN], BF16) for b in (0, 1)]
    tails = [sb(f"tails{b}", [128, DINT * 3], F32) for b in (0, 1)]
    state = [sb(f"state{b}", [128, DINT * NST], F32) for b in (0, 1)]

    PS = [es_top.enter_context(nc.psum_tensor(f"ps{i}", [128, 512], F32)) for i in range(8)]

    es_pre = ExitStack()
    wtf = [es_pre.enter_context(nc.sbuf_tensor(f"wtf{i}", [128, 4096], F32))
           for i in range(2)]
    wto = [es_pre.enter_context(nc.sbuf_tensor(f"wto{i}", [128, 4096], BF16))
           for i in range(2)]
    cast_guard = [0, 0]

    def gmemset(ap, val):
        return kb.run("gpsimd", lambda e, a=ap, v=val: e.memset(a, v), inc="g")

    gmemset(onescol[:, :], 1.0)
    gmemset(ones_k1f[:, :], 1.0)
    for j, val in enumerate([EPS, EPS * EPS, 1.0 + EPS, -0.5, -1.0, float(R127),
                             1.0 / DM, 1.0]):
        gmemset(cconst[:, j:j + 1], val)
    for b in (0, 1):
        gmemset(tails[b][:, :], 0.0)
        gmemset(state[b][:, :], 0.0)
    g_done = kb.cnt["g"]

    for dst, src in [(cg, v_g), (cb_, v_b), (cfb1, v_fb1), (cfb2, v_fb2),
                     (selbc, selbc_h), (identb, ident_h)]:
        kb.dma(dst[:, :], src)
    for b in (0, 1):
        for dst, src in [(cinb[b], v_inb[b]), (ccw[b], v_cw[b]), (ccb[b], v_cb[b]),
                         (cD[b], v_D[b]), (coutb[b], v_outb[b]), (crms[b], v_rms[b])]:
            kb.dma(dst[:, :], src)

    for b in (0, 1):
        d1 = kb.dma(wtf[0][:, 0:DINT * NST], v_alog[b])
        se_ = kb.run("scalar", lambda e, o=cexpA[b][:, :], i=wtf[0][:, 0:DINT * NST]:
                     e.activation(out=o, in_=i, func=AF.Exp), inc="s", after={"d": d1})
        kb.run("vector", lambda e, o=cexpA[b][:, :], i=cexpA[b][:, :]:
               e.tensor_scalar(out=o, in0=i, scalar1=-1.0, scalar2=None, op0=ALU.mult),
               inc="v", after={"s": se_})
        kb.dma(cdtb[b][:, :], v_dtb[b])
        dcb = kb.dma(wtf[0][:, DINT * NST:DINT * NST + DINT], v_cb[b],
                     after={"s": se_})
        kb.run("vector", lambda e, o=cnegcb[b][:, :], i=wtf[0][:, DINT * NST:DINT * NST + DINT]:
               e.tensor_scalar(out=o, in0=i, scalar1=-1.0, scalar2=None, op0=ALU.mult),
               inc="v", after={"d": dcb})
    cnegA = cexpA
    v_after_pre = kb.cnt["v"]

    for b in (0, 1):
        d1 = kb.dma(wtf[1][:, 0:DINT * 96], w_xp[b].rearrange("p j m -> p (j m)"),
                    after={"v": v_after_pre, "s": cast_guard[1]})
        cg1 = kb.run("scalar", lambda e, o=xpw[b][:, :], i=wtf[1][:, 0:DINT * 96]:
                     e.activation(out=o, in_=i, func=AF.Identity), inc="s", after={"d": d1})
        cast_guard[1] = cg1
        d2 = kb.dma(wtf[1][0:64, 2048:4096], w_dt[b], after={"s": cast_guard[1]})
        cg1 = kb.run("scalar", lambda e, o=dtw[b][:, :], i=wtf[1][0:64, 2048:4096]:
                     e.activation(out=o, in_=i, func=AF.Identity), inc="s", after={"d": d2})
        cast_guard[1] = cg1

    def cast_weight_scaled(src, dst, rows, cols, scale_tile, j0=0):
        for j in range(rows // 128):
            jj = (j0 + j) % 2
            d1 = kb.dma(wtf[jj][:, 0:cols], src[j * 128:(j + 1) * 128, :],
                        after={"s": cast_guard[jj]})
            sc = scale_tile[:, j:j + 1]
            cst = kb.run("scalar", lambda e, o=wto[jj][:, 0:cols],
                         i=wtf[jj][:, 0:cols], s=sc:
                         e.activation(out=o, in_=i, func=AF.Identity, scale=s),
                         inc="s", after={"d": d1})
            cast_guard[jj] = cst
            kb.dma(dst[j * 128:(j + 1) * 128, :], wto[jj][:, 0:cols],
                   after={"s": cst})

    cast_weight_scaled(w_in[0], s_win[0], DM, 2 * DIN, crms[0])
    cast_weight_scaled(w_in[1], s_win[1], DM, 2 * DIN, crms[1], j0=0)
    cast_d = {}
    for src, dst, key in [(w_out[0], s_wout[0], "wout1"), (w_out[1], s_wout[1], "wout2"),
                          (w_f1, s_wf1, "wf1"), (w_f2, s_wf2, "wf2"),
                          (w_d, s_wd, "wd")]:
        cast_d[key] = kb.dma(dst, src, eng="gpsimd")
    wgu_v = s_wgu.rearrange("r (m t c) -> r m t c", t=2, c=128)
    wg_v = w_g.rearrange("r (m c) -> r m c", c=128)
    wu_v = w_u.rearrange("r (m c) -> r m c", c=128)
    hh = HIDT // 2
    kb.dma(wgu_v[:, 0:hh, 0, :], wg_v[:, 0:hh, :], eng="gpsimd")
    cast_d["wg"] = kb.dma(wgu_v[:, hh:HIDT, 0, :], wg_v[:, hh:HIDT, :], eng="gpsimd")
    kb.dma(wgu_v[:, 0:hh, 1, :], wu_v[:, 0:hh, :], eng="gpsimd")
    cast_d["wu"] = kb.dma(wgu_v[:, hh:HIDT, 1, :], wu_v[:, hh:HIDT, :], eng="gpsimd")

    pre_s, pre_v, pre_g, pre_d = kb.cnt["s"], kb.cnt["v"], kb.cnt["g"], kb.cnt["d"]
    es_pre.close()

    es1 = ExitStack()
    sb1 = lambda name, shape, dt: es1.enter_context(nc.sbuf_tensor(name, shape, dt))
    x_ln = sb1("x_ln", [128, DMT * TC], F32)
    h_sb = sb1("h_sb", [128, DMT * TC], F32)
    hr_sb = sb1("hr_sb", [128, DMT * TC], BF16)
    st0 = sb1("st0", [1, TC], F32)
    bcA = sb1("bcA", [128, TC], F32)
    bcB = sb1("bcB", [128, TC], F32)
    cbuf = [sb1(f"cbuf{i}", [128, TC], F32) for i in range(2)]
    sgbuf = [sb1(f"sgbuf{i}", [128, TC], F32) for i in range(2)]
    ebuf2 = sb1("ebuf2", [128, TC], F32)
    xcp = sb1("xcp", [128, DINT * (TC + 3)], BF16)
    dl3 = [sb1(f"dl3_{i}", [128, TC], BF16) for i in range(3)]
    z_sb = sb1("z_sb", [128, DINT * TC], BF16)
    xc_sb = sb1("xc_sb", [128, DINT * TC], BF16)
    Ball = sb1("Ball", [128, NST * TC], BF16)
    Call = sb1("Call", [128, (NST // 2) * TC2P], BF16)
    u_sb = sb1("u_sb", [128, DMT * TC], BF16)
    dbc = sb1("dbc", [96, TC], BF16)
    wb_in = [sb1(f"wb_in{i}", [128, DMT * 256], BF16) for i in range(2)]
    wb_out = [sb1(f"wb_out{i}", [128, DINT * 128], BF16) for i in range(2)]
    dxc0 = sb1("dxc0", [128, TC], BF16)
    dAb = [sb1(f"dAb{i}", [128, TC2P], BF16) for i in range(3)]
    dBu4b = [sb1(f"dBu4_{i}", [128, TC2P], BF16) for i in range(2)]
    hbb = [sb1(f"hbb{i}", [128, TC2P], BF16) for i in range(2)]
    chb = [sb1(f"chb{i}", [128, TC2P], BF16) for i in range(2)]
    xcd = [sb1(f"xcd{i}", [128, TC], BF16) for i in range(2)]
    for _ab in dAb:
        gmemset(_ab[:, TC:TC + 1], 0.0)
    g_done = kb.cnt["g"]

    def slc(tile, idx, width, off=0):
        return tile[:, idx * width + off: idx * width + off + width]

    def xcp_sl(dt, a, b):
        return xcp[:, dt * (TC + 3) + a: dt * (TC + 3) + b]

    def s_act(out, in_, func, bias=None, scale=None, after=None):
        kw = {}
        if bias is not None:
            kw["bias"] = bias
        if scale is not None:
            kw["scale"] = scale
        return kb.run("scalar", lambda e, o=out, i=in_, f=func, kw=kw:
                      e.activation(out=o, in_=i, func=f, **kw), inc="s", after=after)

    def v_tt(out, a, b, op, after=None):
        return kb.run("vector", lambda e, o=out, x=a, y=b, p=op:
                      e.tensor_tensor(out=o, in0=x, in1=y, op=p), inc="v", after=after)

    def v_ts(out, a, s1, op0, s2=None, op1=None, after=None):
        if isinstance(s1, np.floating):
            s1 = float(s1)
        if isinstance(s2, np.floating):
            s2 = float(s2)
        def f(e, o=out, x=a, s1=s1, s2=s2, op0=op0, op1=op1):
            if op1 is None:
                return e.tensor_scalar(out=o, in0=x, scalar1=s1, scalar2=None, op0=op0)
            return e.tensor_scalar(out=o, in0=x, scalar1=s1, scalar2=s2, op0=op0, op1=op1)
        return kb.run("vector", f, inc="v", after=after)

    def v_stt(out, in0, scalar, in1, op0, op1, after=None):
        if isinstance(scalar, np.floating):
            scalar = float(scalar)
        return kb.run("vector", lambda e, o=out, a=in0, s=scalar, b=in1, p0=op0, p1=op1:
                      e.scalar_tensor_tensor(out=o, in0=a, scalar=s, in1=b, op0=p0, op1=p1),
                      inc="v", after=after)

    def t_mm(psname, out, lhsT, rhs, start, stop, after=None, inc=False):
        return kb.run("tensor", lambda e, o=out, l=lhsT, r=rhs, st=start, sp=stop:
                      e.matmul(o, l, r, start=st, stop=sp),
                      inc=("t" if inc else None), after=after)

    def stats_pass(tiles_fn, n, pa, after_each=None):
        tgt = None
        for j in range(n):
            rhs, aft = tiles_fn(j)
            a = merge(kb.bank_prod_after(pa) if j == 0 else None, aft, after_each)
            tgt = t_mm(pa, PS[pa][0:1, :], onescol[:, 0:1], rhs,
                       start=(j == 0), stop=(j == n - 1), after=a,
                       inc=(j == n - 1))
        kb.set_bank_prod(pa, tgt)
        return tgt

    st_guard = {"t": 0}

    def bc_sum(dst, src_ps, stage, pa, after, scale_ci=6):
        s1 = s_act(stage[:, :], src_ps, AF.Identity,
                   after=merge(after, {"g": g_done}, {"t": st_guard["t"]}))
        a = merge(kb.bank_prod_after(pa), {"s": s1})
        tm = t_mm(pa, PS[pa][:, :], ones_k1f[:, :], stage[:, :],
                  start=True, stop=True, after=a, inc=True)
        kb.set_bank_prod(pa, tm)
        st_guard["t"] = tm
        sc = s_act(dst, PS[pa][:, :], AF.Identity,
                   scale=(cconst[:, scale_ci:scale_ci + 1]
                          if scale_ci is not None else None),
                   after={"t": tm})
        kb.set_bank_cons(pa, "s", sc)
        return sc

    x3 = x_ln[:, :].rearrange("p (j t) -> p j t", j=DMT)

    ph1_barrier = {"s": pre_s, "v": pre_v, "g": pre_g}

    P = {
        "xmid_v": 0, "xmid_dma": 0, "outp_t": 0, "inproj_t": 0,
        "last_scan": 0,
    }
    state_g = {}
    wb_guard = [0, 0]
    vdx_cur = {}

    def emit_ln(c):
        dx = kb.dma(x3, x_h[:, :, c * TC:(c + 1) * TC],
                    after=merge({"v": P["xmid_v"]}, ph1_barrier if c == 0 else None))
        def xt(j, dx=dx):
            return slc(x_ln, j, TC), {"d": dx, "g": g_done}
        t_s1 = stats_pass(xt, DMT, 6)
        sq_s = [None, None]
        tgt = None
        for j in range(DMT):
            a = merge({"d": dx}, {"t": sq_s[j % 2]} if sq_s[j % 2] else None)
            sq_s[j % 2] = s_act(cbuf[j % 2][:, :], slc(x_ln, j, TC), AF.Square, after=a)
            aft = merge({"s": sq_s[j % 2]}, kb.bank_prod_after(7) if j == 0 else None)
            tgt = t_mm(7, PS[7][0:1, :], onescol[:, 0:1], cbuf[j % 2][:, :],
                       start=(j == 0), stop=(j == DMT - 1), after=aft, inc=True)
            sq_s[j % 2] = tgt
        kb.set_bank_prod(7, tgt)
        t_s2 = tgt
        bmu = bc_sum(bcA[:, :], PS[6][0:1, :], st0, 4, {"t": t_s1})
        kb.set_bank_cons(6, "s", bmu)
        bex = bc_sum(bcB[:, :], PS[7][0:1, :], st0, 5, {"t": t_s2, "s": bmu})
        kb.set_bank_cons(7, "s", bex)
        vq = v_tt(cbuf[0][:, :], bcA[:, :], bcA[:, :], ALU.mult,
                  after={"s": bmu, "t": t_s2})
        vvar = v_tt(bcB[:, :], bcB[:, :], cbuf[0][:, :], ALU.subtract,
                    after={"s": bex})
        vcen = None
        for j in range(DMT):
            vcen = v_tt(slc(x_ln, j, TC), slc(x_ln, j, TC), bcA[:, :], ALU.subtract)
        sa1a = s_act(bcA[:, :], bcB[:, :], AF.Ln, bias=cconst[:, 0:1],
                     after={"v": vcen})
        sa1 = s_act(bcA[:, :], bcA[:, :], AF.Exp, scale=cconst[:, 3:4],
                    after={"s": sa1a})
        sa2a = s_act(bcB[:, :], bcB[:, :], AF.Ln, scale=cconst[:, 2:3],
                     bias=cconst[:, 1:2], after={"s": sa1})
        sa2 = s_act(bcB[:, :], bcB[:, :], AF.Exp, scale=cconst[:, 3:4],
                    after={"s": sa2a})
        h_done = []
        vv = None
        for j in range(DMT):
            vv = v_tt(cbuf[j % 2][:, :], slc(x_ln, j, TC), bcB[:, :], ALU.mult,
                      after=merge({"s": sa2} if j == 0 else None,
                                  {"s": h_done[j - 2]} if j >= 2 else None))
            hd = s_act(slc(h_sb, j, TC), cbuf[j % 2][:, :], AF.Identity,
                       bias=cb_[:, j:j + 1], scale=cg[:, j:j + 1],
                       after=merge({"v": vv}, {"d": P["xmid_dma"]} if j == 0 else None))
            h_done.append(hd)
        vlp = None
        for j in range(DMT):
            vlp = v_tt(slc(x_ln, j, TC), slc(x_ln, j, TC), bcA[:, :], ALU.mult,
                       after={"s": sa1} if j == 0 else None)
        if c == 0:
            dump("h", h_sb[:, :], DMT * TC)
            dump("LNp", x_ln[:, :], DMT * TC)
        sqh = [None, None]
        tgt = None
        for j in range(DMT):
            a = merge({"s": h_done[j]}, {"t": sqh[j % 2]} if sqh[j % 2] else None,
                      {"v": vv} if j == 0 else None,
                      kb.bank_prod_after(6) if j == 0 else None)
            ss = s_act(cbuf[j % 2][:, :], slc(h_sb, j, TC), AF.Square, after=a)
            aft = merge({"s": ss}, kb.bank_prod_after(6) if j == 0 else None)
            tgt = t_mm(6, PS[6][0:1, :], onescol[:, 0:1], cbuf[j % 2][:, :],
                       start=(j == 0), stop=(j == DMT - 1), after=aft, inc=True)
            sqh[j % 2] = tgt
        kb.set_bank_prod(6, tgt)
        brm = bc_sum(bcB[:, :], PS[6][0:1, :], st0, 4, {"t": tgt, "v": vv})
        kb.set_bank_cons(6, "s", brm)
        srm = s_act(bcB[:, :], bcB[:, :], AF.Ln, bias=cconst[:, 0:1],
                    after={"s": brm})
        srm = s_act(bcB[:, :], bcB[:, :], AF.Exp, scale=cconst[:, 3:4],
                    after={"s": srm})
        hr_done = None
        for j in range(DMT):
            hr_done = v_tt(slc(hr_sb, j, TC), slc(h_sb, j, TC), bcB[:, :], ALU.mult,
                           after=merge({"s": srm},
                                       {"t": P["inproj_t"]} if j == 0 else None))
        if c == 0:
            dump("hr", hr_sb[:, :], DMT * TC, bf=True)
        return {"hr": hr_done, "sqh": tgt, "c": c}

    def make_stage1(c, b, lnh):
        sw = s_win[b].rearrange("(j p) m -> p j m", p=128)
        ctx = {"xc": {}, "evt": {}, "evs": {}, "xproj_t": None, "dl": {},
               "zev": {}, "sg": {}, "c": c, "b": b}

        def inproj_m(m):
            buf = m % 2
            wdma = kb.dma(wb_in[buf][:, 0:DMT * 128].rearrange(
                              "p (j m) -> p j m", j=DMT),
                          sw[:, :, m * 128:(m + 1) * 128],
                          after={"t": wb_guard[buf],
                                 "v": lnh["hr"] if m == 0 else None})
            pa = m % 2
            aft = merge({"d": wdma}, {"v": lnh["hr"]},
                        kb.bank_prod_after(pa), kb.bank_cons_after(pa))
            tl = None
            for k in range(DMT):
                tl = t_mm(pa, PS[pa][:, :], wb_in[buf][:, k * 128:(k + 1) * 128],
                          slc(hr_sb, k, TC), start=(k == 0), stop=(k == DMT - 1),
                          after=aft if k == 0 else None, inc=(k == DMT - 1))
            kb.set_bank_prod(pa, tl)
            wb_guard[buf] = tl
            return pa, tl

        def pieceT(dt):
            pa, tl = inproj_m(dt)
            sev = s_act(xcp_sl(dt, 3, TC + 3), PS[pa][:, :], AF.Identity,
                        bias=cinb[b][:, dt:dt + 1], after={"t": tl})
            kb.set_bank_cons(pa, "s", sev)
            ctx["evt"][dt] = tl

        def piece(dt):
            st = s_act(xcp_sl(dt, 0, 3), tails[b][:, dt * 3:(dt + 1) * 3],
                       AF.Identity, after={"g": g_done})
            va = v_ts(cbuf[dt % 2][:, :], xcp_sl(dt, 0, TC),
                      ccw[b][:, dt * 4: dt * 4 + 1], ALU.mult,
                      after={"s": st, "t": lnh["sqh"]})
            for k2 in (1, 2, 3):
                va = v_stt(cbuf[dt % 2][:, :], xcp_sl(dt, k2, TC + k2),
                           ccw[b][:, dt * 4 + k2: dt * 4 + k2 + 1],
                           cbuf[dt % 2][:, :], ALU.mult, ALU.add)
            sb_ = s_act(sgbuf[dt % 2][:, :], cbuf[dt % 2][:, :], AF.Exp,
                        scale=cconst[:, 4:5],
                        bias=cnegcb[b][:, dt:dt + 1], after={"v": va})
            sb_ = s_act(sgbuf[dt % 2][:, :], sgbuf[dt % 2][:, :], AF.Ln,
                        bias=cconst[:, 7:8], after={"s": sb_})
            ctx["sg"][dt] = s_act(sgbuf[dt % 2][:, :], sgbuf[dt % 2][:, :],
                                  AF.Exp, scale=cconst[:, 4:5],
                                  after={"s": sb_})
            s_act(tails[b][:, dt * 3:(dt + 1) * 3], xcp_sl(dt, TC, TC + 3),
                  AF.Identity)
            if dt >= 1:
                conv_fin(dt - 1)
            if dt >= 2:
                k = dt - 2
                aft = merge(kb.bank_prod_after(5) if k == 0 else None,
                            {"v": ctx["xc"][k]})
                ctx["xproj_t"] = t_mm(5, PS[5][0:96, :],
                                      xpw[b][:, k * 96:(k + 1) * 96],
                                      slc(xc_sb, k, TC), start=(k == 0),
                                      stop=False, after=aft,
                                      inc=False)

        def conv_fin(dt):
            ctx["xc"][dt] = v_stt(slc(xc_sb, dt, TC), cbuf[dt % 2][:, :],
                                  ccb[b][:, dt:dt + 1], sgbuf[dt % 2][:, :],
                                  ALU.add, ALU.mult,
                                  after=merge({"s": ctx["sg"][dt]},
                                              {"v": vdx_cur.get(dt)}))

        def dtmm_softplus(dt):
            pa = 6 + (dt % 2)
            aft = merge(kb.bank_prod_after(pa), kb.bank_cons_after(pa),
                        {"s": ctx["dbc_s"]})
            tm = t_mm(pa, PS[pa][:, :], dtw[b][:, dt * 128:(dt + 1) * 128],
                      dbc[0:64, :], start=True, stop=True, after=aft, inc=True)
            kb.set_bank_prod(pa, tm)
            se = s_act(ebuf2[:, :], PS[pa][:, :], AF.Exp,
                       bias=cdtb[b][:, dt:dt + 1], after={"t": tm})
            kb.set_bank_cons(pa, "s", se)
            ctx["dl"][dt] = s_act(dl3[dt % 3][:, :], ebuf2[:, :], AF.Ln,
                                  bias=cconst[:, 7:8],
                                  after=merge({"s": se},
                                              {"v": vdx_cur.get((dt - 2) % DINT)}))

        def tail():
            conv_fin(DINT - 1)
            k = DINT - 2
            aft = merge({"v": ctx["xc"][k]},
                        kb.bank_prod_after(5) if ctx["xproj_t"] is None else None)
            t_mm(5, PS[5][0:96, :], xpw[b][:, k * 96:(k + 1) * 96],
                 slc(xc_sb, k, TC), start=(ctx["xproj_t"] is None),
                 stop=False, after=aft, inc=False)
            ctx["xproj_t"] = 1
            k = DINT - 1
            tl = t_mm(5, PS[5][0:96, :], xpw[b][:, k * 96:(k + 1) * 96],
                      slc(xc_sb, k, TC), start=False,
                      stop=True, after={"v": ctx["xc"][k]}, inc=True)
            kb.set_bank_prod(5, tl)
            ctx["dbc_s"] = s_act(dbc[:, :], PS[5][0:96, :], AF.Identity,
                                 after={"t": tl})
            kb.set_bank_cons(5, "s", ctx["dbc_s"])
            if c == 0 and b == 0:
                dump("dbc", dbc[:, :], 512, bf=True, parts=96)
            dtmm_softplus(0)
            dtmm_softplus(1)

        ctx["piece"] = piece
        ctx["pieceT"] = pieceT
        ctx["tail"] = tail
        ctx["dtmm_softplus"] = dtmm_softplus
        return ctx

    def emit_zpiece(ctx, i):
        b = ctx["b"]
        m = DINT + i
        buf = m % 2
        sw = s_win[b].rearrange("(j p) m -> p j m", p=128)
        wdma = kb.dma(wb_in[buf][:, 0:DMT * 128].rearrange(
                          "p (j m) -> p j m", j=DMT),
                      sw[:, :, m * 128:(m + 1) * 128],
                      after={"t": wb_guard[buf]})
        pa = m % 2
        aft = merge({"d": wdma}, kb.bank_prod_after(pa),
                    kb.bank_cons_after(pa))
        tl = None
        for k in range(DMT):
            tl = t_mm(pa, PS[pa][:, :], wb_in[buf][:, k * 128:(k + 1) * 128],
                      slc(hr_sb, k, TC), start=(k == 0), stop=(k == DMT - 1),
                      after=aft if k == 0 else None, inc=(k == DMT - 1))
        kb.set_bank_prod(pa, tl)
        wb_guard[buf] = tl
        sev = s_act(slc(z_sb, i, TC), PS[pa][:, :], AF.Silu,
                    bias=cinb[b][:, DINT + i:DINT + i + 1],
                    after=merge({"t": tl},
                                {"t": P["outp_t"]} if i == 0 else None))
        kb.set_bank_cons(pa, "s", sev)
        ctx["zev"][i] = sev

    def emit_scan(ctx, nxt_cb, nxt_cb_top=None):
        c, b = ctx["c"], ctx["b"]
        scan_v = {}
        yfin_v = {}
        gsnap = kb.cnt["g"]
        bcast_s = {}
        gsep_g = {(0, 0): kb.run(
            "gpsimd",
            lambda e, o=dBu4b[0][:, TC:TC + 1], i=state[b][:, 1:2]:
            e.tensor_copy(out=o, in_=i),
            inc="g", after={"v": P["last_scan"]})}

        def emit_sda2(dt2, np2, aft_v):
            q4 = (np2 // 2) % 3
            si = dt2 * NST + np2
            sa = s_act(dAb[q4][:, 0:TC], dl3[dt2 % 3][:, :], AF.Exp,
                       scale=cnegA[b][:, si:si + 1], after={"v": aft_v})
            return s_act(dAb[q4][:, TC + 1:TC2P], dl3[dt2 % 3][:, :],
                         AF.Exp, scale=cnegA[b][:, si + 1:si + 2],
                         after={"s": sa})

        sda_q = {}
        for np0 in (0, 2, 4):
            sda_q[(0, np0)] = emit_sda2(0, np0, P["last_scan"])

        def bcast(i):
            pa = 6 + (i % 2)
            aft = merge(kb.bank_prod_after(pa), kb.bank_cons_after(pa),
                        {"s": ctx["dbc_s"]}, {"g": g_done})
            tm = t_mm(pa, PS[pa][:, :], bcol(selbc, i), dbc[:, :],
                      start=True, stop=True, after=aft, inc=True)
            kb.set_bank_prod(pa, tm)
            if i < 16:
                dst = slc(Ball, i, TC)
            else:
                n_ = i - 16
                o_ = (n_ // 2) * TC2P + (n_ % 2) * (TC + 1)
                dst = Call[:, o_:o_ + TC]
            sc = s_act(dst, PS[pa][:, :], AF.Identity, after={"t": tm})
            kb.set_bank_cons(pa, "s", sc)
            return sc

        pstr = list(dxc0[:, :].ap[0])
        psy_t = {}
        psy_ch = {}
        for dt in range(DINT):
            vdx = v_tt(dxc0[:, :], dl3[dt % 3][:, :], slc(xc_sb, dt, TC),
                       ALU.mult, after={"s": ctx["dl"][dt], "g": g_done,
                                        "v": ctx["xc"][dt]})
            vdx_cur[dt] = vdx
            for cb2fn in (nxt_cb_top or {}).get(dt, []):
                cb2fn()
            pyb = 3 + (dt % 2)
            sxc = s_act(xcd[dt % 2][:, :], slc(xc_sb, dt, TC),
                        AF.Identity, scale=cD[b][:, dt:dt + 1],
                        after={"t": psy_t.get(dt - 2)})
            aft0 = merge({"s": sxc}, kb.bank_prod_after(pyb))
            im = t_mm(pyb, PS[pyb][:, :], identb[:, :], xcd[dt % 2][:, :],
                      start=True, stop=False, after=aft0, inc=True)
            psy_t[dt] = im
            for np_ in range(0, NST, 2):
                if np_ == 2:
                    if dt < DINT - 2:
                        ctx["dtmm_softplus"](dt + 2)
                p = np_ // 2
                sidx = dt * NST + np_
                if dt == 0:
                    for i2 in (np_, np_ + 1, 16 + np_, 17 + np_):
                        bcast_s[i2] = bcast(i2)
                sda = sda_q[(dt, np_)]
                dxv = bass.AP(dxc0[:, :].tensor, dxc0[:, :].offset,
                              [pstr, [0, 2], [1, TC]])
                dbu = dBu4b[p % 2]
                dbo = bass.AP(dbu[:, :].tensor, dbu[:, :].offset,
                              [list(dbu[:, :].ap[0]), [TC + 1, 2], [1, TC]])
                vdb = v_tt(dbo, dxv,
                           Ball[:, np_ * TC:(np_ + 2) * TC].rearrange(
                               "p (n t) -> p n t", n=2), ALU.mult,
                           after=merge({"s": bcast_s.get(np_ + 1), "v": vdx}
                                       if dt == 0 else None))
                gsep = gsep_g[(dt, np_)]
                if np_ >= 4:
                    gprev = state_g.get((b, dt, np_ - 4))
                elif dt >= 1:
                    gprev = state_g.get((b, dt - 1, NST + np_ - 4))
                else:
                    gprev = gsnap
                ginit = state_g.get((b, dt, np_))
                vsc = kb.run("vector",
                             lambda e, o=hbb[p % 2][:, :], a=dAb[p % 3][:, :],
                             bb2=dbu[:, :],
                             ini=state[b][:, sidx:sidx + 1]:
                             e.tensor_tensor_scan(out=o, data0=a, data1=bb2,
                                                  initial=ini, op0=ALU.mult,
                                                  op1=ALU.add),
                             inc="v",
                             after=merge({"s": sda}, {"g": gprev},
                                         {"g": ginit}, {"g": gsep}))
                scan_v[(dt, np_)] = vsc
                nq = dt * (NST // 2) + p + 3
                if nq < DINT * (NST // 2):
                    dt2, p2_ = divmod(nq, NST // 2)
                    sda_q[(dt2, p2_ * 2)] = emit_sda2(dt2, p2_ * 2, vsc)
                nxt = (dt, np_ + 2) if np_ + 2 < NST else (dt + 1, 0)
                if nxt[0] < DINT:
                    nsi = nxt[0] * NST + nxt[1]
                    pg_aft = scan_v.get((dt, np_ - 2)) or \
                        scan_v.get((dt - 1, NST + np_ - 2)) or P["last_scan"]
                    gsep_g[nxt] = kb.run(
                        "gpsimd",
                        lambda e, o=dBu4b[(p + 1) % 2][:, TC:TC + 1],
                        i=state[b][:, nsi + 1:nsi + 2]:
                        e.tensor_copy(out=o, in_=i),
                        inc="g", after={"v": pg_aft})
                hb_ = hbb[p % 2][:, TC - 1:TC]
                hsrc = bass.AP(hb_.tensor, hb_.offset,
                               [list(hb_.ap[0]), [TC + 1, 2]])
                state_g[(b, dt, np_)] = kb.run(
                    "gpsimd", lambda e, o=state[b][:, sidx:sidx + 2], i=hsrc:
                    e.tensor_copy(out=o, in_=i),
                    inc="g", after={"v": vsc})
                pch = psy_ch.get((dt, p - 2))
                if pch is None and p < 2:
                    pch = psy_ch.get((dt - 1, p + 6))
                vch = v_tt(chb[p % 2][:, :], hbb[p % 2][:, :],
                           Call[:, p * TC2P:(p + 1) * TC2P], ALU.mult,
                           after=merge({"s": bcast_s.get(17 + np_)}
                                       if dt == 0 else None,
                                       {"t": pch}))
                t_mm(pyb, PS[pyb][:, :], identb[:, :], chb[p % 2][:, 0:TC],
                     start=False, stop=False, after={"v": vch}, inc=False)
                psy_ch[(dt, p)] = t_mm(pyb, PS[pyb][:, :], identb[:, :],
                                       chb[p % 2][:, TC + 1:TC2P],
                                       start=False, stop=(np_ == NST - 2),
                                       after=None, inc=True)
            kb.set_bank_prod(pyb, psy_ch[(dt, 7)])
            if 1 <= dt <= 4:
                for i in range(4 * (dt - 1), 4 * dt):
                    emit_zpiece(ctx, i)
            if dt == 1:
                yfin_v[0] = v_tt(slc(z_sb, 0, TC), PS[3][:, :],
                                 slc(z_sb, 0, TC), ALU.mult,
                                 after={"s": ctx["zev"][0],
                                        "t": psy_ch[(0, 7)]})
                kb.set_bank_cons(3, "v", yfin_v[0])
            if dt >= 1:
                yfin_v[dt] = v_tt(slc(z_sb, dt, TC), PS[pyb][:, :],
                                  slc(z_sb, dt, TC), ALU.mult,
                                  after={"s": ctx["zev"][dt],
                                         "t": psy_ch[(dt, 7)]})
                kb.set_bank_cons(pyb, "v", yfin_v[dt])
            for cb2fn in nxt_cb.get(dt, []):
                cb2fn()
        ctx["yfin_v"] = yfin_v
        P["last_scan"] = scan_v[(DINT - 1, NST - 2)]
        if c == 0 and b == 0:
            dump("yfin15", z_sb[:, 15 * TC:16 * TC], TC, bf=True)

    def emit_outproj(ctx):
        c, b = ctx["c"], ctx["b"]
        cb_user_v = {}
        outp_t = {}
        swo = s_wout[b].rearrange("(j p) m -> p j m", p=128)
        for m in range(DMT):
            wdo = kb.dma(wb_out[m % 2][:, :].rearrange(
                             "p (j m) -> p j m", j=DINT),
                         swo[:, :, m * 128:(m + 1) * 128],
                         after={"t": outp_t.get(m - 2, P["outp_t"]),
                                "d2": cast_d["wout1" if b == 0 else "wout2"]})
            pa = 3 + (m % 2)
            aft = merge({"d": wdo}, {"v": ctx["yfin_v"][DINT - 1]},
                        kb.bank_prod_after(pa), kb.bank_cons_after(pa))
            tl = None
            for k in range(DINT):
                tl = t_mm(pa, PS[pa][:, :], wb_out[m % 2][:, k * 128:(k + 1) * 128],
                          slc(z_sb, k, TC), start=(k == 0), stop=(k == DINT - 1),
                          after=aft if k == 0 else None, inc=(k == DINT - 1))
            kb.set_bank_prod(pa, tl)
            outp_t[m] = tl
            sev = s_act(cbuf[m % 2][:, :], PS[pa][:, :], AF.Identity,
                        bias=coutb[b][:, m:m + 1],
                        after=merge({"t": tl},
                                    {"v": kb.cnt["v"]} if m < 2 else
                                    {"v": cb_user_v.get(m - 2)}))
            kb.set_bank_cons(pa, "s", sev)
            if b == 0:
                cb_user_v[m] = v_tt(slc(u_sb, m, TC), cbuf[m % 2][:, :],
                                    slc(h_sb, m, TC), ALU.add, after={"s": sev})
            else:
                v_tt(cbuf[m % 2][:, :], cbuf[m % 2][:, :], slc(h_sb, m, TC),
                     ALU.add, after={"s": sev})
                v_tt(cbuf[m % 2][:, :], cbuf[m % 2][:, :], slc(u_sb, m, TC),
                     ALU.mult)
                P["xmid_v"] = v_tt(slc(h_sb, m, TC), cbuf[m % 2][:, :],
                                   slc(x_ln, m, TC), ALU.add)
                cb_user_v[m] = P["xmid_v"]
        P["outp_t"] = outp_t[DMT - 1]
        P["inproj_t"] = wb_guard[0] if wb_guard[0] > wb_guard[1] else wb_guard[1]
        if c == 0 and b == 0:
            dump("u", u_sb[:, :], DMT * TC, bf=True)
        if b == 1:
            if c == 0:
                dump("xmid", h_sb[:, :], DMT * TC)
            P["xmid_dma"] = kb.dma(s_xmid[:, :, c * TC:(c + 1) * TC],
                                   h_sb[:, :].rearrange("p (j t) -> p j t", j=DMT),
                                   after={"v": P["xmid_v"]})

    units = [(c, b) for c in range(NCH) for b in (0, 1)]
    lnh = emit_ln(0)
    cur = make_stage1(0, 0, lnh)
    for dt in range(DINT):
        cur["pieceT"](dt)
        cur["piece"](dt)
    cur["tail"]()
    for idx, (c, b) in enumerate(units):
        nxt = units[idx + 1] if idx + 1 < len(units) else None
        nxt_cb = {}
        nctx = None
        nxt_cb_top = {}
        if nxt is not None and nxt[0] == c:
            nctx = make_stage1(nxt[0], nxt[1], lnh)
            for dt in range(DINT):
                nxt_cb_top[dt] = [(lambda d=dt, nc2=nctx: nc2["pieceT"](d))]
                nxt_cb[dt] = [(lambda d=dt, nc2=nctx: nc2["piece"](d))]
            nxt_cb[DINT - 1].append(lambda nc2=nctx: nc2["tail"]())
        if c == 0 and b == 0:
            dump("xc", xc_sb[:, :], DINT * TC, bf=True)
        emit_scan(cur, nxt_cb, nxt_cb_top)
        if c == 0 and b == 0:
            dump("Ball", Ball[:, :], NST * TC, bf=True)
            dump("Call", Call[:, :], NST * TC, bf=True)
        emit_outproj(cur)
        if nxt is not None and nxt[0] != c:
            lnh = emit_ln(nxt[0])
            nctx = make_stage1(nxt[0], nxt[1], lnh)
            for dt in range(DINT):
                nctx["pieceT"](dt)
                nctx["piece"](dt)
            nctx["tail"]()
        if nctx is not None:
            cur = nctx

    ph1 = {k: kb.cnt[k] for k in ("s", "v", "t", "g", "d")}
    es1.close()

    es2 = ExitStack()
    sb2 = lambda name, shape, dt: es2.enter_context(nc.sbuf_tensor(name, shape, dt))
    x0_sb = sb2("x0_sb", [128, DMT * TC], F32)
    xm2 = [sb2(f"xm_sb{i}", [128, DMT * TC], F32) for i in range(3)]
    h2b = [sb2(f"h2_sb{i}", [128, DMT * TC], BF16) for i in range(2)]
    h3_sb = sb2("h3_sb", [128, DMT * TC], BF16)
    g_sb = sb2("g_sb", [128, DFFT * TC], BF16)
    hu_sb = sb2("hu_sb", [128, HIDT * TC], BF16)
    st20 = sb2("st20", [1, TC], F32)
    bc2A = sb2("bc2A", [128, TC], F32)
    bc2B = sb2("bc2B", [128, TC], F32)
    cb2 = [sb2(f"cb2_{i}", [128, TC], F32) for i in range(2)]
    sgb = sb2("sgb", [128, TC], BF16)
    wbF = [sb2(f"wbF{i}", [128, DMT * 256], BF16) for i in range(2)]
    wbB = [sb2(f"wbB{i}", [128, DFFT * 128], BF16) for i in range(2)]
    wbGU = [sb2(f"wbGU{i}", [128, DMT * 256], BF16) for i in range(2)]

    bar2 = dict(ph1)

    S2 = {"statsv": 0, "outv": {}, "x0d": {}, "dmA": {},
          "ffn1_last": {}, "ffn2_last": {}, "dn_last": {},
          "tu_last": {}, "hu_last": {},
          "ffn1_slot": {}, "wg_slot": {}, "wu_slot": {},
          "h2_done": {}, "gelu_last": {}, "h3_done": {}, "dn_t": {}}

    def stats_ln2(src_tile, pa, pb, ln_affine, out_tile, src_g, out_t):
        t1 = None
        for j in range(DMT):
            aft = merge(kb.bank_prod_after(pa) if j == 0 else None,
                        src_g if j == 0 else None, {"g": g_done})
            t1 = t_mm(pa, PS[pa][0:1, :], onescol[:, 0:1], slc(src_tile, j, TC),
                      start=(j == 0), stop=(j == DMT - 1), after=aft,
                      inc=(j == DMT - 1))
        kb.set_bank_prod(pa, t1)
        sq = [None, None]
        t2 = None
        for j in range(DMT):
            a = merge({"t": sq[j % 2]} if sq[j % 2] else None,
                      merge(src_g, {"v": S2["statsv"]}) if j < 2 else None)
            ss = s_act(cb2[j % 2][:, :], slc(src_tile, j, TC), AF.Square, after=a)
            aft = merge({"s": ss}, kb.bank_prod_after(pb) if j == 0 else None)
            t2 = t_mm(pb, PS[pb][0:1, :], onescol[:, 0:1], cb2[j % 2][:, :],
                      start=(j == 0), stop=(j == DMT - 1), after=aft, inc=True)
            sq[j % 2] = t2
        kb.set_bank_prod(pb, t2)
        bmu = bc_sum(bc2A[:, :], PS[pa][0:1, :], st20, pa,
                     merge({"t": t1}, {"v": S2["statsv"]}))
        kb.set_bank_cons(pa, "s", bmu)
        bex = bc_sum(bc2B[:, :], PS[pb][0:1, :], st20, pb, {"t": t2, "s": bmu})
        kb.set_bank_cons(pb, "s", bex)
        vq = v_tt(cb2[0][:, :], bc2A[:, :], bc2A[:, :], ALU.mult,
                  after={"s": bmu, "t": t2})
        vvar = v_tt(bc2B[:, :], bc2B[:, :], cb2[0][:, :], ALU.subtract,
                    after={"s": bex})
        sA = s_act(bc2B[:, :], bc2B[:, :], AF.Ln, bias=cconst[:, 0:1],
                   after={"v": vvar})
        sA = s_act(bc2B[:, :], bc2B[:, :], AF.Exp, scale=cconst[:, 3:4],
                   after={"s": sA})
        outs = []
        vlast = None
        for j in range(DMT):
            vv = v_tt(cb2[j % 2][:, :], slc(src_tile, j, TC), bc2A[:, :],
                      ALU.subtract,
                      after=merge({"s": sA} if j == 0 else None,
                                  {"s": outs[j - 2]} if j >= 2 else None))
            vv = v_tt(cb2[j % 2][:, :], cb2[j % 2][:, :], bc2B[:, :], ALU.mult)
            vlast = vv
            og = merge({"v": vv}, out_t if j == 0 else None)
            if ln_affine:
                o = s_act(slc(out_tile, j, TC), cb2[j % 2][:, :], AF.Identity,
                          bias=cb_[:, j:j + 1], scale=cg[:, j:j + 1], after=og)
            else:
                o = s_act(slc(out_tile, j, TC), cb2[j % 2][:, :], AF.Identity,
                          after=og)
            outs.append(o)
        S2["statsv"] = vlast
        return outs

    swf1 = s_wf1.rearrange("(j p) m -> p j m", p=128)
    swf2 = s_wf2.rearrange("(j p) m -> p j m", p=128)
    swgu = s_wgu.rearrange("(j p) m -> p j m", p=128)
    swd = s_wd.rearrange("(j p) m -> p j m", p=128)

    x03 = x0_sb[:, :].rearrange("p (j t) -> p j t", j=DMT)
    xm3 = [xm2[i][:, :].rearrange("p (j t) -> p j t", j=DMT) for i in range(3)]

    def dmaA(c):
        aft = merge(bar2 if c == 0 else None,
                    {"v": S2["outv"].get(c - 3)} if c >= 3 else None)
        S2["dmA"][c] = kb.dma(xm3[c % 3], s_xmid[:, :, c * TC:(c + 1) * TC],
                              after=aft, eng="gpsimd")

    def statsA(c):
        outt = {"t": S2["ffn1_last"].get(c - 2)} if c >= 2 else None
        S2["h2_done"][c] = stats_ln2(xm2[c % 3], 6, 7, True, h2b[c % 2],
                                     {"d2": S2["dmA"][c]}, outt)

    def stageB(c):
        h2d = S2["h2_done"][c][-1]
        ge_prev = {"t": S2["ffn2_last"].get(c - 1)} if c >= 1 else None
        tl = sev = None
        for mg in range(DFFT // 2):
            slot = S2["ffn1_slot"].get((c, mg - 2),
                                       S2["ffn1_slot"].get((c - 1, mg - 2 + DFFT // 2), 0))
            wd1 = kb.dma(wbF[mg % 2][:, :].rearrange("p (j m) -> p j m", j=DMT),
                         swf1[:, :, mg * 256:(mg + 1) * 256],
                         after={"t": slot, "d2": cast_d["wf1"]})
            for mm in range(2):
                m = mg * 2 + mm
                pa = m % 3
                aft = merge({"d": wd1}, {"s": h2d},
                            kb.bank_prod_after(pa), kb.bank_cons_after(pa))
                tl = None
                for k in range(DMT):
                    tl = t_mm(pa, PS[pa][:, :],
                              wbF[mg % 2][:, k * 256 + mm * 128: k * 256 + (mm + 1) * 128],
                              slc(h2b[c % 2], k, TC), start=(k == 0),
                              stop=(k == DMT - 1),
                              after=aft if k == 0 else None, inc=(k == DMT - 1))
                kb.set_bank_prod(pa, tl)
                sev = s_act(slc(g_sb, m, TC), PS[pa][:, :], AF.Gelu,
                            bias=cfb1[:, m:m + 1],
                            after=merge({"t": tl}, ge_prev if m == 0 else None))
                kb.set_bank_cons(pa, "s", sev)
            S2["ffn1_slot"][(c, mg)] = tl
        S2["ffn1_last"][c] = tl
        S2["gelu_last"][c] = sev

    def stageC(c):
        xmt = xm2[c % 3]
        gelu_l = S2["gelu_last"][c]
        x1_done = {}
        tl = None
        for m in range(DMT):
            slot = x1_done.get(("t", m - 2),
                               S2["dn_t"].get((c - 1, 6 + m), 0) if m < 2 else 0)
            wd2 = kb.dma(wbB[m % 2][:, :].rearrange("p (j m) -> p j m", j=DFFT),
                         swf2[:, :, m * 128:(m + 1) * 128],
                         after={"t": slot, "d2": cast_d["wf2"]})
            pa = 3 + (m % 2)
            aft = merge({"d": wd2}, {"s": gelu_l},
                        kb.bank_prod_after(pa), kb.bank_cons_after(pa))
            tl = None
            for k in range(DFFT):
                tl = t_mm(pa, PS[pa][:, :], wbB[m % 2][:, k * 128:(k + 1) * 128],
                          slc(g_sb, k, TC), start=(k == 0), stop=(k == DFFT - 1),
                          after=aft if k == 0 else None, inc=(k == DFFT - 1))
            kb.set_bank_prod(pa, tl)
            sev = s_act(cb2[m % 2][:, :], PS[pa][:, :], AF.Identity,
                        bias=cfb2[:, m:m + 1],
                        after=merge({"t": tl},
                                    {"v": kb.cnt["v"]} if m < 2 else
                                    {"v": x1_done.get(m - 2)}))
            kb.set_bank_cons(pa, "s", sev)
            v_tt(slc(xmt, m, TC), slc(xmt, m, TC), cb2[m % 2][:, :], ALU.add,
                 after={"s": sev})
            vx1 = v_tt(slc(xmt, m, TC), slc(xmt, m, TC), slc(x0_sb, m, TC),
                       ALU.add, after={"d": S2["x0d"][c]} if m == 0 else None)
            x1_done[("t", m)] = tl
            x1_done[m] = vx1
        S2["ffn2_last"][c] = tl
        outt = {"t": S2["tu_last"].get(c - 1)} if c >= 1 else None
        S2["h3_done"][c] = stats_ln2(xmt, 6, 7, False, h3_sb,
                                     {"v": x1_done[DMT - 1]}, outt)

    def stageD(c):
        h3d = S2["h3_done"][c][-1]
        vhu = None
        tg = tu = None
        for m in range(HIDT):
            gslot = S2["wg_slot"].get((c, m - 2),
                                      S2["wg_slot"].get((c - 1, m - 2 + HIDT), 0))
            wgu_d = kb.dma(wbGU[m % 2][:, :].rearrange("p (j m) -> p j m", j=DMT),
                           swgu[:, :, m * 256:(m + 1) * 256],
                           after={"t": gslot, "d2": max(cast_d["wg"], cast_d["wu"])})
            pg = (m % 2) * 2
            pu = 1 + (m % 2) * 4
            aftg = merge({"d": wgu_d}, {"s": h3d},
                         kb.bank_prod_after(pg), kb.bank_cons_after(pg))
            aftu = merge(kb.bank_prod_after(pu), kb.bank_cons_after(pu))
            for k in range(DMT):
                tg = t_mm(pg, PS[pg][:, :],
                          wbGU[m % 2][:, k * 256: k * 256 + 128],
                          slc(h3_sb, k, TC), start=(k == 0), stop=(k == DMT - 1),
                          after=aftg if k == 0 else None, inc=(k == DMT - 1))
            for k in range(DMT):
                tu = t_mm(pu, PS[pu][:, :],
                          wbGU[m % 2][:, k * 256 + 128: k * 256 + 256],
                          slc(h3_sb, k, TC), start=(k == 0), stop=(k == DMT - 1),
                          after=aftu if k == 0 else None, inc=(k == DMT - 1))
            kb.set_bank_prod(pg, tg)
            kb.set_bank_prod(pu, tu)
            ssg = s_act(sgb[:, :], PS[pg][:, :], AF.Silu,
                        scale=cconst[:, 5:6],
                        after=merge({"t": tg},
                                    {"v": vhu} if vhu else
                                    {"v": S2["hu_last"].get(c - 1)},
                                    {"g": g_done}))
            kb.set_bank_cons(pg, "s", ssg)
            vhu = v_stt(slc(hu_sb, m, TC), PS[pu][:, :], R127, sgb[:, :],
                        ALU.mult, ALU.mult,
                        after=merge({"s": ssg, "t": tu},
                                    {"t": S2["dn_last"].get(c - 1)}
                                    if m == 0 else None))
            kb.set_bank_cons(pu, "v", vhu)
            S2["wg_slot"][(c, m)] = tu
        S2["tu_last"][c] = tu
        S2["hu_last"][c] = vhu

    def stageE(c):
        xmt = xm2[c % 3]
        hu_l = S2["hu_last"][c]
        out_v = {}
        tl = None
        for m in range(DMT):
            slot = S2["dn_t"].get((c, m - 2), S2["ffn2_last"][c])
            wd3 = kb.dma(wbB[m % 2][:, 0:HIDT * 128].rearrange("p (j m) -> p j m", j=HIDT),
                         swd[:, :, m * 128:(m + 1) * 128],
                         after={"t": slot, "d2": cast_d["wd"]})
            pa = 3 + (m % 2)
            aft = merge({"d": wd3}, {"v": hu_l},
                        kb.bank_prod_after(pa), kb.bank_cons_after(pa))
            tl = None
            for k in range(HIDT):
                tl = t_mm(pa, PS[pa][:, :], wbB[m % 2][:, k * 128:(k + 1) * 128],
                          slc(hu_sb, k, TC), start=(k == 0), stop=(k == HIDT - 1),
                          after=aft if k == 0 else None, inc=(k == HIDT - 1))
            kb.set_bank_prod(pa, tl)
            S2["dn_t"][(c, m)] = tl
            sev = s_act(cb2[m % 2][:, :], PS[pa][:, :], AF.Identity,
                        scale=cconst[:, 5:6],
                        after=merge({"t": tl},
                                    {"v": kb.cnt["v"]} if m < 2 else
                                    {"v": out_v.get(m - 2)}, {"g": g_done}))
            kb.set_bank_cons(pa, "s", sev)
            out_v[m] = v_tt(slc(x0_sb, m, TC), slc(xmt, m, TC), cb2[m % 2][:, :],
                            ALU.add, after={"s": sev})
            kb.dma(out_h[:, m, c * TC:(c + 1) * TC], slc(x0_sb, m, TC),
                   after={"v": out_v[m]})
        S2["outv"][c] = out_v[DMT - 1]
        S2["dn_last"][c] = tl
        if c + 1 < NCH:
            S2["x0d"][c + 1] = kb.dma(x03, x_h[:, :, (c + 1) * TC:(c + 2) * TC])

    S2["x0d"][0] = kb.dma(x03, x_h[:, :, 0:TC], after=bar2)
    dmaA(0)
    statsA(0)
    dmaA(1)
    stageB(0)
    for c in range(NCH):
        if c + 1 < NCH:
            statsA(c + 1)
        stageC(c)
        if c + 1 < NCH:
            stageB(c + 1)
        stageD(c)
        stageE(c)
        if c + 2 < NCH:
            dmaA(c + 2)

    es2.close()

    for sn in ("d", "d2", "d3"):
        fv = kb.cnt[sn]
        if fv:
            kb.prog["sync"].append(
                lambda e, v=fv, s=kb.sem[sn]: e.wait_ge(s, v))

    with nc.Block() as blk:
        blk.sync(lambda e: [f(e) for f in kb.prog["sync"]])
        blk.scalar(lambda e: [f(e) for f in kb.prog["scalar"]])
        blk.vector(lambda e: [f(e) for f in kb.prog["vector"]])
        blk.tensor(lambda e: [f(e) for f in kb.prog["tensor"]])
        blk.gpsimd(lambda e: [f(e) for f in kb.prog["gpsimd"]])

    es_top.close()
    return nc


def _vec128(v, k):
    v = np.ascontiguousarray(np.asarray(v, np.float32))
    return np.ascontiguousarray(v.reshape(k, 128).T)


def kernel(**inputs):
    ii = {k: np.asarray(v) for k, v in inputs.items()}
    x = np.ascontiguousarray(ii["L0_attn_x"], dtype=np.float32)

    nc = build_nc()

    common = {}
    for b, pre in ((1, "L0_attn_m1_"), (2, "L0_attn_m2_")):
        common[f"win{b}"] = np.ascontiguousarray(ii[pre + "in_w"], np.float32)
        common[f"wout{b}"] = np.ascontiguousarray(ii[pre + "out_w"], np.float32)
        common[f"wxp{b}"] = np.ascontiguousarray(
            ii[pre + "xproj_w"].astype(np.float32).reshape(16, 128, 96).transpose(1, 0, 2))
        common[f"wdt{b}"] = np.ascontiguousarray(ii[pre + "dt_w"], np.float32)
        common[f"v_rms{b}"] = _vec128(ii[pre + "rms_w"], 8)
        common[f"v_inb{b}"] = _vec128(ii[pre + "in_b"], 32)
        common[f"v_cw{b}"] = np.ascontiguousarray(
            ii[pre + "conv_w"].astype(np.float32).reshape(16, 128, 4)
            .transpose(1, 0, 2).reshape(128, 64))
        common[f"v_cb{b}"] = _vec128(ii[pre + "conv_b"], 16)
        common[f"v_dtb{b}"] = _vec128(ii[pre + "dt_b"], 16)
        common[f"v_alog{b}"] = np.ascontiguousarray(
            ii[pre + "A_log"].astype(np.float32).reshape(16, 128, 16)
            .transpose(1, 0, 2).reshape(128, 256))
        common[f"v_D{b}"] = _vec128(ii[pre + "D"], 16)
        common[f"v_outb{b}"] = _vec128(ii[pre + "out_b"], 8)
    common["wf1"] = np.ascontiguousarray(ii["L0_attn_ffn_w1"], np.float32)
    common["wf2"] = np.ascontiguousarray(ii["L0_attn_ffn_w2"], np.float32)
    common["v_fb1"] = _vec128(ii["L0_attn_ffn_b1"], 32)
    common["v_fb2"] = _vec128(ii["L0_attn_ffn_b2"], 8)
    common["v_g"] = _vec128(ii["L0_attn_ln_gamma"], 8)
    common["v_b"] = _vec128(ii["L0_attn_ln_beta"], 8)
    import ml_dtypes
    sel = np.zeros((96, 32), dtype=ml_dtypes.bfloat16)
    for n in range(16):
        sel[64 + n, n] = 1
        sel[80 + n, 16 + n] = 1
    common["selbc_h"] = sel
    common["ident_h"] = np.eye(128, dtype=ml_dtypes.bfloat16)
    common["wg"] = np.ascontiguousarray(ii["L0_ffn_w_gate"].T.astype(np.int32))
    common["wu"] = np.ascontiguousarray(ii["L0_ffn_w_up"].T.astype(np.int32))
    common["wd"] = np.ascontiguousarray(ii["L0_ffn_w_down"].T.astype(np.int32))

    in_maps = []
    for cidx in range(8):
        m = dict(common)
        xc = np.ascontiguousarray(
            x[cidx].T.reshape(DMT, 128, T).transpose(1, 0, 2))
        m["x"] = xc
        in_maps.append(m)

    res = run_bass_kernel_spmd(nc, in_maps, core_ids=list(range(8)))
    outs = []
    for cidx in range(8):
        o = res.results[cidx]["out"]
        outs.append(np.ascontiguousarray(
            o.transpose(1, 0, 2).reshape(DM, T).T))
    return np.stack(outs, axis=0).astype(np.float32)


# version 24
# speedup vs baseline: 1.1705x; 1.0045x over previous
import numpy as np
from contextlib import ExitStack
import concourse.bass as bass
import concourse.mybir as mybir
from concourse.bass_utils import run_bass_kernel_spmd

F32 = mybir.dt.float32
BF16 = mybir.dt.bfloat16
I32 = mybir.dt.int32
AF = mybir.ActivationFunctionType
ALU = mybir.AluOpType

T = 2048
TC = 512
NCH = T // TC
DM = 1024
DMT = DM // 128
DIN = 2048
DINT = DIN // 128
NST = 16
DFF = 4096
DFFT = DFF // 128
HID = 3072
HIDT = HID // 128
EPS = 1e-5
R127 = np.float32(1.0) / np.float32(127.0)
TC2P = 2 * TC + 1

ENGS = ("sync", "scalar", "vector", "tensor", "gpsimd")


class KB:

    def __init__(self, nc, es):
        self.nc = nc
        self.prog = {e: [] for e in ENGS}
        self.sem = {}
        self.cnt = {}
        self.mark = {e: {} for e in ENGS}
        for s in ("d", "d2", "d3", "s", "v", "t", "g"):
            self.sem[s] = es.enter_context(nc.semaphore("sem_" + s))
            self.cnt[s] = 0
        self.bank = {}

    def wait(self, eng, sname, val):
        if val is None or val <= 0:
            return
        if self.mark[eng].get(sname, 0) >= val:
            return
        self.mark[eng][sname] = val
        sem = self.sem[sname]
        self.prog[eng].append(lambda e, sem=sem, val=val: e.wait_ge(sem, val))

    def waits(self, eng, d):
        if d:
            for k, v in d.items():
                self.wait(eng, k, v)

    def run(self, eng, fn, inc=None, n=1, after=None):
        self.waits(eng, after)
        tgt = None
        if inc:
            self.cnt[inc] += n
            tgt = self.cnt[inc]
            sem = self.sem[inc]
            self.prog[eng].append(lambda e, fn=fn, sem=sem, n=n: fn(e).then_inc(sem, n))
        else:
            self.prog[eng].append(lambda e, fn=fn: fn(e))
        return tgt

    def dma(self, out, in_, after=None, eng="sync"):
        sem = {"gpsimd": "d2", "scalar": "d3"}.get(eng, "d")
        return self.run(eng, lambda e, o=out, i=in_: e.dma_start(out=o, in_=i),
                        inc=sem, n=16, after=after)

    def bank_prod_after(self, name):
        st = self.bank.get(name)
        if st and st.get("cons"):
            return {st["cons"][0]: st["cons"][1]}
        return {}

    def set_bank_prod(self, name, val):
        self.bank[name] = {"prod": ("t", val), "cons": None}

    def bank_cons_after(self, name):
        st = self.bank.get(name)
        if st and st.get("prod"):
            return {st["prod"][0]: st["prod"][1]}
        return {}

    def set_bank_cons(self, name, sem, val):
        st = self.bank.setdefault(name, {"prod": None, "cons": None})
        st["cons"] = (sem, val)


def merge(*ds):
    out = {}
    for d in ds:
        if d:
            for k, v in d.items():
                if v is None:
                    continue
                if k not in out or out[k] < v:
                    out[k] = v
    return out


def bcol(tile, i):
    col = tile[:, i:i + 1]
    return bass.AP(col.tensor, col.offset, [list(col.ap[0]), [0, 128]])


def build_nc(debug=False):
    nc = bass.Bass("TRN2", target_bir_lowering=False)

    x_h = nc.dram_tensor("x", [128, DMT, T], F32, kind="ExternalInput").ap()
    out_h = nc.dram_tensor("out", [128, DMT, T], F32, kind="ExternalOutput").ap()

    w_in = [nc.dram_tensor(f"win{b}", [DM, 2 * DIN], F32, kind="ExternalInput").ap() for b in (1, 2)]
    w_out = [nc.dram_tensor(f"wout{b}", [DIN, DM], F32, kind="ExternalInput").ap() for b in (1, 2)]
    w_xp = [nc.dram_tensor(f"wxp{b}", [128, DINT, 96], F32, kind="ExternalInput").ap() for b in (1, 2)]
    w_dt = [nc.dram_tensor(f"wdt{b}", [64, DIN], F32, kind="ExternalInput").ap() for b in (1, 2)]
    w_f1 = nc.dram_tensor("wf1", [DM, DFF], F32, kind="ExternalInput").ap()
    w_f2 = nc.dram_tensor("wf2", [DFF, DM], F32, kind="ExternalInput").ap()
    w_g = nc.dram_tensor("wg", [DM, HID], I32, kind="ExternalInput").ap()
    w_u = nc.dram_tensor("wu", [DM, HID], I32, kind="ExternalInput").ap()
    w_d = nc.dram_tensor("wd", [HID, DM], I32, kind="ExternalInput").ap()

    v_g = nc.dram_tensor("v_g", [128, DMT], F32, kind="ExternalInput").ap()
    v_b = nc.dram_tensor("v_b", [128, DMT], F32, kind="ExternalInput").ap()
    v_rms = [nc.dram_tensor(f"v_rms{b}", [128, DMT], F32, kind="ExternalInput").ap() for b in (1, 2)]
    v_inb = [nc.dram_tensor(f"v_inb{b}", [128, 32], F32, kind="ExternalInput").ap() for b in (1, 2)]
    v_cw = [nc.dram_tensor(f"v_cw{b}", [128, 64], F32, kind="ExternalInput").ap() for b in (1, 2)]
    v_cb = [nc.dram_tensor(f"v_cb{b}", [128, DINT], F32, kind="ExternalInput").ap() for b in (1, 2)]
    v_dtb = [nc.dram_tensor(f"v_dtb{b}", [128, DINT], F32, kind="ExternalInput").ap() for b in (1, 2)]
    v_alog = [nc.dram_tensor(f"v_alog{b}", [128, DINT * NST], F32, kind="ExternalInput").ap() for b in (1, 2)]
    v_D = [nc.dram_tensor(f"v_D{b}", [128, DINT], F32, kind="ExternalInput").ap() for b in (1, 2)]
    v_outb = [nc.dram_tensor(f"v_outb{b}", [128, DMT], F32, kind="ExternalInput").ap() for b in (1, 2)]
    v_fb1 = nc.dram_tensor("v_fb1", [128, DFFT], F32, kind="ExternalInput").ap()
    v_fb2 = nc.dram_tensor("v_fb2", [128, DMT], F32, kind="ExternalInput").ap()
    selbc_h = nc.dram_tensor("selbc_h", [96, 32], BF16, kind="ExternalInput").ap()
    ident_h = nc.dram_tensor("ident_h", [128, 128], BF16, kind="ExternalInput").ap()

    s_win = [nc.dram_tensor(f"s_win{b}", [DM, 2 * DIN], BF16, kind="Internal").ap() for b in (1, 2)]
    s_wout = [nc.dram_tensor(f"s_wout{b}", [DIN, DM], BF16, kind="Internal").ap() for b in (1, 2)]
    s_wf1 = nc.dram_tensor("s_wf1", [DM, DFF], BF16, kind="Internal").ap()
    s_wf2 = nc.dram_tensor("s_wf2", [DFF, DM], BF16, kind="Internal").ap()
    s_wgu = nc.dram_tensor("s_wgu", [DM, 2 * HID], BF16, kind="Internal").ap()
    s_wd = nc.dram_tensor("s_wd", [HID, DM], BF16, kind="Internal").ap()
    s_xmid = nc.dram_tensor("s_xmid", [128, DMT, T], F32, kind="Internal").ap()

    es_top = ExitStack()
    kb = KB(nc, es_top)

    dbg_names = []
    dbg_h = nc.dram_tensor("dbg", [128, 40 * 4096], F32, kind="ExternalOutput").ap() if debug else None
    dbgb_h = nc.dram_tensor("dbgb", [128, 40 * 4096], BF16, kind="ExternalOutput").ap() if debug else None
    dbg_off = [0, 0]

    def dump(name, ap, count, bf=False, parts=128):
        if not debug:
            return
        k = 1 if bf else 0
        dst = (dbgb_h if bf else dbg_h)[0:parts, dbg_off[k]:dbg_off[k] + count]
        aftr = {kk: kb.cnt[kk] for kk in ("s", "v", "t", "g")}
        tgt = kb.dma(dst, ap, after=aftr)
        dbg_names.append((name, k, dbg_off[k], count))
        dbg_off[k] += count
        for e in ("scalar", "vector", "tensor", "gpsimd", "sync"):
            kb.wait(e, "d", tgt)
    nc._dbg_names = dbg_names

    sb = lambda name, shape, dt: es_top.enter_context(nc.sbuf_tensor(name, shape, dt))
    onescol = sb("onescol", [128, 1], F32)
    ones_k1f = sb("ones_k1f", [1, 128], F32)
    cconst = sb("cconst", [128, 8], F32)
    selbc = sb("selbc", [96, 32], BF16)
    identb = sb("identb", [128, 128], BF16)
    cg = sb("cg", [128, DMT], F32)
    cb_ = sb("cb_", [128, DMT], F32)
    cinb = [sb(f"cinb{b}", [128, 32], F32) for b in (0, 1)]
    ccw = [sb(f"ccw{b}", [128, 64], F32) for b in (0, 1)]
    ccb = [sb(f"ccb{b}", [128, DINT], F32) for b in (0, 1)]
    cdtb = [sb(f"cdtb{b}", [128, DINT], F32) for b in (0, 1)]
    cnegcb = [sb(f"cnegcb{b}", [128, DINT], F32) for b in (0, 1)]
    cexpA = [sb(f"cexpA{b}", [128, DINT * NST], F32) for b in (0, 1)]
    cD = [sb(f"cD{b}", [128, DINT], F32) for b in (0, 1)]
    coutb = [sb(f"coutb{b}", [128, DMT], F32) for b in (0, 1)]
    cfb1 = sb("cfb1", [128, DFFT], F32)
    cfb2 = sb("cfb2", [128, DMT], F32)
    crms = [sb(f"crms{b}", [128, DMT], F32) for b in (0, 1)]
    xpw = [sb(f"xpw{b}", [128, DINT * 96], BF16) for b in (0, 1)]
    dtw = [sb(f"dtw{b}", [64, DIN], BF16) for b in (0, 1)]
    tails = [sb(f"tails{b}", [128, DINT * 3], F32) for b in (0, 1)]
    state = [sb(f"state{b}", [128, DINT * NST], F32) for b in (0, 1)]

    PS = [es_top.enter_context(nc.psum_tensor(f"ps{i}", [128, 512], F32)) for i in range(8)]

    es_pre = ExitStack()
    wtf = [es_pre.enter_context(nc.sbuf_tensor(f"wtf{i}", [128, 4096], F32))
           for i in range(2)]
    wto = [es_pre.enter_context(nc.sbuf_tensor(f"wto{i}", [128, 4096], BF16))
           for i in range(2)]
    cast_guard = [0, 0]

    def gmemset(ap, val):
        return kb.run("gpsimd", lambda e, a=ap, v=val: e.memset(a, v), inc="g")

    gmemset(onescol[:, :], 1.0)
    gmemset(ones_k1f[:, :], 1.0)
    for j, val in enumerate([EPS, EPS * EPS, 1.0 + EPS, -0.5, -1.0, float(R127),
                             1.0 / DM, 1.0]):
        gmemset(cconst[:, j:j + 1], val)
    for b in (0, 1):
        gmemset(tails[b][:, :], 0.0)
        gmemset(state[b][:, :], 0.0)
    g_done = kb.cnt["g"]

    for dst, src in [(cg, v_g), (cb_, v_b), (cfb1, v_fb1), (cfb2, v_fb2),
                     (selbc, selbc_h), (identb, ident_h)]:
        kb.dma(dst[:, :], src)
    for b in (0, 1):
        for dst, src in [(cinb[b], v_inb[b]), (ccw[b], v_cw[b]), (ccb[b], v_cb[b]),
                         (cD[b], v_D[b]), (coutb[b], v_outb[b]), (crms[b], v_rms[b])]:
            kb.dma(dst[:, :], src)

    for b in (0, 1):
        d1 = kb.dma(wtf[0][:, 0:DINT * NST], v_alog[b])
        se_ = kb.run("scalar", lambda e, o=cexpA[b][:, :], i=wtf[0][:, 0:DINT * NST]:
                     e.activation(out=o, in_=i, func=AF.Exp), inc="s", after={"d": d1})
        kb.run("vector", lambda e, o=cexpA[b][:, :], i=cexpA[b][:, :]:
               e.tensor_scalar(out=o, in0=i, scalar1=-1.0, scalar2=None, op0=ALU.mult),
               inc="v", after={"s": se_})
        kb.dma(cdtb[b][:, :], v_dtb[b])
        dcb = kb.dma(wtf[0][:, DINT * NST:DINT * NST + DINT], v_cb[b],
                     after={"s": se_})
        kb.run("vector", lambda e, o=cnegcb[b][:, :], i=wtf[0][:, DINT * NST:DINT * NST + DINT]:
               e.tensor_scalar(out=o, in0=i, scalar1=-1.0, scalar2=None, op0=ALU.mult),
               inc="v", after={"d": dcb})
    cnegA = cexpA
    v_after_pre = kb.cnt["v"]

    for b in (0, 1):
        d1 = kb.dma(wtf[1][:, 0:DINT * 96], w_xp[b].rearrange("p j m -> p (j m)"),
                    after={"v": v_after_pre, "s": cast_guard[1]})
        cg1 = kb.run("scalar", lambda e, o=xpw[b][:, :], i=wtf[1][:, 0:DINT * 96]:
                     e.activation(out=o, in_=i, func=AF.Identity), inc="s", after={"d": d1})
        cast_guard[1] = cg1
        d2 = kb.dma(wtf[1][0:64, 2048:4096], w_dt[b], after={"s": cast_guard[1]})
        cg1 = kb.run("scalar", lambda e, o=dtw[b][:, :], i=wtf[1][0:64, 2048:4096]:
                     e.activation(out=o, in_=i, func=AF.Identity), inc="s", after={"d": d2})
        cast_guard[1] = cg1

    def cast_weight_scaled(src, dst, rows, cols, scale_tile, j0=0):
        for j in range(rows // 128):
            jj = (j0 + j) % 2
            d1 = kb.dma(wtf[jj][:, 0:cols], src[j * 128:(j + 1) * 128, :],
                        after={"s": cast_guard[jj]})
            sc = scale_tile[:, j:j + 1]
            cst = kb.run("scalar", lambda e, o=wto[jj][:, 0:cols],
                         i=wtf[jj][:, 0:cols], s=sc:
                         e.activation(out=o, in_=i, func=AF.Identity, scale=s),
                         inc="s", after={"d": d1})
            cast_guard[jj] = cst
            kb.dma(dst[j * 128:(j + 1) * 128, :], wto[jj][:, 0:cols],
                   after={"s": cst})

    cast_weight_scaled(w_in[0], s_win[0], DM, 2 * DIN, crms[0])
    cast_weight_scaled(w_in[1], s_win[1], DM, 2 * DIN, crms[1], j0=0)
    cast_d = {}
    for src, dst, key in [(w_out[0], s_wout[0], "wout1"), (w_out[1], s_wout[1], "wout2"),
                          (w_f1, s_wf1, "wf1"), (w_f2, s_wf2, "wf2"),
                          (w_d, s_wd, "wd")]:
        cast_d[key] = kb.dma(dst, src, eng="gpsimd")
    wgu_v = s_wgu.rearrange("r (m t c) -> r m t c", t=2, c=128)
    wg_v = w_g.rearrange("r (m c) -> r m c", c=128)
    wu_v = w_u.rearrange("r (m c) -> r m c", c=128)
    hh = HIDT // 2
    kb.dma(wgu_v[:, 0:hh, 0, :], wg_v[:, 0:hh, :], eng="gpsimd")
    cast_d["wg"] = kb.dma(wgu_v[:, hh:HIDT, 0, :], wg_v[:, hh:HIDT, :], eng="gpsimd")
    kb.dma(wgu_v[:, 0:hh, 1, :], wu_v[:, 0:hh, :], eng="gpsimd")
    cast_d["wu"] = kb.dma(wgu_v[:, hh:HIDT, 1, :], wu_v[:, hh:HIDT, :], eng="gpsimd")

    pre_s, pre_v, pre_g, pre_d = kb.cnt["s"], kb.cnt["v"], kb.cnt["g"], kb.cnt["d"]
    es_pre.close()

    es1 = ExitStack()
    sb1 = lambda name, shape, dt: es1.enter_context(nc.sbuf_tensor(name, shape, dt))
    x_ln = sb1("x_ln", [128, DMT * TC], F32)
    h_sb = sb1("h_sb", [128, DMT * TC], F32)
    hr_sb = sb1("hr_sb", [128, DMT * TC], BF16)
    st0 = sb1("st0", [1, TC], F32)
    bcA = sb1("bcA", [128, TC], F32)
    bcB = sb1("bcB", [128, TC], F32)
    cbuf = [sb1(f"cbuf{i}", [128, TC], F32) for i in range(2)]
    sgbuf = [sb1(f"sgbuf{i}", [128, TC], F32) for i in range(2)]
    ebuf2 = sb1("ebuf2", [128, TC], F32)
    xcp = sb1("xcp", [128, DINT * (TC + 3)], BF16)
    dl3 = [sb1(f"dl3_{i}", [128, TC], BF16) for i in range(3)]
    z_sb = sb1("z_sb", [128, DINT * TC], BF16)
    xc_sb = sb1("xc_sb", [128, DINT * TC], BF16)
    Ball = sb1("Ball", [128, NST * TC], BF16)
    Call = sb1("Call", [128, (NST // 2) * TC2P], BF16)
    u_sb = sb1("u_sb", [128, DMT * TC], BF16)
    dbc = sb1("dbc", [96, TC], BF16)
    wb_in = [sb1(f"wb_in{i}", [128, DMT * 256], BF16) for i in range(2)]
    wb_out = [sb1(f"wb_out{i}", [128, DINT * 128], BF16) for i in range(2)]
    dxc0 = sb1("dxc0", [128, TC], BF16)
    dAb = [sb1(f"dAb{i}", [128, TC2P], BF16) for i in range(3)]
    dBu4b = [sb1(f"dBu4_{i}", [128, TC2P], BF16) for i in range(2)]
    hbb = [sb1(f"hbb{i}", [128, TC2P], BF16) for i in range(2)]
    chb = [sb1(f"chb{i}", [128, TC2P], BF16) for i in range(2)]
    xcd = [sb1(f"xcd{i}", [128, TC], BF16) for i in range(2)]
    for _ab in dAb:
        gmemset(_ab[:, TC:TC + 1], 0.0)
    g_done = kb.cnt["g"]

    def slc(tile, idx, width, off=0):
        return tile[:, idx * width + off: idx * width + off + width]

    def xcp_sl(dt, a, b):
        return xcp[:, dt * (TC + 3) + a: dt * (TC + 3) + b]

    def s_act(out, in_, func, bias=None, scale=None, after=None):
        kw = {}
        if bias is not None:
            kw["bias"] = bias
        if scale is not None:
            kw["scale"] = scale
        return kb.run("scalar", lambda e, o=out, i=in_, f=func, kw=kw:
                      e.activation(out=o, in_=i, func=f, **kw), inc="s", after=after)

    def v_tt(out, a, b, op, after=None):
        return kb.run("vector", lambda e, o=out, x=a, y=b, p=op:
                      e.tensor_tensor(out=o, in0=x, in1=y, op=p), inc="v", after=after)

    def v_ts(out, a, s1, op0, s2=None, op1=None, after=None):
        if isinstance(s1, np.floating):
            s1 = float(s1)
        if isinstance(s2, np.floating):
            s2 = float(s2)
        def f(e, o=out, x=a, s1=s1, s2=s2, op0=op0, op1=op1):
            if op1 is None:
                return e.tensor_scalar(out=o, in0=x, scalar1=s1, scalar2=None, op0=op0)
            return e.tensor_scalar(out=o, in0=x, scalar1=s1, scalar2=s2, op0=op0, op1=op1)
        return kb.run("vector", f, inc="v", after=after)

    def v_stt(out, in0, scalar, in1, op0, op1, after=None):
        if isinstance(scalar, np.floating):
            scalar = float(scalar)
        return kb.run("vector", lambda e, o=out, a=in0, s=scalar, b=in1, p0=op0, p1=op1:
                      e.scalar_tensor_tensor(out=o, in0=a, scalar=s, in1=b, op0=p0, op1=p1),
                      inc="v", after=after)

    def t_mm(psname, out, lhsT, rhs, start, stop, after=None, inc=False):
        return kb.run("tensor", lambda e, o=out, l=lhsT, r=rhs, st=start, sp=stop:
                      e.matmul(o, l, r, start=st, stop=sp),
                      inc=("t" if inc else None), after=after)

    def stats_pass(tiles_fn, n, pa, after_each=None):
        tgt = None
        for j in range(n):
            rhs, aft = tiles_fn(j)
            a = merge(kb.bank_prod_after(pa) if j == 0 else None, aft, after_each)
            tgt = t_mm(pa, PS[pa][0:1, :], onescol[:, 0:1], rhs,
                       start=(j == 0), stop=(j == n - 1), after=a,
                       inc=(j == n - 1))
        kb.set_bank_prod(pa, tgt)
        return tgt

    st_guard = {"t": 0}

    def bc_sum(dst, src_ps, stage, pa, after, scale_ci=6):
        s1 = s_act(stage[:, :], src_ps, AF.Identity,
                   after=merge(after, {"g": g_done}, {"t": st_guard["t"]}))
        a = merge(kb.bank_prod_after(pa), {"s": s1})
        tm = t_mm(pa, PS[pa][:, :], ones_k1f[:, :], stage[:, :],
                  start=True, stop=True, after=a, inc=True)
        kb.set_bank_prod(pa, tm)
        st_guard["t"] = tm
        sc = s_act(dst, PS[pa][:, :], AF.Identity,
                   scale=(cconst[:, scale_ci:scale_ci + 1]
                          if scale_ci is not None else None),
                   after={"t": tm})
        kb.set_bank_cons(pa, "s", sc)
        return sc

    x3 = x_ln[:, :].rearrange("p (j t) -> p j t", j=DMT)

    ph1_barrier = {"s": pre_s, "v": pre_v, "g": pre_g}

    P = {
        "xmid_v": 0, "xmid_dma": 0, "outp_t": 0, "inproj_t": 0,
        "last_scan": 0,
    }
    state_g = {}
    wb_guard = [0, 0]
    vdx_cur = {}

    def emit_ln(c):
        dx = kb.dma(x3, x_h[:, :, c * TC:(c + 1) * TC],
                    after=merge({"v": P["xmid_v"]}, ph1_barrier if c == 0 else None))
        def xt(j, dx=dx):
            return slc(x_ln, j, TC), {"d": dx, "g": g_done}
        t_s1 = stats_pass(xt, DMT, 6)
        sq_s = [None, None]
        tgt = None
        for j in range(DMT):
            a = merge({"d": dx}, {"t": sq_s[j % 2]} if sq_s[j % 2] else None)
            sq_s[j % 2] = s_act(cbuf[j % 2][:, :], slc(x_ln, j, TC), AF.Square, after=a)
            aft = merge({"s": sq_s[j % 2]}, kb.bank_prod_after(7) if j == 0 else None)
            tgt = t_mm(7, PS[7][0:1, :], onescol[:, 0:1], cbuf[j % 2][:, :],
                       start=(j == 0), stop=(j == DMT - 1), after=aft, inc=True)
            sq_s[j % 2] = tgt
        kb.set_bank_prod(7, tgt)
        t_s2 = tgt
        bmu = bc_sum(bcA[:, :], PS[6][0:1, :], st0, 4, {"t": t_s1})
        kb.set_bank_cons(6, "s", bmu)
        bex = bc_sum(bcB[:, :], PS[7][0:1, :], st0, 5, {"t": t_s2, "s": bmu})
        kb.set_bank_cons(7, "s", bex)
        vq = v_tt(cbuf[0][:, :], bcA[:, :], bcA[:, :], ALU.mult,
                  after={"s": bmu, "t": t_s2})
        vvar = v_tt(bcB[:, :], bcB[:, :], cbuf[0][:, :], ALU.subtract,
                    after={"s": bex})
        vcen = None
        for j in range(DMT):
            vcen = v_tt(slc(x_ln, j, TC), slc(x_ln, j, TC), bcA[:, :], ALU.subtract)
        sa1a = s_act(bcA[:, :], bcB[:, :], AF.Ln, bias=cconst[:, 0:1],
                     after={"v": vcen})
        sa1 = s_act(bcA[:, :], bcA[:, :], AF.Exp, scale=cconst[:, 3:4],
                    after={"s": sa1a})
        sa2a = s_act(bcB[:, :], bcB[:, :], AF.Ln, scale=cconst[:, 2:3],
                     bias=cconst[:, 1:2], after={"s": sa1})
        sa2 = s_act(bcB[:, :], bcB[:, :], AF.Exp, scale=cconst[:, 3:4],
                    after={"s": sa2a})
        h_done = []
        vv = None
        for j in range(DMT):
            vv = v_tt(cbuf[j % 2][:, :], slc(x_ln, j, TC), bcB[:, :], ALU.mult,
                      after=merge({"s": sa2} if j == 0 else None,
                                  {"s": h_done[j - 2]} if j >= 2 else None))
            hd = s_act(slc(h_sb, j, TC), cbuf[j % 2][:, :], AF.Identity,
                       bias=cb_[:, j:j + 1], scale=cg[:, j:j + 1],
                       after=merge({"v": vv}, {"d": P["xmid_dma"]} if j == 0 else None))
            h_done.append(hd)
        vlp = None
        for j in range(DMT):
            vlp = v_tt(slc(x_ln, j, TC), slc(x_ln, j, TC), bcA[:, :], ALU.mult,
                       after={"s": sa1} if j == 0 else None)
        if c == 0:
            dump("h", h_sb[:, :], DMT * TC)
            dump("LNp", x_ln[:, :], DMT * TC)
        sqh = [None, None]
        tgt = None
        for j in range(DMT):
            a = merge({"s": h_done[j]}, {"t": sqh[j % 2]} if sqh[j % 2] else None,
                      {"v": vv} if j == 0 else None,
                      kb.bank_prod_after(6) if j == 0 else None)
            ss = s_act(cbuf[j % 2][:, :], slc(h_sb, j, TC), AF.Square, after=a)
            aft = merge({"s": ss}, kb.bank_prod_after(6) if j == 0 else None)
            tgt = t_mm(6, PS[6][0:1, :], onescol[:, 0:1], cbuf[j % 2][:, :],
                       start=(j == 0), stop=(j == DMT - 1), after=aft, inc=True)
            sqh[j % 2] = tgt
        kb.set_bank_prod(6, tgt)
        brm = bc_sum(bcB[:, :], PS[6][0:1, :], st0, 4, {"t": tgt, "v": vv})
        kb.set_bank_cons(6, "s", brm)
        srm = s_act(bcB[:, :], bcB[:, :], AF.Ln, bias=cconst[:, 0:1],
                    after={"s": brm})
        srm = s_act(bcB[:, :], bcB[:, :], AF.Exp, scale=cconst[:, 3:4],
                    after={"s": srm})
        hr_done = None
        for j in range(DMT):
            hr_done = v_tt(slc(hr_sb, j, TC), slc(h_sb, j, TC), bcB[:, :], ALU.mult,
                           after=merge({"s": srm},
                                       {"t": P["inproj_t"]} if j == 0 else None))
        if c == 0:
            dump("hr", hr_sb[:, :], DMT * TC, bf=True)
        return {"hr": hr_done, "sqh": tgt, "c": c}

    def make_stage1(c, b, lnh):
        sw = s_win[b].rearrange("(j p) m -> p j m", p=128)
        ctx = {"xc": {}, "evt": {}, "evs": {}, "xproj_t": None, "dl": {},
               "zev": {}, "sg": {}, "c": c, "b": b}

        def inproj_m(m):
            buf = m % 2
            wdma = kb.dma(wb_in[buf][:, 0:DMT * 128].rearrange(
                              "p (j m) -> p j m", j=DMT),
                          sw[:, :, m * 128:(m + 1) * 128],
                          after={"t": wb_guard[buf],
                                 "v": lnh["hr"] if m == 0 else None})
            pa = m % 2
            aft = merge({"d": wdma}, {"v": lnh["hr"]},
                        kb.bank_prod_after(pa), kb.bank_cons_after(pa))
            tl = None
            for k in range(DMT):
                tl = t_mm(pa, PS[pa][:, :], wb_in[buf][:, k * 128:(k + 1) * 128],
                          slc(hr_sb, k, TC), start=(k == 0), stop=(k == DMT - 1),
                          after=aft if k == 0 else None, inc=(k == DMT - 1))
            kb.set_bank_prod(pa, tl)
            wb_guard[buf] = tl
            return pa, tl

        def pieceT(dt):
            pa, tl = inproj_m(dt)
            sev = s_act(xcp_sl(dt, 3, TC + 3), PS[pa][:, :], AF.Identity,
                        bias=cinb[b][:, dt:dt + 1], after={"t": tl})
            kb.set_bank_cons(pa, "s", sev)
            ctx["evt"][dt] = tl

        def piece(dt):
            st = s_act(xcp_sl(dt, 0, 3), tails[b][:, dt * 3:(dt + 1) * 3],
                       AF.Identity, after={"g": g_done})
            va = v_ts(cbuf[dt % 2][:, :], xcp_sl(dt, 0, TC),
                      ccw[b][:, dt * 4: dt * 4 + 1], ALU.mult,
                      after={"s": st, "t": lnh["sqh"]})
            for k2 in (1, 2, 3):
                va = v_stt(cbuf[dt % 2][:, :], xcp_sl(dt, k2, TC + k2),
                           ccw[b][:, dt * 4 + k2: dt * 4 + k2 + 1],
                           cbuf[dt % 2][:, :], ALU.mult, ALU.add)
            sb_ = s_act(sgbuf[dt % 2][:, :], cbuf[dt % 2][:, :], AF.Exp,
                        scale=cconst[:, 4:5],
                        bias=cnegcb[b][:, dt:dt + 1], after={"v": va})
            sb_ = s_act(sgbuf[dt % 2][:, :], sgbuf[dt % 2][:, :], AF.Ln,
                        bias=cconst[:, 7:8], after={"s": sb_})
            ctx["sg"][dt] = s_act(sgbuf[dt % 2][:, :], sgbuf[dt % 2][:, :],
                                  AF.Exp, scale=cconst[:, 4:5],
                                  after={"s": sb_})
            s_act(tails[b][:, dt * 3:(dt + 1) * 3], xcp_sl(dt, TC, TC + 3),
                  AF.Identity)
            if dt >= 1:
                conv_fin(dt - 1)
            if dt >= 2:
                k = dt - 2
                aft = merge(kb.bank_prod_after(5) if k == 0 else None,
                            {"v": ctx["xc"][k]})
                ctx["xproj_t"] = t_mm(5, PS[5][0:96, :],
                                      xpw[b][:, k * 96:(k + 1) * 96],
                                      slc(xc_sb, k, TC), start=(k == 0),
                                      stop=False, after=aft,
                                      inc=False)

        def conv_fin(dt):
            ctx["xc"][dt] = v_stt(slc(xc_sb, dt, TC), cbuf[dt % 2][:, :],
                                  ccb[b][:, dt:dt + 1], sgbuf[dt % 2][:, :],
                                  ALU.add, ALU.mult,
                                  after=merge({"s": ctx["sg"][dt]},
                                              {"v": vdx_cur.get(dt)}))

        def dtmm_softplus(dt):
            pa = 6 + (dt % 2)
            aft = merge(kb.bank_prod_after(pa), kb.bank_cons_after(pa),
                        {"s": ctx["dbc_s"]})
            tm = t_mm(pa, PS[pa][:, :], dtw[b][:, dt * 128:(dt + 1) * 128],
                      dbc[0:64, :], start=True, stop=True, after=aft, inc=True)
            kb.set_bank_prod(pa, tm)
            se = s_act(ebuf2[:, :], PS[pa][:, :], AF.Exp,
                       bias=cdtb[b][:, dt:dt + 1], after={"t": tm})
            kb.set_bank_cons(pa, "s", se)
            ctx["dl"][dt] = s_act(dl3[dt % 3][:, :], ebuf2[:, :], AF.Ln,
                                  bias=cconst[:, 7:8],
                                  after=merge({"s": se},
                                              {"v": vdx_cur.get((dt - 2) % DINT)}))

        def tail():
            conv_fin(DINT - 1)
            k = DINT - 2
            aft = merge({"v": ctx["xc"][k]},
                        kb.bank_prod_after(5) if ctx["xproj_t"] is None else None)
            t_mm(5, PS[5][0:96, :], xpw[b][:, k * 96:(k + 1) * 96],
                 slc(xc_sb, k, TC), start=(ctx["xproj_t"] is None),
                 stop=False, after=aft, inc=False)
            ctx["xproj_t"] = 1
            k = DINT - 1
            tl = t_mm(5, PS[5][0:96, :], xpw[b][:, k * 96:(k + 1) * 96],
                      slc(xc_sb, k, TC), start=False,
                      stop=True, after={"v": ctx["xc"][k]}, inc=True)
            kb.set_bank_prod(5, tl)
            ctx["dbc_s"] = s_act(dbc[:, :], PS[5][0:96, :], AF.Identity,
                                 after={"t": tl})
            kb.set_bank_cons(5, "s", ctx["dbc_s"])
            if c == 0 and b == 0:
                dump("dbc", dbc[:, :], 512, bf=True, parts=96)
            dtmm_softplus(0)
            dtmm_softplus(1)

        ctx["piece"] = piece
        ctx["pieceT"] = pieceT
        ctx["tail"] = tail
        ctx["dtmm_softplus"] = dtmm_softplus
        return ctx

    def emit_zpiece(ctx, i):
        b = ctx["b"]
        m = DINT + i
        buf = m % 2
        sw = s_win[b].rearrange("(j p) m -> p j m", p=128)
        wdma = kb.dma(wb_in[buf][:, 0:DMT * 128].rearrange(
                          "p (j m) -> p j m", j=DMT),
                      sw[:, :, m * 128:(m + 1) * 128],
                      after={"t": wb_guard[buf]})
        pa = m % 2
        aft = merge({"d": wdma}, kb.bank_prod_after(pa),
                    kb.bank_cons_after(pa))
        tl = None
        for k in range(DMT):
            tl = t_mm(pa, PS[pa][:, :], wb_in[buf][:, k * 128:(k + 1) * 128],
                      slc(hr_sb, k, TC), start=(k == 0), stop=(k == DMT - 1),
                      after=aft if k == 0 else None, inc=(k == DMT - 1))
        kb.set_bank_prod(pa, tl)
        wb_guard[buf] = tl
        sev = s_act(slc(z_sb, i, TC), PS[pa][:, :], AF.Silu,
                    bias=cinb[b][:, DINT + i:DINT + i + 1],
                    after=merge({"t": tl},
                                {"t": P["outp_t"]} if i == 0 else None))
        kb.set_bank_cons(pa, "s", sev)
        ctx["zev"][i] = sev

    def bcast_one(ctx2, i, extra_v):
        pa = 6 + (i % 2)
        aft = merge(kb.bank_prod_after(pa), kb.bank_cons_after(pa),
                    {"s": ctx2["dbc_s"]}, {"g": g_done},
                    {"v": extra_v} if extra_v else None)
        tm = t_mm(pa, PS[pa][:, :], bcol(selbc, i), dbc[:, :],
                  start=True, stop=True, after=aft, inc=True)
        kb.set_bank_prod(pa, tm)
        if i < 16:
            dst = slc(Ball, i, TC)
        else:
            n_ = i - 16
            o_ = (n_ // 2) * TC2P + (n_ % 2) * (TC + 1)
            dst = Call[:, o_:o_ + TC]
        sc = s_act(dst, PS[pa][:, :], AF.Identity, after={"t": tm})
        kb.set_bank_cons(pa, "s", sc)
        return sc

    def emit_scan(ctx, nxt_cb, nxt_cb_top=None):
        c, b = ctx["c"], ctx["b"]
        scan_v = {}
        yfin_v = {}
        gsnap = kb.cnt["g"]
        bcast_s = dict(ctx.get("pre_bc") or {})
        gsep_g = {(0, 0): kb.run(
            "gpsimd",
            lambda e, o=dBu4b[0][:, TC:TC + 1], i=state[b][:, 1:2]:
            e.tensor_copy(out=o, in_=i),
            inc="g", after={"v": P["last_scan"]})}

        def emit_sda2(dt2, np2, aft_v):
            q4 = (np2 // 2) % 3
            si = dt2 * NST + np2
            sa = s_act(dAb[q4][:, 0:TC], dl3[dt2 % 3][:, :], AF.Exp,
                       scale=cnegA[b][:, si:si + 1], after={"v": aft_v})
            return s_act(dAb[q4][:, TC + 1:TC2P], dl3[dt2 % 3][:, :],
                         AF.Exp, scale=cnegA[b][:, si + 1:si + 2],
                         after={"s": sa})

        sda_q = {}
        for np0 in (0, 2, 4):
            sda_q[(0, np0)] = emit_sda2(0, np0, P["last_scan"])

        def bcast(i):
            return bcast_one(ctx, i, None)

        pstr = list(dxc0[:, :].ap[0])
        psy_t = {}
        psy_ch = {}
        for dt in range(DINT):
            vdx = v_tt(dxc0[:, :], dl3[dt % 3][:, :], slc(xc_sb, dt, TC),
                       ALU.mult, after={"s": ctx["dl"][dt], "g": g_done,
                                        "v": ctx["xc"][dt]})
            vdx_cur[dt] = vdx
            for cb2fn in (nxt_cb_top or {}).get(dt, []):
                cb2fn()
            pyb = 3 + (dt % 2)
            sxc = s_act(xcd[dt % 2][:, :], slc(xc_sb, dt, TC),
                        AF.Identity, scale=cD[b][:, dt:dt + 1],
                        after={"t": psy_t.get(dt - 2)})
            aft0 = merge({"s": sxc}, kb.bank_prod_after(pyb))
            im = t_mm(pyb, PS[pyb][:, :], identb[:, :], xcd[dt % 2][:, :],
                      start=True, stop=False, after=aft0, inc=True)
            psy_t[dt] = im
            for np_ in range(0, NST, 2):
                if np_ == 2:
                    if dt < DINT - 2:
                        ctx["dtmm_softplus"](dt + 2)
                p = np_ // 2
                sidx = dt * NST + np_
                if dt == 0:
                    for i2 in (np_, np_ + 1, 16 + np_, 17 + np_):
                        if i2 not in bcast_s:
                            bcast_s[i2] = bcast(i2)
                sda = sda_q[(dt, np_)]
                dxv = bass.AP(dxc0[:, :].tensor, dxc0[:, :].offset,
                              [pstr, [0, 2], [1, TC]])
                dbu = dBu4b[p % 2]
                dbo = bass.AP(dbu[:, :].tensor, dbu[:, :].offset,
                              [list(dbu[:, :].ap[0]), [TC + 1, 2], [1, TC]])
                vdb = v_tt(dbo, dxv,
                           Ball[:, np_ * TC:(np_ + 2) * TC].rearrange(
                               "p (n t) -> p n t", n=2), ALU.mult,
                           after=merge({"s": bcast_s.get(np_ + 1), "v": vdx}
                                       if dt == 0 else None))
                gsep = gsep_g[(dt, np_)]
                if np_ >= 4:
                    gprev = state_g.get((b, dt, np_ - 4))
                elif dt >= 1:
                    gprev = state_g.get((b, dt - 1, NST + np_ - 4))
                else:
                    gprev = gsnap
                ginit = state_g.get((b, dt, np_))
                vsc = kb.run("vector",
                             lambda e, o=hbb[p % 2][:, :], a=dAb[p % 3][:, :],
                             bb2=dbu[:, :],
                             ini=state[b][:, sidx:sidx + 1]:
                             e.tensor_tensor_scan(out=o, data0=a, data1=bb2,
                                                  initial=ini, op0=ALU.mult,
                                                  op1=ALU.add),
                             inc="v",
                             after=merge({"s": sda}, {"g": gprev},
                                         {"g": ginit}, {"g": gsep}))
                scan_v[(dt, np_)] = vsc
                nq = dt * (NST // 2) + p + 3
                if nq < DINT * (NST // 2):
                    dt2, p2_ = divmod(nq, NST // 2)
                    sda_q[(dt2, p2_ * 2)] = emit_sda2(dt2, p2_ * 2, vsc)
                nxt = (dt, np_ + 2) if np_ + 2 < NST else (dt + 1, 0)
                if nxt[0] < DINT:
                    nsi = nxt[0] * NST + nxt[1]
                    pg_aft = scan_v.get((dt, np_ - 2)) or \
                        scan_v.get((dt - 1, NST + np_ - 2)) or P["last_scan"]
                    gsep_g[nxt] = kb.run(
                        "gpsimd",
                        lambda e, o=dBu4b[(p + 1) % 2][:, TC:TC + 1],
                        i=state[b][:, nsi + 1:nsi + 2]:
                        e.tensor_copy(out=o, in_=i),
                        inc="g", after={"v": pg_aft})
                hb_ = hbb[p % 2][:, TC - 1:TC]
                hsrc = bass.AP(hb_.tensor, hb_.offset,
                               [list(hb_.ap[0]), [TC + 1, 2]])
                state_g[(b, dt, np_)] = kb.run(
                    "gpsimd", lambda e, o=state[b][:, sidx:sidx + 2], i=hsrc:
                    e.tensor_copy(out=o, in_=i),
                    inc="g", after={"v": vsc})
                pch = psy_ch.get((dt, p - 2))
                if pch is None and p < 2:
                    pch = psy_ch.get((dt - 1, p + 6))
                vch = v_tt(chb[p % 2][:, :], hbb[p % 2][:, :],
                           Call[:, p * TC2P:(p + 1) * TC2P], ALU.mult,
                           after=merge({"s": bcast_s.get(17 + np_)}
                                       if dt == 0 else None,
                                       {"t": pch}))
                t_mm(pyb, PS[pyb][:, :], identb[:, :], chb[p % 2][:, 0:TC],
                     start=False, stop=False, after={"v": vch}, inc=False)
                psy_ch[(dt, p)] = t_mm(pyb, PS[pyb][:, :], identb[:, :],
                                       chb[p % 2][:, TC + 1:TC2P],
                                       start=False, stop=(np_ == NST - 2),
                                       after=None, inc=True)
            kb.set_bank_prod(pyb, psy_ch[(dt, 7)])
            if 1 <= dt <= 4:
                for i in range(4 * (dt - 1), 4 * dt):
                    emit_zpiece(ctx, i)
            if dt == 1:
                yfin_v[0] = v_tt(slc(z_sb, 0, TC), PS[3][:, :],
                                 slc(z_sb, 0, TC), ALU.mult,
                                 after={"s": ctx["zev"][0],
                                        "t": psy_ch[(0, 7)]})
                kb.set_bank_cons(3, "v", yfin_v[0])
            if dt >= 1:
                yfin_v[dt] = v_tt(slc(z_sb, dt, TC), PS[pyb][:, :],
                                  slc(z_sb, dt, TC), ALU.mult,
                                  after={"s": ctx["zev"][dt],
                                         "t": psy_ch[(dt, 7)]})
                kb.set_bank_cons(pyb, "v", yfin_v[dt])
            for cb2fn in nxt_cb.get(dt, []):
                cb2fn()
        ctx["yfin_v"] = yfin_v
        P["last_scan"] = scan_v[(DINT - 1, NST - 2)]
        if c == 0 and b == 0:
            dump("yfin15", z_sb[:, 15 * TC:16 * TC], TC, bf=True)

    def emit_outproj(ctx, nctx=None):
        c, b = ctx["c"], ctx["b"]
        if nctx is not None:
            pre = {}
            for np_ in range(0, NST, 2):
                for i2 in (np_, np_ + 1, 16 + np_, 17 + np_):
                    pre[i2] = bcast_one(nctx, i2, P["last_scan"])
            nctx["pre_bc"] = pre
        cb_user_v = {}
        outp_t = {}
        swo = s_wout[b].rearrange("(j p) m -> p j m", p=128)
        for m in range(DMT):
            wdo = kb.dma(wb_out[m % 2][:, :].rearrange(
                             "p (j m) -> p j m", j=DINT),
                         swo[:, :, m * 128:(m + 1) * 128],
                         after={"t": outp_t.get(m - 2, P["outp_t"]),
                                "d2": cast_d["wout1" if b == 0 else "wout2"]})
            pa = 3 + (m % 2)
            aft = merge({"d": wdo}, {"v": ctx["yfin_v"][DINT - 1]},
                        kb.bank_prod_after(pa), kb.bank_cons_after(pa))
            tl = None
            for k in range(DINT):
                tl = t_mm(pa, PS[pa][:, :], wb_out[m % 2][:, k * 128:(k + 1) * 128],
                          slc(z_sb, k, TC), start=(k == 0), stop=(k == DINT - 1),
                          after=aft if k == 0 else None, inc=(k == DINT - 1))
            kb.set_bank_prod(pa, tl)
            outp_t[m] = tl
            sev = s_act(cbuf[m % 2][:, :], PS[pa][:, :], AF.Identity,
                        bias=coutb[b][:, m:m + 1],
                        after=merge({"t": tl},
                                    {"v": kb.cnt["v"]} if m < 2 else
                                    {"v": cb_user_v.get(m - 2)}))
            kb.set_bank_cons(pa, "s", sev)
            if b == 0:
                cb_user_v[m] = v_tt(slc(u_sb, m, TC), cbuf[m % 2][:, :],
                                    slc(h_sb, m, TC), ALU.add, after={"s": sev})
            else:
                v_tt(cbuf[m % 2][:, :], cbuf[m % 2][:, :], slc(h_sb, m, TC),
                     ALU.add, after={"s": sev})
                v_tt(cbuf[m % 2][:, :], cbuf[m % 2][:, :], slc(u_sb, m, TC),
                     ALU.mult)
                P["xmid_v"] = v_tt(slc(h_sb, m, TC), cbuf[m % 2][:, :],
                                   slc(x_ln, m, TC), ALU.add)
                cb_user_v[m] = P["xmid_v"]
        P["outp_t"] = outp_t[DMT - 1]
        P["inproj_t"] = wb_guard[0] if wb_guard[0] > wb_guard[1] else wb_guard[1]
        if c == 0 and b == 0:
            dump("u", u_sb[:, :], DMT * TC, bf=True)
        if b == 1:
            if c == 0:
                dump("xmid", h_sb[:, :], DMT * TC)
            P["xmid_dma"] = kb.dma(s_xmid[:, :, c * TC:(c + 1) * TC],
                                   h_sb[:, :].rearrange("p (j t) -> p j t", j=DMT),
                                   after={"v": P["xmid_v"]})

    units = [(c, b) for c in range(NCH) for b in (0, 1)]
    lnh = emit_ln(0)
    cur = make_stage1(0, 0, lnh)
    for dt in range(DINT):
        cur["pieceT"](dt)
        cur["piece"](dt)
    cur["tail"]()
    for idx, (c, b) in enumerate(units):
        nxt = units[idx + 1] if idx + 1 < len(units) else None
        nxt_cb = {}
        nctx = None
        nxt_cb_top = {}
        if nxt is not None and nxt[0] == c:
            nctx = make_stage1(nxt[0], nxt[1], lnh)
            for dt in range(DINT):
                nxt_cb_top[dt] = [(lambda d=dt, nc2=nctx: nc2["pieceT"](d))]
                nxt_cb[dt] = [(lambda d=dt, nc2=nctx: nc2["piece"](d))]
            nxt_cb[DINT - 1].append(lambda nc2=nctx: nc2["tail"]())
        if c == 0 and b == 0:
            dump("xc", xc_sb[:, :], DINT * TC, bf=True)
        emit_scan(cur, nxt_cb, nxt_cb_top)
        if c == 0 and b == 0:
            dump("Ball", Ball[:, :], NST * TC, bf=True)
            dump("Call", Call[:, :], NST * TC, bf=True)
        emit_outproj(cur, nctx if (nxt is not None and nxt[0] == c) else None)
        if nxt is not None and nxt[0] != c:
            lnh = emit_ln(nxt[0])
            nctx = make_stage1(nxt[0], nxt[1], lnh)
            for dt in range(DINT):
                nctx["pieceT"](dt)
                nctx["piece"](dt)
            nctx["tail"]()
        if nctx is not None:
            cur = nctx

    ph1 = {k: kb.cnt[k] for k in ("s", "v", "t", "g", "d")}
    es1.close()

    es2 = ExitStack()
    sb2 = lambda name, shape, dt: es2.enter_context(nc.sbuf_tensor(name, shape, dt))
    x0_sb = sb2("x0_sb", [128, DMT * TC], F32)
    xm2 = [sb2(f"xm_sb{i}", [128, DMT * TC], F32) for i in range(3)]
    h2b = [sb2(f"h2_sb{i}", [128, DMT * TC], BF16) for i in range(2)]
    h3_sb = sb2("h3_sb", [128, DMT * TC], BF16)
    g_sb = sb2("g_sb", [128, DFFT * TC], BF16)
    hu_sb = sb2("hu_sb", [128, HIDT * TC], BF16)
    st20 = sb2("st20", [1, TC], F32)
    bc2A = sb2("bc2A", [128, TC], F32)
    bc2B = sb2("bc2B", [128, TC], F32)
    cb2 = [sb2(f"cb2_{i}", [128, TC], F32) for i in range(2)]
    sgb = sb2("sgb", [128, TC], BF16)
    wbF = [sb2(f"wbF{i}", [128, DMT * 256], BF16) for i in range(2)]
    wbB = [sb2(f"wbB{i}", [128, DFFT * 128], BF16) for i in range(2)]
    wbGU = [sb2(f"wbGU{i}", [128, DMT * 256], BF16) for i in range(2)]

    bar2 = dict(ph1)

    S2 = {"statsv": 0, "outv": {}, "x0d": {}, "dmA": {},
          "ffn1_last": {}, "ffn2_last": {}, "dn_last": {},
          "tu_last": {}, "hu_last": {},
          "ffn1_slot": {}, "wg_slot": {}, "wu_slot": {},
          "h2_done": {}, "gelu_last": {}, "h3_done": {}, "dn_t": {}}

    def stats_ln2(src_tile, pa, pb, ln_affine, out_tile, src_g, out_t):
        t1 = None
        for j in range(DMT):
            aft = merge(kb.bank_prod_after(pa) if j == 0 else None,
                        src_g if j == 0 else None, {"g": g_done})
            t1 = t_mm(pa, PS[pa][0:1, :], onescol[:, 0:1], slc(src_tile, j, TC),
                      start=(j == 0), stop=(j == DMT - 1), after=aft,
                      inc=(j == DMT - 1))
        kb.set_bank_prod(pa, t1)
        sq = [None, None]
        t2 = None
        for j in range(DMT):
            a = merge({"t": sq[j % 2]} if sq[j % 2] else None,
                      merge(src_g, {"v": S2["statsv"]}) if j < 2 else None)
            ss = s_act(cb2[j % 2][:, :], slc(src_tile, j, TC), AF.Square, after=a)
            aft = merge({"s": ss}, kb.bank_prod_after(pb) if j == 0 else None)
            t2 = t_mm(pb, PS[pb][0:1, :], onescol[:, 0:1], cb2[j % 2][:, :],
                      start=(j == 0), stop=(j == DMT - 1), after=aft, inc=True)
            sq[j % 2] = t2
        kb.set_bank_prod(pb, t2)
        bmu = bc_sum(bc2A[:, :], PS[pa][0:1, :], st20, pa,
                     merge({"t": t1}, {"v": S2["statsv"]}))
        kb.set_bank_cons(pa, "s", bmu)
        bex = bc_sum(bc2B[:, :], PS[pb][0:1, :], st20, pb, {"t": t2, "s": bmu})
        kb.set_bank_cons(pb, "s", bex)
        vq = v_tt(cb2[0][:, :], bc2A[:, :], bc2A[:, :], ALU.mult,
                  after={"s": bmu, "t": t2})
        vvar = v_tt(bc2B[:, :], bc2B[:, :], cb2[0][:, :], ALU.subtract,
                    after={"s": bex})
        sA = s_act(bc2B[:, :], bc2B[:, :], AF.Ln, bias=cconst[:, 0:1],
                   after={"v": vvar})
        sA = s_act(bc2B[:, :], bc2B[:, :], AF.Exp, scale=cconst[:, 3:4],
                   after={"s": sA})
        outs = []
        vlast = None
        for j in range(DMT):
            vv = v_tt(cb2[j % 2][:, :], slc(src_tile, j, TC), bc2A[:, :],
                      ALU.subtract,
                      after=merge({"s": sA} if j == 0 else None,
                                  {"s": outs[j - 2]} if j >= 2 else None))
            vv = v_tt(cb2[j % 2][:, :], cb2[j % 2][:, :], bc2B[:, :], ALU.mult)
            vlast = vv
            og = merge({"v": vv}, out_t if j == 0 else None)
            if ln_affine:
                o = s_act(slc(out_tile, j, TC), cb2[j % 2][:, :], AF.Identity,
                          bias=cb_[:, j:j + 1], scale=cg[:, j:j + 1], after=og)
            else:
                o = s_act(slc(out_tile, j, TC), cb2[j % 2][:, :], AF.Identity,
                          after=og)
            outs.append(o)
        S2["statsv"] = vlast
        return outs

    swf1 = s_wf1.rearrange("(j p) m -> p j m", p=128)
    swf2 = s_wf2.rearrange("(j p) m -> p j m", p=128)
    swgu = s_wgu.rearrange("(j p) m -> p j m", p=128)
    swd = s_wd.rearrange("(j p) m -> p j m", p=128)

    x03 = x0_sb[:, :].rearrange("p (j t) -> p j t", j=DMT)
    xm3 = [xm2[i][:, :].rearrange("p (j t) -> p j t", j=DMT) for i in range(3)]

    def dmaA(c):
        aft = merge(bar2 if c == 0 else None,
                    {"v": S2["outv"].get(c - 3)} if c >= 3 else None)
        S2["dmA"][c] = kb.dma(xm3[c % 3], s_xmid[:, :, c * TC:(c + 1) * TC],
                              after=aft, eng="gpsimd")

    def statsA(c):
        outt = {"t": S2["ffn1_last"].get(c - 2)} if c >= 2 else None
        S2["h2_done"][c] = stats_ln2(xm2[c % 3], 6, 7, True, h2b[c % 2],
                                     {"d2": S2["dmA"][c]}, outt)

    def stageB(c):
        h2d = S2["h2_done"][c][-1]
        ge_prev = {"t": S2["ffn2_last"].get(c - 1)} if c >= 1 else None
        tl = sev = None
        for mg in range(DFFT // 2):
            slot = S2["ffn1_slot"].get((c, mg - 2),
                                       S2["ffn1_slot"].get((c - 1, mg - 2 + DFFT // 2), 0))
            wd1 = kb.dma(wbF[mg % 2][:, :].rearrange("p (j m) -> p j m", j=DMT),
                         swf1[:, :, mg * 256:(mg + 1) * 256],
                         after={"t": slot, "d2": cast_d["wf1"]})
            for mm in range(2):
                m = mg * 2 + mm
                pa = m % 3
                aft = merge({"d": wd1}, {"s": h2d},
                            kb.bank_prod_after(pa), kb.bank_cons_after(pa))
                tl = None
                for k in range(DMT):
                    tl = t_mm(pa, PS[pa][:, :],
                              wbF[mg % 2][:, k * 256 + mm * 128: k * 256 + (mm + 1) * 128],
                              slc(h2b[c % 2], k, TC), start=(k == 0),
                              stop=(k == DMT - 1),
                              after=aft if k == 0 else None, inc=(k == DMT - 1))
                kb.set_bank_prod(pa, tl)
                sev = s_act(slc(g_sb, m, TC), PS[pa][:, :], AF.Gelu,
                            bias=cfb1[:, m:m + 1],
                            after=merge({"t": tl}, ge_prev if m == 0 else None))
                kb.set_bank_cons(pa, "s", sev)
            S2["ffn1_slot"][(c, mg)] = tl
        S2["ffn1_last"][c] = tl
        S2["gelu_last"][c] = sev

    def stageC(c):
        xmt = xm2[c % 3]
        gelu_l = S2["gelu_last"][c]
        x1_done = {}
        tl = None
        for m in range(DMT):
            slot = x1_done.get(("t", m - 2),
                               S2["dn_t"].get((c - 1, 6 + m), 0) if m < 2 else 0)
            wd2 = kb.dma(wbB[m % 2][:, :].rearrange("p (j m) -> p j m", j=DFFT),
                         swf2[:, :, m * 128:(m + 1) * 128],
                         after={"t": slot, "d2": cast_d["wf2"]})
            pa = 3 + (m % 2)
            aft = merge({"d": wd2}, {"s": gelu_l},
                        kb.bank_prod_after(pa), kb.bank_cons_after(pa))
            tl = None
            for k in range(DFFT):
                tl = t_mm(pa, PS[pa][:, :], wbB[m % 2][:, k * 128:(k + 1) * 128],
                          slc(g_sb, k, TC), start=(k == 0), stop=(k == DFFT - 1),
                          after=aft if k == 0 else None, inc=(k == DFFT - 1))
            kb.set_bank_prod(pa, tl)
            sev = s_act(cb2[m % 2][:, :], PS[pa][:, :], AF.Identity,
                        bias=cfb2[:, m:m + 1],
                        after=merge({"t": tl},
                                    {"v": kb.cnt["v"]} if m < 2 else
                                    {"v": x1_done.get(m - 2)}))
            kb.set_bank_cons(pa, "s", sev)
            v_tt(slc(xmt, m, TC), slc(xmt, m, TC), cb2[m % 2][:, :], ALU.add,
                 after={"s": sev})
            vx1 = v_tt(slc(xmt, m, TC), slc(xmt, m, TC), slc(x0_sb, m, TC),
                       ALU.add, after={"d": S2["x0d"][c]} if m == 0 else None)
            x1_done[("t", m)] = tl
            x1_done[m] = vx1
        S2["ffn2_last"][c] = tl
        outt = {"t": S2["tu_last"].get(c - 1)} if c >= 1 else None
        S2["h3_done"][c] = stats_ln2(xmt, 6, 7, False, h3_sb,
                                     {"v": x1_done[DMT - 1]}, outt)

    def stageD(c):
        h3d = S2["h3_done"][c][-1]
        vhu = None
        tg = tu = None
        for m in range(HIDT):
            gslot = S2["wg_slot"].get((c, m - 2),
                                      S2["wg_slot"].get((c - 1, m - 2 + HIDT), 0))
            wgu_d = kb.dma(wbGU[m % 2][:, :].rearrange("p (j m) -> p j m", j=DMT),
                           swgu[:, :, m * 256:(m + 1) * 256],
                           after={"t": gslot, "d2": max(cast_d["wg"], cast_d["wu"])})
            pg = (m % 2) * 2
            pu = 1 + (m % 2) * 4
            aftg = merge({"d": wgu_d}, {"s": h3d},
                         kb.bank_prod_after(pg), kb.bank_cons_after(pg))
            aftu = merge(kb.bank_prod_after(pu), kb.bank_cons_after(pu))
            for k in range(DMT):
                tg = t_mm(pg, PS[pg][:, :],
                          wbGU[m % 2][:, k * 256: k * 256 + 128],
                          slc(h3_sb, k, TC), start=(k == 0), stop=(k == DMT - 1),
                          after=aftg if k == 0 else None, inc=(k == DMT - 1))
            for k in range(DMT):
                tu = t_mm(pu, PS[pu][:, :],
                          wbGU[m % 2][:, k * 256 + 128: k * 256 + 256],
                          slc(h3_sb, k, TC), start=(k == 0), stop=(k == DMT - 1),
                          after=aftu if k == 0 else None, inc=(k == DMT - 1))
            kb.set_bank_prod(pg, tg)
            kb.set_bank_prod(pu, tu)
            ssg = s_act(sgb[:, :], PS[pg][:, :], AF.Silu,
                        scale=cconst[:, 5:6],
                        after=merge({"t": tg},
                                    {"v": vhu} if vhu else
                                    {"v": S2["hu_last"].get(c - 1)},
                                    {"g": g_done}))
            kb.set_bank_cons(pg, "s", ssg)
            vhu = v_stt(slc(hu_sb, m, TC), PS[pu][:, :], R127, sgb[:, :],
                        ALU.mult, ALU.mult,
                        after=merge({"s": ssg, "t": tu},
                                    {"t": S2["dn_last"].get(c - 1)}
                                    if m == 0 else None))
            kb.set_bank_cons(pu, "v", vhu)
            S2["wg_slot"][(c, m)] = tu
        S2["tu_last"][c] = tu
        S2["hu_last"][c] = vhu

    def stageE(c):
        xmt = xm2[c % 3]
        hu_l = S2["hu_last"][c]
        out_v = {}
        tl = None
        for m in range(DMT):
            slot = S2["dn_t"].get((c, m - 2), S2["ffn2_last"][c])
            wd3 = kb.dma(wbB[m % 2][:, 0:HIDT * 128].rearrange("p (j m) -> p j m", j=HIDT),
                         swd[:, :, m * 128:(m + 1) * 128],
                         after={"t": slot, "d2": cast_d["wd"]})
            pa = 3 + (m % 2)
            aft = merge({"d": wd3}, {"v": hu_l},
                        kb.bank_prod_after(pa), kb.bank_cons_after(pa))
            tl = None
            for k in range(HIDT):
                tl = t_mm(pa, PS[pa][:, :], wbB[m % 2][:, k * 128:(k + 1) * 128],
                          slc(hu_sb, k, TC), start=(k == 0), stop=(k == HIDT - 1),
                          after=aft if k == 0 else None, inc=(k == HIDT - 1))
            kb.set_bank_prod(pa, tl)
            S2["dn_t"][(c, m)] = tl
            sev = s_act(cb2[m % 2][:, :], PS[pa][:, :], AF.Identity,
                        scale=cconst[:, 5:6],
                        after=merge({"t": tl},
                                    {"v": kb.cnt["v"]} if m < 2 else
                                    {"v": out_v.get(m - 2)}, {"g": g_done}))
            kb.set_bank_cons(pa, "s", sev)
            out_v[m] = v_tt(slc(x0_sb, m, TC), slc(xmt, m, TC), cb2[m % 2][:, :],
                            ALU.add, after={"s": sev})
            kb.dma(out_h[:, m, c * TC:(c + 1) * TC], slc(x0_sb, m, TC),
                   after={"v": out_v[m]})
        S2["outv"][c] = out_v[DMT - 1]
        S2["dn_last"][c] = tl
        if c + 1 < NCH:
            S2["x0d"][c + 1] = kb.dma(x03, x_h[:, :, (c + 1) * TC:(c + 2) * TC])

    S2["x0d"][0] = kb.dma(x03, x_h[:, :, 0:TC], after=bar2)
    dmaA(0)
    statsA(0)
    dmaA(1)
    stageB(0)
    for c in range(NCH):
        if c + 1 < NCH:
            statsA(c + 1)
        stageC(c)
        if c + 1 < NCH:
            stageB(c + 1)
        stageD(c)
        stageE(c)
        if c + 2 < NCH:
            dmaA(c + 2)

    es2.close()

    for sn in ("d", "d2", "d3"):
        fv = kb.cnt[sn]
        if fv:
            kb.prog["sync"].append(
                lambda e, v=fv, s=kb.sem[sn]: e.wait_ge(s, v))

    with nc.Block() as blk:
        blk.sync(lambda e: [f(e) for f in kb.prog["sync"]])
        blk.scalar(lambda e: [f(e) for f in kb.prog["scalar"]])
        blk.vector(lambda e: [f(e) for f in kb.prog["vector"]])
        blk.tensor(lambda e: [f(e) for f in kb.prog["tensor"]])
        blk.gpsimd(lambda e: [f(e) for f in kb.prog["gpsimd"]])

    es_top.close()
    return nc


def _vec128(v, k):
    v = np.ascontiguousarray(np.asarray(v, np.float32))
    return np.ascontiguousarray(v.reshape(k, 128).T)


def kernel(**inputs):
    ii = {k: np.asarray(v) for k, v in inputs.items()}
    x = np.ascontiguousarray(ii["L0_attn_x"], dtype=np.float32)

    nc = build_nc()

    common = {}
    for b, pre in ((1, "L0_attn_m1_"), (2, "L0_attn_m2_")):
        common[f"win{b}"] = np.ascontiguousarray(ii[pre + "in_w"], np.float32)
        common[f"wout{b}"] = np.ascontiguousarray(ii[pre + "out_w"], np.float32)
        common[f"wxp{b}"] = np.ascontiguousarray(
            ii[pre + "xproj_w"].astype(np.float32).reshape(16, 128, 96).transpose(1, 0, 2))
        common[f"wdt{b}"] = np.ascontiguousarray(ii[pre + "dt_w"], np.float32)
        common[f"v_rms{b}"] = _vec128(ii[pre + "rms_w"], 8)
        common[f"v_inb{b}"] = _vec128(ii[pre + "in_b"], 32)
        common[f"v_cw{b}"] = np.ascontiguousarray(
            ii[pre + "conv_w"].astype(np.float32).reshape(16, 128, 4)
            .transpose(1, 0, 2).reshape(128, 64))
        common[f"v_cb{b}"] = _vec128(ii[pre + "conv_b"], 16)
        common[f"v_dtb{b}"] = _vec128(ii[pre + "dt_b"], 16)
        common[f"v_alog{b}"] = np.ascontiguousarray(
            ii[pre + "A_log"].astype(np.float32).reshape(16, 128, 16)
            .transpose(1, 0, 2).reshape(128, 256))
        common[f"v_D{b}"] = _vec128(ii[pre + "D"], 16)
        common[f"v_outb{b}"] = _vec128(ii[pre + "out_b"], 8)
    common["wf1"] = np.ascontiguousarray(ii["L0_attn_ffn_w1"], np.float32)
    common["wf2"] = np.ascontiguousarray(ii["L0_attn_ffn_w2"], np.float32)
    common["v_fb1"] = _vec128(ii["L0_attn_ffn_b1"], 32)
    common["v_fb2"] = _vec128(ii["L0_attn_ffn_b2"], 8)
    common["v_g"] = _vec128(ii["L0_attn_ln_gamma"], 8)
    common["v_b"] = _vec128(ii["L0_attn_ln_beta"], 8)
    import ml_dtypes
    sel = np.zeros((96, 32), dtype=ml_dtypes.bfloat16)
    for n in range(16):
        sel[64 + n, n] = 1
        sel[80 + n, 16 + n] = 1
    common["selbc_h"] = sel
    common["ident_h"] = np.eye(128, dtype=ml_dtypes.bfloat16)
    common["wg"] = np.ascontiguousarray(ii["L0_ffn_w_gate"].T.astype(np.int32))
    common["wu"] = np.ascontiguousarray(ii["L0_ffn_w_up"].T.astype(np.int32))
    common["wd"] = np.ascontiguousarray(ii["L0_ffn_w_down"].T.astype(np.int32))

    in_maps = []
    for cidx in range(8):
        m = dict(common)
        xc = np.ascontiguousarray(
            x[cidx].T.reshape(DMT, 128, T).transpose(1, 0, 2))
        m["x"] = xc
        in_maps.append(m)

    res = run_bass_kernel_spmd(nc, in_maps, core_ids=list(range(8)))
    outs = []
    for cidx in range(8):
        o = res.results[cidx]["out"]
        outs.append(np.ascontiguousarray(
            o.transpose(1, 0, 2).reshape(DM, T).T))
    return np.stack(outs, axis=0).astype(np.float32)
